# version 17 (final)
# baseline (speedup 1.0000x reference)
import contextlib
import math
import os as _os
import numpy as np
import ml_dtypes
import concourse.bass as bass
import concourse.mybir as mybir
from concourse.bass_utils import run_bass_kernel_spmd

F32 = mybir.dt.float32
BF16 = mybir.dt.bfloat16
AF = mybir.ActivationFunctionType
ALU = mybir.AluOpType
AX = mybir.AxisListType
BF = ml_dtypes.bfloat16

NCORES = 8
BPC = 2
T = 2048
DM = 1024
NDT = DM // 128
H = 16
HD = 64
SCALE = math.sqrt(HD * 3.0)
DH = 4096
NCT = DH // 128
DI = 2048
NIT = DI // 128
NS = 16
EPS = 1e-5
MAGIC = float(3 << 22)

HW = 765
TBLW = 2 * HW + 1
TBLP = 1536
MBLK = 128 * TBLP
MGUARD = MBLK
BANDR = 4

SUBLAYERS = tuple(int(t) for t in _os.environ.get("SUBL", "0,1,2,3").split(","))
DEV_CORES = int(_os.environ.get("DEV_CORES", "0"))
TCH = 512
NCH = T // TCH

R_MEAN, R_RSTD, R_MAX, R_TMP, R_T2, R_DQ = range(6)
R_DEN = R_MAX
R_QS = R_T2


def _log_bucket_np(rel):
    mid = 128
    rel = np.clip(rel, -511, 511)
    sign = np.sign(rel).astype(np.float64)
    abs_pos = np.maximum(np.abs(rel), 1)
    log_base = math.log(511 / mid)
    log_pos = np.ceil(np.log(abs_pos.astype(np.float64) / mid) /
                      log_base * (mid - 1)) + mid
    bucket = np.where(abs_pos < mid, abs_pos.astype(np.float64), log_pos) * sign
    return bucket.astype(np.int64)


def _col_of_d(d):
    return np.clip(_log_bucket_np(np.asarray(d)) + 256, 0, 511)


def build_E():
    cols_f = _col_of_d(np.arange(TBLW) - HW)
    cols_r = _col_of_d(HW - np.arange(TBLW))
    Ef = np.zeros((512, TBLP), dtype=BF)
    Er = np.zeros((512, TBLP), dtype=BF)
    Ef[cols_f, np.arange(TBLW)] = BF(1.0)
    Er[cols_r, np.arange(TBLW)] = BF(1.0)
    return Ef, Er


def il(v, k):
    return np.ascontiguousarray(np.asarray(v).reshape(k, 128).T)


class SP:
    def __init__(self, h):
        self.h = h
        self.c = 0


_AUTO_OPS = frozenset((
    "tensor_copy", "tensor_tensor", "tensor_add", "tensor_sub", "tensor_mul",
    "tensor_max", "tensor_scalar", "tensor_single_scalar", "tensor_scalar_add",
    "tensor_scalar_sub", "tensor_scalar_mul", "tensor_scalar_max",
    "tensor_scalar_min", "tensor_relu", "tensor_reduce", "reduce_max",
    "reduce_sum", "scalar_tensor_tensor", "tensor_tensor_scan", "memset",
    "memzero", "reciprocal", "select", "copy_predicated",
    "activation", "copy", "add", "mul", "square", "sqrt", "sign",
))


class EP:

    def __init__(self, e, sp):
        self._e = e
        self._sp = sp

    def __getattr__(self, name):
        fn = getattr(self._e, name)
        if name in _AUTO_OPS:
            sp = self._sp
            e = self._e

            def call(*a, **k):
                ins = fn(*a, **k)
                ins._wait_ge(sp.h, sp.c)
                ins.then_inc(sp.h, 1)
                sp.c += 1
                return ins
            return call
        return fn


def build_program():
    nc = bass.Bass("TRN2", target_bir_lowering=False)
    gstack = contextlib.ExitStack()
    _uid = [0]

    def U(name):
        _uid[0] += 1
        return "%s_%d" % (name, _uid[0])

    def din(name, shape, dt):
        return nc.dram_tensor(name, list(shape), dt, kind="ExternalInput").ap()

    xin = din("xT", (BPC, DM, T), F32)
    outT = nc.dram_tensor("outT", [BPC, DM, T], F32, kind="ExternalOutput").ap()

    relembT_in = din("relembT", (DM, 512), F32)
    Wq_in = din("Wq", (DM, DM), F32)
    Wk_in = din("Wk", (DM, DM), F32)
    Wv_in = din("Wv", (DM, DM), F32)
    bq_row_in = din("bq_row", (1, DM), F32)
    bk_row_in = din("bk_row", (1, DM), F32)
    bv_row_in = din("bv_row", (1, DM), F32)
    bq_il_in = din("bq_il", (128, NDT), F32)
    bk_il_in = din("bk_il", (128, NDT), F32)
    Erev_in = din("Erev", (512, TBLP), BF16)
    Efwd_in = din("Efwd", (512, TBLP), BF16)
    identf_in = din("identf", (128, 128), F32)
    identb_in = din("identb", (128, 128), BF16)
    ones_row_in = din("ones_row", (1, 128), BF16)

    w1T_in = din("w1T", (DM, DH), F32)
    w2T_in = din("w2T", (DH, DM), F32)
    fconvw_in = din("fconvw_il", (128, 96), F32)
    fconvb_in = din("fconvb_il", (128, 32), F32)

    WinT_in = din("WinT", (DM, 2 * DI), F32)
    WxT_in = din("WxT", (DI, DI + 2 * NS), F32)
    WoutT_in = din("WoutT", (DI, DM), F32)
    bin_il_in = din("bin_il", (128, 32), F32)
    bxd_il_in = din("bxd_il", (128, 16), F32)
    bx_bc_in = din("bx_bc", (32, 1), F32)
    bout_il_in = din("bout_il", (128, 8), F32)
    bconv_il_in = din("bconv_il", (128, 16), F32)
    aconvw_in = din("aconvw_il", (128, 64), F32)
    A_il_in = din("A_il", (128, 256), F32)
    D_il_in = din("D_il", (128, 16), F32)

    xs = nc.dram_tensor("xs", [BPC, DM, T], F32, kind="Internal").ap()
    WinT_bf = nc.dram_tensor("WinT_bf", [DM, 2 * DI], BF16, kind="Internal").ap()
    WxT_bf = nc.dram_tensor("WxT_bf", [DI, DI + 2 * NS], BF16, kind="Internal").ap()
    WoutT_bf = nc.dram_tensor("WoutT_bf", [DI, DM], BF16, kind="Internal").ap()
    Wq_bf = nc.dram_tensor("Wq_bf", [DM, DM], BF16, kind="Internal").ap()
    Wk_bf = nc.dram_tensor("Wk_bf", [DM, DM], BF16, kind="Internal").ap()
    Wv_bf = nc.dram_tensor("Wv_bf", [DM, DM], BF16, kind="Internal").ap()
    relembT_bf = nc.dram_tensor("relembT_bf", [DM, 512], BF16, kind="Internal").ap()
    m1d = nc.dram_tensor("m1d", [2 * MGUARD + 16 * MBLK], BF16, kind="Internal").ap()
    m2d = nc.dram_tensor("m2d", [2 * MGUARD + 16 * MBLK], BF16, kind="Internal").ap()
    qt_d = nc.dram_tensor("qt_d", [DM, T], BF16, kind="Internal").ap()
    kt_d = nc.dram_tensor("kt_d", [DM, T], BF16, kind="Internal").ap()
    poskcd_d = nc.dram_tensor("poskcd_d", [512, DM], BF16, kind="Internal").ap()
    posqcd_d = nc.dram_tensor("posqcd_d", [512, DM], BF16, kind="Internal").ap()
    w1q_d = nc.dram_tensor("w1q_d", [DM, DH], BF16, kind="Internal").ap()
    w2q_d = nc.dram_tensor("w2q_d", [DH, DM], BF16, kind="Internal").ap()
    hsil_d = nc.dram_tensor("hsil_d", [DH, T], F32, kind="Internal").ap()
    xq2_d = nc.dram_tensor("xq2_d", [DH, T], BF16, kind="Internal").ap()
    xi_d = nc.dram_tensor("xi_d", [DI, T], F32, kind="Internal").ap()
    res_d = nc.dram_tensor("res_d", [DI, T], F32, kind="Internal").ap()
    dlt_d = nc.dram_tensor("dlt_d", [DI, T], F32, kind="Internal").ap()
    dltx_d = nc.dram_tensor("dltx_d", [DI, T], F32, kind="Internal").ap()
    yt_d = nc.dram_tensor("yt_d", [DI, T], BF16, kind="Internal").ap()
    row_d = nc.dram_tensor("row_d", [1, T], F32, kind="Internal").ap()
    bc_d = nc.dram_tensor("bc_d", [32, T], BF16, kind="Internal").ap()

    with nc.Block() as block:
        identf = gstack.enter_context(nc.sbuf_tensor("sb_identf", [128, 128], F32))
        identb = gstack.enter_context(nc.sbuf_tensor("sb_identb", [128, 128], BF16))
        ones_row = gstack.enter_context(nc.sbuf_tensor("sb_ones", [1, 128], BF16))
        rows = gstack.enter_context(nc.sbuf_tensor("sb_rows", [1, 6 * T], F32))
        rows2 = gstack.enter_context(nc.sbuf_tensor("sb_rows2", [1, 64], F32))
        red16 = gstack.enter_context(nc.sbuf_tensor("sb_red16", [128, 16], F32))
        r16t = gstack.enter_context(nc.sbuf_tensor("sb_r16t", [16, 128], F32))
        zcol = gstack.enter_context(nc.sbuf_tensor("sb_zcol", [128, 1], F32))

        sems = {}

        def sp(name):
            if name not in sems:
                sems[name] = SP(gstack.enter_context(nc.semaphore(name)))
            return sems[name]

        mm = sp("mm")
        ev = sp("ev")
        sc = sp("sc")
        SYNC_SLOTS = [sp(n) for n in
                      ("c0", "x0", "x1", "r0", "r1", "rw", "bc", "m0",
                       "s0", "s1", "s2", "s3", "s4", "s5", "s6", "s7")]
        GQ_SLOTS = [sp(n) for n in ("g0", "g1", "g2", "g3", "sw0")]
        (c0, x0, x1, r0, r1, rw, bcs, m0,
         s0, s1, s2, s3, s4, s5, s6, s7) = SYNC_SLOTS
        g0, g1, g2, g3, sw0 = GQ_SLOTS
        xsl = [x0, x1]
        rsl = [r0, r1]
        bsy = sp("bsy")
        bte = sp("bte")
        bve = sp("bve")
        bse = sp("bse")
        bge = sp("bge")

        def BV(f):
            @block.vector
            def _(e, f=f):
                f(EP(e, ev))

        def BS(f):
            @block.scalar
            def _(e, f=f):
                f(EP(e, sc))

        gq = sp("gq")

        def BG(f):
            @block.gpsimd
            def _(e, f=f):
                f(EP(e, gq))

        def R(i):
            return rows[0:1, i * T:(i + 1) * T]

        def wave(e, slot, pairs, waits=()):
            e.wait_ge(slot.h, slot.c)
            for s_, v_ in waits:
                e.wait_ge(s_.h, v_)
            for out, in_ in pairs:
                e.dma_start(out=out, in_=in_).then_inc(slot.h, 16)
                slot.c += 16

        def barrier():
            @block.sync
            def _(e):
                for s_ in SYNC_SLOTS:
                    e.wait_ge(s_.h, s_.c)
                e.sem_inc(bsy.h, 1)
            bsy.c += 1

            @block.gpsimd
            def _(e):
                for s_ in GQ_SLOTS:
                    e.wait_ge(s_.h, s_.c)
                e.sem_inc(bge.h, 1)
            bge.c += 1

            @block.tensor
            def _(e):
                e.sem_inc(bte.h, 1)
            bte.c += 1

            @block.vector
            def _(e):
                e.wait_ge(ev.h, ev.c)
                e.sem_inc(bve.h, 1)
            bve.c += 1

            @block.scalar
            def _(e):
                e.wait_ge(sc.h, sc.c)
                e.sem_inc(bse.h, 1)
            bse.c += 1

            for en, own in (("sync", bsy), ("gpsimd", bge), ("tensor", bte),
                            ("vector", bve), ("scalar", bse)):
                @getattr(block, en)
                def _(e, own=own):
                    for s_ in (bsy, bge, bte, bve, bse):
                        if s_ is not own:
                            e.wait_ge(s_.h, s_.c)

        @block.sync
        def _(e):
            wave(e, c0, [(identf[:], identf_in), (identb[:], identb_in),
                         (ones_row[:], ones_row_in)])

        @block.gpsimd
        def _(e):
            CH = 65536
            pairs = []
            for dst, src, n in ((WinT_bf, WinT_in, DM * 2 * DI),
                                (WxT_bf, WxT_in, DI * (DI + 2 * NS)),
                                (WoutT_bf, WoutT_in, DI * DM),
                                (Wq_bf, Wq_in, DM * DM),
                                (Wk_bf, Wk_in, DM * DM),
                                (Wv_bf, Wv_in, DM * DM),
                                (relembT_bf, relembT_in, DM * 512)):
                assert n % CH == 0, n
                pairs.append((bass.AP(dst.tensor, 0, [[CH, n // CH], [1, CH]]),
                              bass.AP(src.tensor, 0, [[CH, n // CH], [1, CH]])))
            wave(e, sw0, pairs)
        barrier()

        @BV
        def _(e):
            e.memset(zcol[:], 0.0)

        def tr_reduce(src_sb, op, pst):
            for k in range(16):
                @block.tensor
                def _(e, k=k):
                    e.wait_ge(ev.h, ev.c)
                    e.wait_ge(sc.h, sc.c)
                    e.wait_ge(c0.h, c0.c)
                    e.matmul(pst[:], src_sb[:, k * 128:(k + 1) * 128], identf[:],
                             start=True, stop=True,
                             is_transpose=True).then_inc(mm.h, 1)
                mm.c += 1

                @BV
                def _(e, k=k):
                    e.wait_ge(mm.h, mm.c)
                    e.tensor_reduce(out=red16[:, k:k + 1], in_=pst[:], axis=AX.X,
                                    op=op)

        def row_from_16(out_row, pst):
            @block.tensor
            def _(e):
                e.wait_ge(ev.h, ev.c)
                e.wait_ge(sc.h, sc.c)
                e.wait_ge(c0.h, c0.c)
                e.matmul(pst[0:16, 0:128], red16[:, 0:16], identf[:], start=True,
                         stop=True, is_transpose=True).then_inc(mm.h, 1)
            mm.c += 1

            @BS
            def _(e):
                e.wait_ge(mm.h, mm.c)
                e.wait_ge(rw.h, rw.c)
                e.activation(out=r16t[:], in_=pst[0:16, 0:128], func=AF.Copy)

            @block.sync
            def _(e):
                wave(e, rw, [(row_d[0:1, :], r16t[:])],
                     waits=[(sc, sc.c), (bcs, bcs.c)])
                wave(e, rw, [(out_row, row_d[0:1, :])])

        def bcast_row(dst_pairs, waits=()):
            @block.sync
            def _(e):
                for dst_sb, src_row, parts in dst_pairs:
                    w = src_row.shape[-1]
                    wave(e, rw, [(row_d[0:1, 0:w], src_row)],
                         waits=list(waits) + [(bcs, bcs.c)])
                    wave(e, bcs, [(dst_sb, row_d[0:1, 0:w].partition_broadcast(parts))],
                         waits=[(rw, rw.c)])

        def ln_stats(b, src, ph):
            for dt in range(NDT):
                buf = ph["xt2"][dt % 2]

                @block.sync
                def _(e, dt=dt, buf=buf):
                    wave(e, xsl[dt % 2], [(buf[:], src[b, dt * 128:(dt + 1) * 128, :])],
                         waits=[(ev, ev.c)])

                @BS
                def _(e, dt=dt, buf=buf):
                    e.wait_ge(xsl[dt % 2].h, xsl[dt % 2].c)
                    e.wait_ge(ev.h, ev.c)
                    e.activation(out=ph["tmp"][:], in_=buf[:], func=AF.Square)

                @BV
                def _(e, dt=dt, buf=buf):
                    e.wait_ge(xsl[dt % 2].h, xsl[dt % 2].c)
                    if dt == 0:
                        e.tensor_copy(out=ph["acc_s"][:], in_=buf[:])
                    else:
                        e.tensor_add(ph["acc_s"][:], ph["acc_s"][:], buf[:])
                    e.wait_ge(sc.h, sc.c)
                    if dt == 0:
                        e.tensor_copy(out=ph["acc_q"][:], in_=ph["tmp"][:])
                    else:
                        e.tensor_add(ph["acc_q"][:], ph["acc_q"][:], ph["tmp"][:])

            tr_reduce(ph["acc_s"], ALU.add, ph["pst"])
            row_from_16(R(R_MEAN), ph["pst"])
            tr_reduce(ph["acc_q"], ALU.add, ph["pst"])
            row_from_16(R(R_TMP), ph["pst"])

            @BV
            def _(e):
                e.wait_ge(rw.h, rw.c)
                e.tensor_scalar(out=R(R_MEAN), in0=R(R_MEAN), scalar1=1.0 / DM,
                                scalar2=None, op0=ALU.mult)
                e.tensor_mul(R(R_T2), R(R_MEAN), R(R_MEAN))
                e.tensor_scalar(out=R(R_TMP), in0=R(R_TMP), scalar1=1.0 / DM,
                                scalar2=None, op0=ALU.mult)
                e.tensor_sub(R(R_TMP), R(R_TMP), R(R_T2))
                e.tensor_scalar(out=R(R_TMP), in0=R(R_TMP), scalar1=EPS,
                                scalar2=None, op0=ALU.add)

            @BS
            def _(e):
                e.wait_ge(ev.h, ev.c)
                e.activation(out=R(R_TMP), in_=R(R_TMP), func=AF.Ln)
                e.activation(out=R(R_RSTD), in_=R(R_TMP), func=AF.Exp, scale=-0.5)

            pairs = [(ph["bc_mean"][:], R(R_MEAN), 128)]
            if ph.get("bc_rstd") is not None:
                pairs.append((ph["bc_rstd"][:], R(R_RSTD), 128))
            bcast_row(pairs, waits=[(sc, sc.c)])

        def ln_norm_bf16(b, src, out_sb, ph):
            for dt in range(NDT):
                buf = ph["xt2"][dt % 2]

                @block.sync
                def _(e, dt=dt, buf=buf):
                    wave(e, xsl[dt % 2], [(buf[:], src[b, dt * 128:(dt + 1) * 128, :])],
                         waits=[(ev, ev.c)])

                @BV
                def _(e, dt=dt, buf=buf):
                    e.wait_ge(xsl[dt % 2].h, xsl[dt % 2].c)
                    e.wait_ge(bcs.h, bcs.c)
                    e.tensor_sub(ph["tmp"][:], buf[:], ph["bc_mean"][:])
                    e.tensor_mul(out_sb[:, dt * T:(dt + 1) * T], ph["tmp"][:],
                                 ph["bc_rstd"][:])

        def ternarize(wT_ap, nt, cols, wq_ap, mw_slot, ph):
            for i in range(nt):
                buf = ph["xt2"][i % 2]

                @block.sync
                def _(e, i=i, buf=buf):
                    wave(e, xsl[i % 2], [(buf[:, 0:cols], wT_ap[i * 128:(i + 1) * 128, :])],
                         waits=[(ev, ev.c)])

                @BV
                def _(e, i=i, buf=buf):
                    e.wait_ge(xsl[i % 2].h, xsl[i % 2].c)
                    e.tensor_reduce(out=red16[:, 0:1], in_=buf[:, 0:cols], axis=AX.X,
                                    op=ALU.add, apply_absolute_value=True)
                    if i == 0:
                        e.tensor_copy(out=red16[:, 1:2], in_=red16[:, 0:1])
                    else:
                        e.tensor_add(red16[:, 1:2], red16[:, 1:2], red16[:, 0:1])

            @block.tensor
            def _(e):
                e.wait_ge(ev.h, ev.c)
                e.wait_ge(c0.h, c0.c)
                e.matmul(ph["pst"][0:1, 0:128], red16[:, 1:2], identf[:], start=True,
                         stop=True, is_transpose=True).then_inc(mm.h, 1)
            mm.c += 1

            @BV
            def _(e):
                e.wait_ge(mm.h, mm.c)
                e.tensor_reduce(out=rows2[0:1, mw_slot:mw_slot + 1],
                                in_=ph["pst"][0:1, 0:128], axis=AX.X, op=ALU.add)
                e.tensor_scalar(out=rows2[0:1, mw_slot:mw_slot + 1],
                                in0=rows2[0:1, mw_slot:mw_slot + 1],
                                scalar1=1.0 / (nt * 128 * cols), scalar2=1e-5,
                                op0=ALU.mult, op1=ALU.max)
                e.reciprocal(out=rows2[0:1, mw_slot + 8:mw_slot + 9],
                             in_=rows2[0:1, mw_slot:mw_slot + 1])

            bcast_row([(ph["swb"][:, 0:1], rows2[0:1, mw_slot + 8:mw_slot + 9], 128)],
                      waits=[(ev, ev.c)])

            for i in range(nt):
                buf = ph["xt2"][i % 2]

                @block.sync
                def _(e, i=i, buf=buf):
                    wave(e, xsl[i % 2], [(buf[:, 0:cols], wT_ap[i * 128:(i + 1) * 128, :])],
                         waits=[(ev, ev.c), (bcs, bcs.c)])

                @BV
                def _(e, i=i, buf=buf):
                    e.wait_ge(xsl[i % 2].h, xsl[i % 2].c)
                    e.wait_ge(bcs.h, bcs.c)
                    e.wait_ge(sw0.h, sw0.c)
                    e.tensor_scalar(out=ph["tmp"][:, 0:cols], in0=buf[:, 0:cols],
                                    scalar1=ph["swb"][:, 0:1], scalar2=MAGIC,
                                    op0=ALU.mult, op1=ALU.add)
                    e.tensor_scalar(out=ph["tmp"][:, 0:cols], in0=ph["tmp"][:, 0:cols],
                                    scalar1=MAGIC, scalar2=1.0,
                                    op0=ALU.subtract, op1=ALU.min)
                    e.tensor_scalar(out=ph["tmp"][:, 0:cols], in0=ph["tmp"][:, 0:cols],
                                    scalar1=-1.0, scalar2=None, op0=ALU.max)

                @block.gpsimd
                def _(e, i=i):
                    wave(e, sw0, [(wq_ap[i * 128:(i + 1) * 128, :], ph["tmp"][:, 0:cols])],
                         waits=[(ev, ev.c)])

        if 1 in SUBLAYERS or 3 in SUBLAYERS:
            with contextlib.ExitStack() as ps:
                ph = dict(
                    xt2=[ps.enter_context(nc.sbuf_tensor(U("tz_a"), [128, DH], F32)),
                         ps.enter_context(nc.sbuf_tensor(U("tz_b"), [128, DH], F32))],
                    tmp=ps.enter_context(nc.sbuf_tensor(U("tz_t"), [128, DH], F32)),
                    swb=ps.enter_context(nc.sbuf_tensor(U("tz_s"), [128, 1], F32)),
                    pst=ps.enter_context(nc.psum_tensor(U("tz_p"), [128, 128], F32)),
                )
                ternarize(w1T_in, NDT, DH, w1q_d, 0, ph)
                ternarize(w2T_in, NCT, DM, w2q_d, 1, ph)
                barrier()

        def emit_ffn(b, src, dst):
            with contextlib.ExitStack() as ps:
                xt2 = [ps.enter_context(nc.sbuf_tensor(U("f_xa"), [128, T], F32)),
                       ps.enter_context(nc.sbuf_tensor(U("f_xb"), [128, T], F32))]
                tmp = ps.enter_context(nc.sbuf_tensor(U("f_tmp"), [128, T], F32))
                acc_s = ps.enter_context(nc.sbuf_tensor(U("f_as"), [128, T], F32))
                acc_q = ps.enter_context(nc.sbuf_tensor(U("f_aq"), [128, T], F32))
                bc_mean = ps.enter_context(nc.sbuf_tensor(U("f_bm"), [128, T], F32))
                bc_b = ps.enter_context(nc.sbuf_tensor(U("f_bb"), [128, T], F32))
                bc_c = acc_s
                xq = ps.enter_context(nc.sbuf_tensor(U("f_xq"), [128, NDT * T], BF16))
                fwt = ps.enter_context(nc.sbuf_tensor(U("f_fwt"), [128, 96], F32))
                fbt = ps.enter_context(nc.sbuf_tensor(U("f_fbt"), [128, 32], F32))
                ph = dict(xt2=xt2, tmp=tmp, acc_s=acc_s, acc_q=acc_q,
                          bc_mean=bc_mean, bc_rstd=None)

                @block.sync
                def _(e):
                    wave(e, c0, [(fwt[:], fconvw_in), (fbt[:], fconvb_in)])

                with nc.psum_tensor(U("f_pst"), [128, 128], F32) as pst:
                    ph["pst"] = pst
                    ln_stats(b, src, ph)

                    for dt in range(NDT):
                        buf = xt2[dt % 2]

                        @block.sync
                        def _(e, dt=dt, buf=buf):
                            wave(e, xsl[dt % 2],
                                 [(buf[:], src[b, dt * 128:(dt + 1) * 128, :])],
                                 waits=[(ev, ev.c)])

                        @BV
                        def _(e, dt=dt, buf=buf):
                            e.wait_ge(xsl[dt % 2].h, xsl[dt % 2].c)
                            e.wait_ge(bcs.h, bcs.c)
                            e.tensor_sub(tmp[:], buf[:], bc_mean[:])
                            if dt == 0:
                                e.tensor_copy(out=acc_q[:], in_=tmp[:])
                                e.tensor_copy(out=bc_b[:], in_=tmp[:])
                            else:
                                e.tensor_tensor(out=acc_q[:], in0=acc_q[:], in1=tmp[:],
                                                op=ALU.max)
                                e.tensor_tensor(out=bc_b[:], in0=bc_b[:], in1=tmp[:],
                                                op=ALU.min)

                    tr_reduce(acc_q, ALU.max, pst)
                    row_from_16(R(R_MAX), pst)
                    tr_reduce(bc_b, ALU.min, pst)
                    row_from_16(R(R_TMP), pst)

                @BV
                def _(e):
                    e.wait_ge(rw.h, rw.c)
                    e.tensor_scalar(out=R(R_TMP), in0=R(R_TMP), scalar1=-1.0,
                                    scalar2=None, op0=ALU.mult)
                    e.tensor_tensor(out=R(R_MAX), in0=R(R_MAX), in1=R(R_TMP),
                                    op=ALU.max)
                    e.tensor_mul(R(R_TMP), R(R_RSTD), R(R_MAX))
                    e.tensor_scalar(out=R(R_TMP), in0=R(R_TMP), scalar1=1e-5,
                                    scalar2=None, op0=ALU.max)
                    e.tensor_scalar(out=R(R_DQ), in0=R(R_TMP),
                                    scalar1=rows2[0:1, 0:1], scalar2=1.0 / 127.0,
                                    op0=ALU.mult, op1=ALU.mult)
                    e.reciprocal(out=R(R_T2), in_=R(R_TMP))
                    e.tensor_scalar(out=R(R_T2), in0=R(R_T2), scalar1=127.0,
                                    scalar2=None, op0=ALU.mult)
                    e.tensor_mul(R(R_QS), R(R_RSTD), R(R_T2))

                bcast_row([(bc_b[:], R(R_QS), 128), (bc_c[:], R(R_DQ), 128)],
                          waits=[(ev, ev.c)])

                for dt in range(NDT):
                    buf = xt2[dt % 2]

                    @block.sync
                    def _(e, dt=dt, buf=buf):
                        wave(e, xsl[dt % 2], [(buf[:], src[b, dt * 128:(dt + 1) * 128, :])],
                             waits=[(ev, ev.c)])

                    @BV
                    def _(e, dt=dt, buf=buf):
                        e.wait_ge(xsl[dt % 2].h, xsl[dt % 2].c)
                        e.wait_ge(bcs.h, bcs.c)
                        e.tensor_sub(tmp[:], buf[:], bc_mean[:])
                        e.tensor_mul(tmp[:], tmp[:], bc_b[:])
                        e.tensor_scalar(out=xq[:, dt * T:(dt + 1) * T], in0=tmp[:],
                                        scalar1=MAGIC, scalar2=MAGIC,
                                        op0=ALU.add, op1=ALU.subtract)
                barrier()

                with contextlib.ExitStack() as p1:
                    mnb = p1.enter_context(nc.sbuf_tensor(U("f_mn"), [128, T], F32))
                    hrow = bc_mean
                    cacc = tmp
                    w1t = [p1.enter_context(nc.sbuf_tensor(U("f_w1t"), [128, 8 * 128], BF16))
                           for _ in range(2)]
                    psq = [p1.enter_context(nc.psum_tensor(U("f_p%d" % i), [128, 512], F32))
                           for i in range(2)]

                    evmark = [0, 0]
                    wmark = [0, 0]
                    for ct in range(NCT):
                        wb = w1t[ct % 2]

                        @block.sync
                        def _(e, ct=ct, wb=wb, w=wmark[ct % 2]):
                            wave(e, xsl[ct % 2],
                                 [(wb[:].rearrange("p (c u) -> p c u", c=8),
                                   w1q_d[:, ct * 128:(ct + 1) * 128].rearrange(
                                       "(c p) u -> p c u", p=128))],
                                 waits=[(mm, w)])
                        for tc in range(4):
                            pb = psq[tc % 2]

                            @block.tensor
                            def _(e, ct=ct, tc=tc, pb=pb, wb=wb, w=evmark[tc % 2]):
                                e.wait_ge(xsl[ct % 2].h, xsl[ct % 2].c)
                                e.wait_ge(ev.h, w)
                                for mc in range(8):
                                    ins = e.matmul(
                                        pb[:], wb[:, mc * 128:mc * 128 + 128],
                                        xq[:, mc * T + tc * 512:mc * T + tc * 512 + 512],
                                        start=(mc == 0), stop=(mc == 7))
                                ins.then_inc(mm.h, 1)
                            mm.c += 1
                            if tc == 3:
                                wmark[ct % 2] = mm.c

                            @BV
                            def _(e, ct=ct, tc=tc, pb=pb):
                                e.wait_ge(mm.h, mm.c)
                                e.wait_ge(bcs.h, bcs.c)
                                e.scalar_tensor_tensor(
                                    out=hrow[:, tc * 512:tc * 512 + 512],
                                    in0=pb[:], scalar=1.0,
                                    in1=bc_c[:, tc * 512:tc * 512 + 512],
                                    op0=ALU.mult, op1=ALU.mult)
                            evmark[tc % 2] = ev.c

                        @BV
                        def _(e, ct=ct):
                            e.wait_ge(c0.h, c0.c)
                            e.wait_ge(s0.h, s0.c)
                            e.tensor_scalar(out=cacc[:], in0=hrow[:, 0:T],
                                            scalar1=fwt[:, ct * 3 + 1:ct * 3 + 2],
                                            scalar2=None, op0=ALU.mult)
                            e.scalar_tensor_tensor(out=cacc[:, 1:T], in0=hrow[:, 0:T - 1],
                                                   scalar=fwt[:, ct * 3:ct * 3 + 1],
                                                   in1=cacc[:, 1:T], op0=ALU.mult,
                                                   op1=ALU.add)
                            e.scalar_tensor_tensor(out=cacc[:, 0:T - 1], in0=hrow[:, 1:T],
                                                   scalar=fwt[:, ct * 3 + 2:ct * 3 + 3],
                                                   in1=cacc[:, 0:T - 1], op0=ALU.mult,
                                                   op1=ALU.add)

                        @BS
                        def _(e, ct=ct):
                            e.wait_ge(ev.h, ev.c)
                            e.wait_ge(s0.h, s0.c)
                            e.wait_ge(c0.h, c0.c)
                            e.activation(out=cacc[:], in_=cacc[:], func=AF.Silu,
                                         bias=fbt[:, ct:ct + 1])

                        @BV
                        def _(e, ct=ct):
                            e.wait_ge(sc.h, sc.c)
                            if ct == 0:
                                e.tensor_copy(out=acc_q[:], in_=cacc[:])
                                e.tensor_copy(out=mnb[:], in_=cacc[:])
                            else:
                                e.tensor_tensor(out=acc_q[:], in0=acc_q[:], in1=cacc[:],
                                                op=ALU.max)
                                e.tensor_tensor(out=mnb[:], in0=mnb[:], in1=cacc[:],
                                                op=ALU.min)

                        @block.sync
                        def _(e, ct=ct):
                            wave(e, s0, [(hsil_d[ct * 128:(ct + 1) * 128, :], cacc[:])],
                                 waits=[(ev, ev.c)])

                    with nc.psum_tensor(U("fh_pst"), [128, 128], F32) as pst2:
                        tr_reduce(acc_q, ALU.max, pst2)
                        row_from_16(R(R_MAX), pst2)
                        tr_reduce(mnb, ALU.min, pst2)
                        row_from_16(R(R_TMP), pst2)

                    @BV
                    def _(e):
                        e.wait_ge(rw.h, rw.c)
                        e.tensor_scalar(out=R(R_TMP), in0=R(R_TMP), scalar1=-1.0,
                                        scalar2=None, op0=ALU.mult)
                        e.tensor_tensor(out=R(R_MAX), in0=R(R_MAX), in1=R(R_TMP),
                                        op=ALU.max)
                        e.tensor_scalar(out=R(R_TMP), in0=R(R_MAX), scalar1=1e-5,
                                        scalar2=None, op0=ALU.max)
                        e.tensor_scalar(out=R(R_DQ), in0=R(R_TMP),
                                        scalar1=rows2[0:1, 1:2], scalar2=1.0 / 127.0,
                                        op0=ALU.mult, op1=ALU.mult)
                        e.reciprocal(out=R(R_QS), in_=R(R_TMP))
                        e.tensor_scalar(out=R(R_QS), in0=R(R_QS), scalar1=127.0,
                                        scalar2=None, op0=ALU.mult)

                    bcast_row([(bc_b[:], R(R_QS), 128), (bc_c[:], R(R_DQ), 128)],
                              waits=[(ev, ev.c)])

                    for ct in range(NCT):
                        buf = xt2[ct % 2]

                        @block.sync
                        def _(e, ct=ct, buf=buf):
                            wave(e, xsl[ct % 2],
                                 [(buf[:], hsil_d[ct * 128:(ct + 1) * 128, :])],
                                 waits=[(ev, ev.c), (s0, s0.c)])

                        @BV
                        def _(e, ct=ct, buf=buf):
                            e.wait_ge(xsl[ct % 2].h, xsl[ct % 2].c)
                            e.wait_ge(bcs.h, bcs.c)
                            e.wait_ge(s1.h, s1.c)
                            e.tensor_mul(cacc[:], buf[:], bc_b[:])
                            e.tensor_scalar(out=hrow[:, 0:T].bitcast(BF16)[:, 0:T],
                                            in0=cacc[:], scalar1=MAGIC, scalar2=MAGIC,
                                            op0=ALU.add, op1=ALU.subtract)

                        @block.sync
                        def _(e, ct=ct):
                            wave(e, s1, [(xq2_d[ct * 128:(ct + 1) * 128, :],
                                          hrow[:, 0:T].bitcast(BF16)[:, 0:T])],
                                 waits=[(ev, ev.c)])
                    barrier()

                with contextlib.ExitStack() as p2:
                    w2 = p2.enter_context(nc.sbuf_tensor(U("f_w2"), [128, NCT * DM], BF16))
                    r2 = [p2.enter_context(nc.sbuf_tensor(U("f_r2"), [128, 512], BF16))
                          for _ in range(2)]
                    psd = [p2.enter_context(nc.psum_tensor(U("f_pd%d" % i), [128, 512], F32))
                           for i in range(8)]

                    @block.sync
                    def _(e):
                        wave(e, c0, [(w2[:, cc * DM:(cc + 1) * DM],
                                      w2q_d[cc * 128:(cc + 1) * 128, :])
                                     for cc in range(NCT)])

                    for tc in range(4):
                        for cc in range(NCT):
                            buf = r2[cc % 2]

                            @block.sync
                            def _(e, tc=tc, cc=cc, buf=buf, w=mm.c):
                                wave(e, rsl[cc % 2],
                                     [(buf[:], xq2_d[cc * 128:(cc + 1) * 128,
                                                     tc * 512:tc * 512 + 512])],
                                     waits=[(mm, w)])

                            @block.tensor
                            def _(e, tc=tc, cc=cc, buf=buf):
                                e.wait_ge(rsl[cc % 2].h, rsl[cc % 2].c)
                                e.wait_ge(c0.h, c0.c)
                                if cc == 0:
                                    e.wait_ge(ev.h, ev.c)
                                for dm_ in range(8):
                                    ins = e.matmul(
                                        psd[dm_][:],
                                        w2[:, cc * DM + dm_ * 128:cc * DM + dm_ * 128 + 128],
                                        buf[:], start=(cc == 0), stop=(cc == NCT - 1))
                                ins.then_inc(mm.h, 1)
                            mm.c += 1

                        for dm_ in range(8):
                            @block.sync
                            def _(e, tc=tc, dm_=dm_):
                                wave(e, m0, [(xt2[0][:, 0:512],
                                              src[b, dm_ * 128:(dm_ + 1) * 128,
                                                  tc * 512:tc * 512 + 512])],
                                     waits=[(ev, ev.c)])

                            @BV
                            def _(e, tc=tc, dm_=dm_):
                                e.wait_ge(mm.h, mm.c)
                                e.wait_ge(m0.h, m0.c)
                                e.wait_ge(s1.h, s1.c)
                                e.tensor_mul(tmp[:, 0:512], psd[dm_][:],
                                             bc_c[:, tc * 512:tc * 512 + 512])
                                e.tensor_add(tmp[:, 0:512], tmp[:, 0:512],
                                             xt2[0][:, 0:512])

                            @block.sync
                            def _(e, tc=tc, dm_=dm_):
                                wave(e, s1, [(dst[b, dm_ * 128:(dm_ + 1) * 128,
                                                  tc * 512:tc * 512 + 512],
                                              tmp[:, 0:512])],
                                     waits=[(ev, ev.c)])
                    barrier()

        def emit_ssm(b, src, dst):
            with contextlib.ExitStack() as ps:
                xibf = ps.enter_context(nc.sbuf_tensor(U("s_xibf"), [128, NIT * T], BF16))
                with contextlib.ExitStack() as p1:
                    lnx = p1.enter_context(nc.sbuf_tensor(U("s_lnx"), [128, NDT * T], BF16))
                    with contextlib.ExitStack() as pln:
                        xt2 = [pln.enter_context(nc.sbuf_tensor(U("s_xa"), [128, T], F32)),
                               pln.enter_context(nc.sbuf_tensor(U("s_xb"), [128, T], F32))]
                        tmp = pln.enter_context(nc.sbuf_tensor(U("s_tmp"), [128, T], F32))
                        acc_s = pln.enter_context(nc.sbuf_tensor(U("s_as"), [128, T], F32))
                        acc_q = pln.enter_context(nc.sbuf_tensor(U("s_aq"), [128, T], F32))
                        bc_mean = pln.enter_context(nc.sbuf_tensor(U("s_bm"), [128, T], F32))
                        bc_rstd = pln.enter_context(nc.sbuf_tensor(U("s_br"), [128, T], F32))
                        pst = pln.enter_context(nc.psum_tensor(U("s_pst"), [128, 128], F32))
                        ph = dict(xt2=xt2, tmp=tmp, acc_s=acc_s, acc_q=acc_q,
                                  bc_mean=bc_mean, bc_rstd=bc_rstd, pst=pst)
                        ln_stats(b, src, ph)
                        ln_norm_bf16(b, src, lnx, ph)
                        barrier()

                    wint = [p1.enter_context(nc.sbuf_tensor(U("s_wint"), [128, 8 * 128], BF16))
                            for _ in range(2)]
                    awt = p1.enter_context(nc.sbuf_tensor(U("s_awt"), [128, 64], F32))
                    abt = p1.enter_context(nc.sbuf_tensor(U("s_abt"), [128, 16], F32))
                    bint = p1.enter_context(nc.sbuf_tensor(U("s_bint"), [128, 32], F32))
                    hrow = p1.enter_context(nc.sbuf_tensor(U("s_hr"), [128, T], F32))
                    cacc = p1.enter_context(nc.sbuf_tensor(U("s_ca"), [128, T], F32))
                    psq = [p1.enter_context(nc.psum_tensor(U("s_p%d" % i), [128, 512], F32))
                           for i in range(2)]

                    @block.sync
                    def _(e):
                        wave(e, c0, [(awt[:], aconvw_in), (abt[:], bconv_il_in),
                                     (bint[:], bin_il_in)])

                    evmark = [0, 0]
                    wmark = [0, 0]
                    gsl = [g1, g2]
                    for ct in range(2 * NIT):
                        is_xi = ct < NIT
                        wb = wint[ct % 2]

                        @block.sync
                        def _(e, ct=ct, wb=wb, w=wmark[ct % 2]):
                            wave(e, gsl[ct % 2],
                                 [(wb[:].rearrange("p (c u) -> p c u", c=8),
                                   WinT_bf[:, ct * 128:(ct + 1) * 128].rearrange(
                                       "(c p) u -> p c u", p=128))],
                                 waits=[(mm, w)])
                        for tc in range(4):
                            pb = psq[tc % 2]

                            @block.tensor
                            def _(e, ct=ct, tc=tc, pb=pb, wb=wb, w=evmark[tc % 2]):
                                e.wait_ge(gsl[ct % 2].h, gsl[ct % 2].c)
                                e.wait_ge(c0.h, c0.c)
                                e.wait_ge(ev.h, w)
                                for mc in range(8):
                                    ins = e.matmul(
                                        pb[:], wb[:, mc * 128:mc * 128 + 128],
                                        lnx[:, mc * T + tc * 512:mc * T + tc * 512 + 512],
                                        start=(mc == 0), stop=(mc == 7))
                                ins.then_inc(mm.h, 1)
                            mm.c += 1
                            if tc == 3:
                                wmark[ct % 2] = mm.c

                            @BV
                            def _(e, ct=ct, tc=tc, pb=pb):
                                e.wait_ge(mm.h, mm.c)
                                e.wait_ge(s6.h, s6.c)
                                e.wait_ge(c0.h, c0.c)
                                e.tensor_scalar(
                                    out=hrow[:, tc * 512:tc * 512 + 512], in0=pb[:],
                                    scalar1=bint[:, ct:ct + 1], scalar2=None,
                                    op0=ALU.add)
                            evmark[tc % 2] = ev.c

                        if is_xi:
                            @BV
                            def _(e, ct=ct):
                                e.wait_ge(c0.h, c0.c)
                                e.tensor_scalar(out=cacc[:], in0=hrow[:, 0:T],
                                                scalar1=awt[:, ct * 4 + 3:ct * 4 + 4],
                                                scalar2=None, op0=ALU.mult)
                                for kk in range(3):
                                    sh = 3 - kk
                                    e.scalar_tensor_tensor(
                                        out=cacc[:, sh:T], in0=hrow[:, 0:T - sh],
                                        scalar=awt[:, ct * 4 + kk:ct * 4 + kk + 1],
                                        in1=cacc[:, sh:T], op0=ALU.mult, op1=ALU.add)

                            @BS
                            def _(e, ct=ct):
                                e.wait_ge(ev.h, ev.c)
                                e.wait_ge(s6.h, s6.c)
                                e.wait_ge(c0.h, c0.c)
                                e.activation(out=cacc[:], in_=cacc[:], func=AF.Silu,
                                             bias=abt[:, ct:ct + 1])

                            @BV
                            def _(e, ct=ct):
                                e.wait_ge(sc.h, sc.c)
                                e.tensor_copy(out=xibf[:, ct * T:(ct + 1) * T],
                                              in_=cacc[:])

                            @block.sync
                            def _(e, ct=ct):
                                wave(e, s6, [(xi_d[ct * 128:(ct + 1) * 128, :], cacc[:])],
                                     waits=[(sc, sc.c)])
                        else:
                            @BS
                            def _(e, ct=ct):
                                e.wait_ge(ev.h, ev.c)
                                e.wait_ge(s6.h, s6.c)
                                e.activation(out=cacc[:], in_=hrow[:], func=AF.Silu)

                            @block.sync
                            def _(e, ct=ct):
                                wave(e, s6,
                                     [(res_d[(ct - NIT) * 128:(ct - NIT + 1) * 128, :],
                                       cacc[:])],
                                     waits=[(sc, sc.c)])
                    barrier()

                with contextlib.ExitStack() as p2:
                    wxt = [p2.enter_context(nc.sbuf_tensor(U("s_wxt"), [128, NIT * 128], BF16))
                           for _ in range(2)]
                    wxbc = p2.enter_context(nc.sbuf_tensor(U("s_wxbc"), [128, NIT * 32], BF16))
                    bxd = p2.enter_context(nc.sbuf_tensor(U("s_bxd"), [128, 16], F32))
                    bxbc = p2.enter_context(nc.sbuf_tensor(U("s_bxbc"), [32, 1], F32))
                    drow = p2.enter_context(nc.sbuf_tensor(U("s_dr"), [128, T], F32))
                    dxrow = p2.enter_context(nc.sbuf_tensor(U("s_dxr"), [128, T], F32))
                    xif = p2.enter_context(nc.sbuf_tensor(U("s_xif"), [128, T], F32))
                    bcsb = p2.enter_context(nc.sbuf_tensor(U("s_bc"), [32, T], BF16))
                    psq = [p2.enter_context(nc.psum_tensor(U("s_q%d" % i), [128, 512], F32))
                           for i in range(2)]

                    @block.sync
                    def _(e):
                        wave(e, c0, [(bxd[:], bxd_il_in), (bxbc[:], bx_bc_in)])

                    @block.sync
                    def _(e):
                        wave(e, g3, [(wxbc[:].rearrange("p (c u) -> p c u", c=NIT),
                                      WxT_bf[:, 2048:2080].rearrange(
                                          "(c p) u -> p c u", p=128))])

                    wmark = [0, 0]
                    gsl = [g1, g2]
                    for pc in range(NIT):
                        wb = wxt[pc % 2]

                        @block.sync
                        def _(e, pc=pc, wb=wb, w=wmark[pc % 2]):
                            wave(e, gsl[pc % 2],
                                 [(wb[:].rearrange("p (c u) -> p c u", c=NIT),
                                   WxT_bf[:, pc * 128:(pc + 1) * 128].rearrange(
                                       "(c p) u -> p c u", p=128))],
                                 waits=[(mm, w)])
                        @block.sync
                        def _(e, pc=pc):
                            wave(e, m0, [(xif[:], xi_d[pc * 128:(pc + 1) * 128, :])],
                                 waits=[(ev, ev.c), (s6, s6.c)])
                        for tc in range(4):
                            pb = psq[tc % 2]

                            @block.tensor
                            def _(e, pc=pc, tc=tc, pb=pb, wb=wb, w=sc.c):
                                e.wait_ge(gsl[pc % 2].h, gsl[pc % 2].c)
                                e.wait_ge(sc.h, w)
                                for cc in range(NIT):
                                    ins = e.matmul(
                                        pb[:], wb[:, cc * 128:cc * 128 + 128],
                                        xibf[:, cc * T + tc * 512:cc * T + tc * 512 + 512],
                                        start=(cc == 0), stop=(cc == NIT - 1))
                                ins.then_inc(mm.h, 1)
                            mm.c += 1
                            if tc == 3:
                                wmark[pc % 2] = mm.c

                            @BS
                            def _(e, pc=pc, tc=tc, pb=pb):
                                e.wait_ge(mm.h, mm.c)
                                e.wait_ge(s2.h, s2.c)
                                e.wait_ge(c0.h, c0.c)
                                e.activation(out=drow[:, tc * 512:tc * 512 + 512],
                                             in_=pb[:], func=AF.Exp,
                                             bias=bxd[:, pc:pc + 1])
                                e.activation(out=drow[:, tc * 512:tc * 512 + 512],
                                             in_=drow[:, tc * 512:tc * 512 + 512],
                                             func=AF.Ln, bias=1.0)

                        @BV
                        def _(e, pc=pc):
                            e.wait_ge(sc.h, sc.c)
                            e.wait_ge(m0.h, m0.c)
                            e.wait_ge(s2.h, s2.c)
                            e.tensor_mul(dxrow[:], drow[:], xif[:])

                        @block.sync
                        def _(e, pc=pc):
                            wave(e, s2, [(dlt_d[pc * 128:(pc + 1) * 128, :], drow[:]),
                                         (dltx_d[pc * 128:(pc + 1) * 128, :], dxrow[:])],
                                 waits=[(ev, ev.c)])

                    for tc in range(4):
                        pb = psq[tc % 2]

                        @block.tensor
                        def _(e, tc=tc, pb=pb, w=ev.c):
                            e.wait_ge(ev.h, w)
                            e.wait_ge(g3.h, g3.c)
                            for cc in range(NIT):
                                ins = e.matmul(
                                    pb[0:32, :],
                                    wxbc[:, cc * 32:cc * 32 + 32],
                                    xibf[:, cc * T + tc * 512:cc * T + tc * 512 + 512],
                                    start=(cc == 0), stop=(cc == NIT - 1))
                            ins.then_inc(mm.h, 1)
                        mm.c += 1

                        @BV
                        def _(e, tc=tc, pb=pb):
                            e.wait_ge(mm.h, mm.c)
                            e.wait_ge(c0.h, c0.c)
                            e.tensor_scalar(out=bcsb[:, tc * 512:tc * 512 + 512],
                                            in0=pb[0:32, :], scalar1=bxbc[:, 0:1],
                                            scalar2=None, op0=ALU.add)

                    @block.sync
                    def _(e):
                        wave(e, s7, [(bc_d, bcsb[:])], waits=[(ev, ev.c)])
                    barrier()

            with contextlib.ExitStack() as p3:
                Bb2 = [p3.enter_context(nc.sbuf_tensor(U("s_Bb"), [128, NS * TCH], BF16))
                       for _ in range(2)]
                Cb2 = [p3.enter_context(nc.sbuf_tensor(U("s_Cb"), [128, NS * TCH], BF16))
                       for _ in range(2)]
                dch2 = [p3.enter_context(nc.sbuf_tensor(U("s_dch"), [128, TCH], F32))
                        for _ in range(2)]
                dxch2 = [p3.enter_context(nc.sbuf_tensor(U("s_dxch"), [128, TCH], F32))
                         for _ in range(2)]
                dA2 = [p3.enter_context(nc.sbuf_tensor(U("s_dA"), [128, TCH], F32))
                       for _ in range(2)]
                dBx2 = [p3.enter_context(nc.sbuf_tensor(U("s_dBx"), [128, TCH], F32))
                        for _ in range(2)]
                hbuf = p3.enter_context(nc.sbuf_tensor(U("s_h"), [128, TCH], F32))
                ytmp = p3.enter_context(nc.sbuf_tensor(U("s_yt"), [128, TCH], F32))
                yacc = p3.enter_context(nc.sbuf_tensor(U("s_ya"), [128, TCH], F32))
                ybf = p3.enter_context(nc.sbuf_tensor(U("s_ybf"), [128, TCH], BF16))
                stt = p3.enter_context(nc.sbuf_tensor(U("s_st"), [128, NIT * NS], F32))
                Aneg = p3.enter_context(nc.sbuf_tensor(U("s_An"), [128, 256], F32))
                Dil = p3.enter_context(nc.sbuf_tensor(U("s_Dil"), [128, 16], F32))
                xich2 = [p3.enter_context(nc.sbuf_tensor(U("s_xich"), [128, TCH], F32))
                         for _ in range(2)]
                rsch2 = [p3.enter_context(nc.sbuf_tensor(U("s_rsch"), [128, TCH], F32))
                         for _ in range(2)]

                @block.sync
                def _(e):
                    wave(e, c0, [(Aneg[:], A_il_in), (Dil[:], D_il_in)])

                @BS
                def _(e):
                    e.wait_ge(c0.h, c0.c)
                    e.activation(out=Aneg[:], in_=Aneg[:], func=AF.Exp)

                @BV
                def _(e):
                    e.wait_ge(sc.h, sc.c)
                    e.tensor_scalar(out=Aneg[:], in0=Aneg[:], scalar1=-1.0,
                                    scalar2=None, op0=ALU.mult)

                def issue_bc(ch, wev, wgq):
                    @block.sync
                    def _(e, ch=ch, wev=wev, wgq=wgq):
                        wave(e, bcs,
                             [(Bb2[ch % 2][:, n * TCH:(n + 1) * TCH],
                               bc_d[n:n + 1, ch * TCH:(ch + 1) * TCH].partition_broadcast(128))
                              for n in range(NS)] +
                             [(Cb2[ch % 2][:, n * TCH:(n + 1) * TCH],
                               bc_d[NS + n:NS + n + 1, ch * TCH:(ch + 1) * TCH].partition_broadcast(128))
                              for n in range(NS)],
                             waits=[(ev, wev), (s7, s7.c), (gq, wgq)])
                    return bcs.c

                save_ev = [ev.c, ev.c]
                save_gq = [gq.c, gq.c]
                bwa = [0] * NCH
                bwa[0] = issue_bc(0, ev.c, gq.c)
                for ch in range(NCH):
                    Bb = Bb2[ch % 2]
                    Cb = Cb2[ch % 2]
                    bw = bwa[ch]

                    chm_ev = [0, 0]
                    chm_sc = [0, 0]
                    chm_gq = [0, 0]
                    csl = [m0, s0]
                    for g in range(NIT):
                        gp = g % 2
                        dch, dxch = dch2[gp], dxch2[gp]
                        xich, rsch = xich2[gp], rsch2[gp]

                        @block.sync
                        def _(e, ch=ch, g=g, gp=gp, dch=dch, dxch=dxch,
                              xich=xich, rsch=rsch,
                              wv=chm_ev[gp], ws=chm_sc[gp], wg=chm_gq[gp]):
                            wave(e, csl[gp],
                                 [(dch[:], dlt_d[g * 128:(g + 1) * 128,
                                                 ch * TCH:(ch + 1) * TCH]),
                                  (dxch[:], dltx_d[g * 128:(g + 1) * 128,
                                                   ch * TCH:(ch + 1) * TCH]),
                                  (xich[:], xi_d[g * 128:(g + 1) * 128,
                                                 ch * TCH:(ch + 1) * TCH]),
                                  (rsch[:], res_d[g * 128:(g + 1) * 128,
                                                  ch * TCH:(ch + 1) * TCH])],
                                 waits=[(ev, wv), (sc, ws), (gq, wg)])

                        S0 = sc.c
                        E0 = ev.c
                        G0 = gq.c
                        opsn = 4 if ch < NCH - 1 else 3

                        @BS
                        def _(e, ch=ch, g=g, gp=gp, dch=dch, E0=E0, opsn=opsn):
                            e.wait_ge(csl[gp].h, csl[gp].c)
                            e.wait_ge(ev.h, E0)
                            for n in range(NS):
                                if n >= 2:
                                    e.wait_ge(ev.h, E0 + (n - 1) * opsn)
                                e.activation(out=dA2[n % 2][:], in_=dch[:],
                                             func=AF.Exp,
                                             scale=Aneg[:, g * NS + n:g * NS + n + 1])

                        @BG
                        def _(e, ch=ch, g=g, gp=gp, dxch=dxch, E0=E0, opsn=opsn,
                              bw=bw):
                            e.wait_ge(csl[gp].h, csl[gp].c)
                            e.wait_ge(bcs.h, bw)
                            e.wait_ge(ev.h, E0)
                            for n in range(NS):
                                if n >= 2:
                                    e.wait_ge(ev.h, E0 + (n - 1) * opsn)
                                e.tensor_mul(dBx2[n % 2][:], dxch[:],
                                             Bb[:, n * TCH:(n + 1) * TCH])

                        @BV
                        def _(e, ch=ch, g=g, gp=gp, w=s3.c, S0=S0, G0=G0, bw=bw):
                            e.wait_ge(bcs.h, bw)
                            e.wait_ge(s3.h, w)
                            for n in range(NS):
                                dA = dA2[n % 2]
                                e.wait_ge(sc.h, S0 + n + 1)
                                e.wait_ge(gq.h, G0 + n + 1)
                                e.tensor_tensor_scan(
                                    out=hbuf[:], data0=dA[:], data1=dBx2[n % 2][:],
                                    initial=(0.0 if ch == 0 else
                                             stt[:, g * NS + n:g * NS + n + 1]),
                                    op0=ALU.mult, op1=ALU.add)
                                if ch < NCH - 1:
                                    e.tensor_copy(out=stt[:, g * NS + n:g * NS + n + 1],
                                                  in_=hbuf[:, TCH - 1:TCH])
                                e.tensor_mul(ytmp[:], hbuf[:],
                                             Cb[:, n * TCH:(n + 1) * TCH])
                                if n == 0:
                                    e.tensor_copy(out=yacc[:], in_=ytmp[:])
                                else:
                                    e.tensor_add(yacc[:], yacc[:], ytmp[:])

                        @BV
                        def _(e, ch=ch, g=g, gp=gp, xich=xich, rsch=rsch, w=s3.c):
                            e.wait_ge(csl[gp].h, csl[gp].c)
                            e.wait_ge(s3.h, w)
                            e.wait_ge(c0.h, c0.c)
                            e.scalar_tensor_tensor(out=yacc[:], in0=xich[:],
                                                   scalar=Dil[:, g:g + 1],
                                                   in1=yacc[:], op0=ALU.mult,
                                                   op1=ALU.add)
                            e.tensor_mul(ytmp[:], yacc[:], rsch[:])
                            e.tensor_copy(out=ybf[:], in_=ytmp[:])
                        chm_ev[gp] = ev.c
                        chm_sc[gp] = sc.c
                        chm_gq[gp] = gq.c

                        @block.sync
                        def _(e, ch=ch, g=g):
                            wave(e, s3, [(yt_d[g * 128:(g + 1) * 128,
                                               ch * TCH:(ch + 1) * TCH], ybf[:])],
                                 waits=[(ev, ev.c)])
                    save_ev[ch % 2] = ev.c
                    save_gq[ch % 2] = gq.c
                    if ch + 1 < NCH:
                        bwa[ch + 1] = issue_bc(ch + 1, save_ev[(ch + 1) % 2],
                                               save_gq[(ch + 1) % 2])
                barrier()

            with contextlib.ExitStack() as p4:
                wo = p4.enter_context(nc.sbuf_tensor(U("s_wo"), [128, NIT * DM], BF16))
                yt2 = [p4.enter_context(nc.sbuf_tensor(U("s_yt2"), [128, 512], BF16))
                       for _ in range(2)]
                bout = p4.enter_context(nc.sbuf_tensor(U("s_bo"), [128, 8], F32))
                xres = p4.enter_context(nc.sbuf_tensor(U("s_xr2"), [128, 512], F32))
                otmp = p4.enter_context(nc.sbuf_tensor(U("s_ot"), [128, 512], F32))
                psd = [p4.enter_context(nc.psum_tensor(U("s_pd%d" % i), [128, 512], F32))
                       for i in range(8)]

                @block.sync
                def _(e):
                    wave(e, c0, [(bout[:], bout_il_in)])

                @block.sync
                def _(e):
                    wave(e, g1, [(wo[:, cc * DM:(cc + 1) * DM],
                                  WoutT_bf[cc * 128:(cc + 1) * 128, :])
                                 for cc in range(NIT)])

                for tc in range(4):
                    for cc in range(NIT):
                        buf = yt2[cc % 2]

                        @block.sync
                        def _(e, tc=tc, cc=cc, buf=buf, w=mm.c):
                            wave(e, rsl[cc % 2],
                                 [(buf[:], yt_d[cc * 128:(cc + 1) * 128,
                                                tc * 512:tc * 512 + 512])],
                                 waits=[(mm, w), (s3, s3.c)])

                        @block.tensor
                        def _(e, tc=tc, cc=cc, buf=buf):
                            e.wait_ge(rsl[cc % 2].h, rsl[cc % 2].c)
                            e.wait_ge(g1.h, g1.c)
                            if cc == 0:
                                e.wait_ge(ev.h, ev.c)
                            for dm_ in range(8):
                                ins = e.matmul(
                                    psd[dm_][:],
                                    wo[:, cc * DM + dm_ * 128:cc * DM + dm_ * 128 + 128],
                                    buf[:], start=(cc == 0), stop=(cc == NIT - 1))
                            ins.then_inc(mm.h, 1)
                        mm.c += 1

                    for dm_ in range(8):
                        @block.sync
                        def _(e, tc=tc, dm_=dm_):
                            wave(e, m0, [(xres[:], src[b, dm_ * 128:(dm_ + 1) * 128,
                                                       tc * 512:tc * 512 + 512])],
                                 waits=[(ev, ev.c)])

                        @BV
                        def _(e, tc=tc, dm_=dm_):
                            e.wait_ge(mm.h, mm.c)
                            e.wait_ge(m0.h, m0.c)
                            e.wait_ge(c0.h, c0.c)
                            e.wait_ge(s1.h, s1.c)
                            e.scalar_tensor_tensor(out=otmp[:], in0=psd[dm_][:],
                                                   scalar=bout[:, dm_:dm_ + 1],
                                                   in1=xres[:], op0=ALU.add,
                                                   op1=ALU.add)

                        @block.sync
                        def _(e, tc=tc, dm_=dm_):
                            wave(e, s1, [(dst[b, dm_ * 128:(dm_ + 1) * 128,
                                              tc * 512:tc * 512 + 512], otmp[:])],
                                 waits=[(ev, ev.c)])
                barrier()

        def emit_attn(b, src, dst):
            with contextlib.ExitStack() as pa:
                vall = pa.enter_context(nc.sbuf_tensor(U("a_vall"), [128, 16 * 1040], BF16))
                with contextlib.ExitStack() as p1:
                    lnx = p1.enter_context(nc.sbuf_tensor(U("a_lnx"), [128, NDT * T], BF16))
                    with contextlib.ExitStack() as pln:
                        xt2 = [pln.enter_context(nc.sbuf_tensor(U("a_xa"), [128, T], F32)),
                               pln.enter_context(nc.sbuf_tensor(U("a_xb"), [128, T], F32))]
                        tmp = pln.enter_context(nc.sbuf_tensor(U("a_tmp"), [128, T], F32))
                        acc_s = pln.enter_context(nc.sbuf_tensor(U("a_as"), [128, T], F32))
                        acc_q = pln.enter_context(nc.sbuf_tensor(U("a_aq"), [128, T], F32))
                        bc_mean = pln.enter_context(nc.sbuf_tensor(U("a_bm"), [128, T], F32))
                        bc_rstd = pln.enter_context(nc.sbuf_tensor(U("a_br"), [128, T], F32))
                        pst = pln.enter_context(nc.psum_tensor(U("a_pst"), [128, 128], F32))
                        ph = dict(xt2=xt2, tmp=tmp, acc_s=acc_s, acc_q=acc_q,
                                  bc_mean=bc_mean, bc_rstd=bc_rstd, pst=pst)
                        ln_stats(b, src, ph)
                        ln_norm_bf16(b, src, lnx, ph)
                        barrier()

                    wsb = p1.enter_context(nc.sbuf_tensor(U("a_wsb"), [128, 8 * DM], BF16))
                    bil = p1.enter_context(nc.sbuf_tensor(U("a_bil"), [128, 2 * NDT], F32))
                    brow = p1.enter_context(nc.sbuf_tensor(U("a_brow"), [1, 3 * DM], BF16))
                    stg = p1.enter_context(nc.sbuf_tensor(U("a_stg"), [128, 512], BF16))
                    psq = [p1.enter_context(nc.psum_tensor(U("a_p%d" % i), [128, 512], F32))
                           for i in range(2)]

                    @block.sync
                    def _(e):
                        wave(e, c0, [(bil[:, 0:NDT], bq_il_in),
                                     (bil[:, NDT:2 * NDT], bk_il_in)])

                    @block.gpsimd
                    def _(e):
                        wave(e, sw0, [(brow[0:1, i * DM:(i + 1) * DM], srcrow)
                                     for i, srcrow in enumerate(
                                         (bq_row_in, bk_row_in, bv_row_in))])

                    def load_w(W_in_ap):
                        @block.sync
                        def _(e, W_in_ap=W_in_ap, w=mm.c):
                            wave(e, g1, [(wsb[:, mc * DM:(mc + 1) * DM],
                                          W_in_ap[mc * 128:(mc + 1) * 128, :])
                                         for mc in range(8)],
                                 waits=[(mm, w)])

                    for wi, (W_in_ap, dram, bslot) in enumerate(
                            ((Wq_bf, qt_d, 0), (Wk_bf, kt_d, 1))):
                        load_w(W_in_ap)
                        for hdc in range(NDT):
                            for tc in range(4):
                                pb = psq[tc % 2]

                                @block.tensor
                                def _(e, hdc=hdc, tc=tc, pb=pb, w=sc.c):
                                    e.wait_ge(g1.h, g1.c)
                                    e.wait_ge(sc.h, w)
                                    for mc in range(8):
                                        ins = e.matmul(
                                            pb[:],
                                            wsb[:, mc * DM + hdc * 128:mc * DM + hdc * 128 + 128],
                                            lnx[:, mc * T + tc * 512:mc * T + tc * 512 + 512],
                                            start=(mc == 0), stop=(mc == 7))
                                    ins.then_inc(mm.h, 1)
                                mm.c += 1

                                @BS
                                def _(e, hdc=hdc, tc=tc, pb=pb, bslot=bslot):
                                    e.wait_ge(mm.h, mm.c)
                                    e.wait_ge(s4.h, s4.c)
                                    e.wait_ge(c0.h, c0.c)
                                    e.activation(out=stg[:], in_=pb[:], func=AF.Identity,
                                                 bias=bil[:, bslot * NDT + hdc:bslot * NDT + hdc + 1])

                                @block.sync
                                def _(e, hdc=hdc, tc=tc, dram=dram):
                                    wave(e, s4, [(dram[hdc * 128:(hdc + 1) * 128,
                                                       tc * 512:tc * 512 + 512], stg[:])],
                                         waits=[(sc, sc.c)])

                    load_w(Wv_bf)

                    @BV
                    def _(e):
                        e.memset(vall[:].rearrange("p (c h u) -> p c h u",
                                                   c=16, h=16)[:, :, :, 64:65], 1.0)
                    for tt in range(16):
                        for half in range(2):
                            pb = psq[half]

                            @block.tensor
                            def _(e, tt=tt, half=half, pb=pb, w=ev.c):
                                e.wait_ge(g1.h, g1.c)
                                e.wait_ge(g2.h, g2.c)
                                e.wait_ge(sw0.h, sw0.c)
                                e.wait_ge(c0.h, c0.c)
                                e.wait_ge(sc.h, sc.c)
                                e.wait_ge(ev.h, w)
                                for mc in range(8):
                                    e.matmul(pb[:],
                                             lnx[:, mc * T + tt * 128:mc * T + tt * 128 + 128],
                                             wsb[:, mc * DM + half * 512:mc * DM + half * 512 + 512],
                                             start=(mc == 0), stop=False)
                                ins = e.matmul(
                                    pb[:], ones_row[:],
                                    brow[0:1, 2 * DM + half * 512:2 * DM + half * 512 + 512],
                                    start=False, stop=True)
                                ins.then_inc(mm.h, 1)
                            mm.c += 1

                            @BV
                            def _(e, tt=tt, half=half, pb=pb):
                                e.wait_ge(mm.h, mm.c)
                                base = tt * 1040 + half * 520
                                e.tensor_copy(
                                    out=vall[:, base:base + 520].rearrange(
                                        "p (h u) -> p h u", h=8)[:, :, 0:64],
                                    in_=pb[:].rearrange("p (h u) -> p h u", h=8))

                    if b == 0:
                        relsb = p1.enter_context(
                            nc.sbuf_tensor(U("a_rel"), [128, 8 * 512], BF16))

                        @block.sync
                        def _(e, w=mm.c):
                            wave(e, g2, [(relsb[:, mc * 512:(mc + 1) * 512],
                                          relembT_bf[mc * 128:(mc + 1) * 128, :])
                                         for mc in range(8)],
                                 waits=[(mm, w)])
                        for wi, (W_in_ap, dram, bro) in enumerate(
                                ((Wk_bf, poskcd_d, 1), (Wq_bf, posqcd_d, 0))):
                            load_w(W_in_ap)
                            for cc in range(4):
                                for half in range(2):
                                    pb = psq[half]

                                    @block.tensor
                                    def _(e, cc=cc, half=half, pb=pb, bro=bro, w=sc.c):
                                        e.wait_ge(g1.h, g1.c)
                                        e.wait_ge(g2.h, g2.c)
                                        e.wait_ge(sw0.h, sw0.c)
                                        e.wait_ge(c0.h, c0.c)
                                        e.wait_ge(ev.h, ev.c)
                                        e.wait_ge(sc.h, w)
                                        for mc in range(8):
                                            e.matmul(
                                                pb[:],
                                                relsb[:, mc * 512 + cc * 128:mc * 512 + cc * 128 + 128],
                                                wsb[:, mc * DM + half * 512:mc * DM + half * 512 + 512],
                                                start=(mc == 0), stop=False)
                                        ins = e.matmul(
                                            pb[:], ones_row[:],
                                            brow[0:1, bro * DM + half * 512:bro * DM + half * 512 + 512],
                                            start=False, stop=True)
                                        ins.then_inc(mm.h, 1)
                                    mm.c += 1

                                    @BS
                                    def _(e, cc=cc, half=half, pb=pb):
                                        e.wait_ge(mm.h, mm.c)
                                        e.wait_ge(s4.h, s4.c)
                                        e.activation(out=stg[:], in_=pb[:], func=AF.Copy)

                                    @block.sync
                                    def _(e, cc=cc, half=half, dram=dram):
                                        wave(e, s4,
                                             [(dram[cc * 128:(cc + 1) * 128,
                                                    half * 512:half * 512 + 512], stg[:])],
                                             waits=[(sc, sc.c)])
                    barrier()

                with contextlib.ExitStack() as p2:
                    esb = p2.enter_context(nc.sbuf_tensor(U("a_esb"), [128, 2 * 4 * TBLP], BF16))
                    pk4 = p2.enter_context(nc.sbuf_tensor(U("a_pk4"), [128, 2 * 256], BF16))
                    tblk = p2.enter_context(nc.sbuf_tensor(U("a_tbk"), [64, TBLP], BF16))
                    tblq = p2.enter_context(nc.sbuf_tensor(U("a_tbq"), [64, TBLP], BF16))
                    qth = p2.enter_context(nc.sbuf_tensor(U("a_qth"), [64, T], BF16))
                    kth = p2.enter_context(nc.sbuf_tensor(U("a_kth"), [64, T], BF16))
                    m1sb = [p2.enter_context(nc.sbuf_tensor(U("a_m1"), [128, TBLP], BF16))
                            for _ in range(2)]
                    m2sb = [p2.enter_context(nc.sbuf_tensor(U("a_m2"), [128, TBLP], BF16))
                            for _ in range(2)]
                    m2bias = p2.enter_context(nc.sbuf_tensor(U("a_m2b"), [128, 32], F32))
                    g1sb = p2.enter_context(nc.sbuf_tensor(U("a_g1"), [128, 4 * 1152], BF16))
                    g2sb = p2.enter_context(nc.sbuf_tensor(U("a_g2"), [128, 16 * 512], BF16))
                    ptb = p2.enter_context(nc.sbuf_tensor(U("a_pt"), [128, 16 * 512], BF16))
                    rowlo = p2.enter_context(nc.sbuf_tensor(U("a_rlo"), [1, 512], BF16))
                    rowhi = p2.enter_context(nc.sbuf_tensor(U("a_rhi"), [1, 512], BF16))
                    ctxsb = p2.enter_context(nc.sbuf_tensor(U("a_ctx"), [64, T], F32))
                    dnb = p2.enter_context(nc.sbuf_tensor(U("a_dnb"), [64, T], F32))
                    xh = p2.enter_context(nc.sbuf_tensor(U("a_xh"), [64, T], F32))
                    psS = [p2.enter_context(nc.psum_tensor(U("a_s%d" % i), [128, 512], F32))
                           for i in range(2)]
                    psM2 = [p2.enter_context(nc.psum_tensor(U("a_m%d" % i), [128, 1024], F32))
                            for i in range(2)]
                    psC = p2.enter_context(nc.psum_tensor(U("a_c"), [65, 512], F32))
                    psE = p2.enter_context(nc.psum_tensor(U("a_e"), [1, 512], F32))

                    @block.sync
                    def _(e):
                        wave(e, c0,
                             [(esb[:, 0:4 * TBLP],
                               Erev_in.rearrange("(c p) w -> p c w", p=128)),
                              (esb[:, 4 * TBLP:8 * TBLP],
                               Efwd_in.rearrange("(c p) w -> p c w", p=128))])

                    for h in range(H):
                        @block.sync
                        def _(e, h=h, w=mm.c):
                            wave(e, m0,
                                 [(pk4[:, ti * 256:(ti + 1) * 256].rearrange(
                                     "p (c u) -> p c u", c=4),
                                   bass.AP(dram.tensor, h * 64,
                                           [[DM, 128], [128 * DM, 4], [1, 64]]))
                                  for ti, dram in enumerate((poskcd_d, posqcd_d))],
                                 waits=[(mm, w), (s4, s4.c)])
                            wave(e, x0, [(qth[:], qt_d[h * 64:(h + 1) * 64, :]),
                                         (kth[:], kt_d[h * 64:(h + 1) * 64, :])],
                                 waits=[(mm, w), (s4, s4.c)])

                        tmark = [ev.c, ev.c]
                        for ti, tbl in enumerate((tblk, tblq)):
                            for w3 in range(3):
                                q = ti * 3 + w3
                                pt = psM2[q % 2]

                                @block.tensor
                                def _(e, h=h, ti=ti, w3=w3, pt=pt, w=tmark[q % 2]):
                                    e.wait_ge(m0.h, m0.c)
                                    e.wait_ge(c0.h, c0.c)
                                    e.wait_ge(ev.h, w)
                                    for cc in range(4):
                                        ins = e.matmul(
                                            pt[0:64, 0:512],
                                            pk4[:, ti * 256 + cc * 64:ti * 256 + cc * 64 + 64],
                                            esb[:, ti * 4 * TBLP + cc * TBLP + w3 * 512:
                                                ti * 4 * TBLP + cc * TBLP + w3 * 512 + 512],
                                            start=(cc == 0), stop=(cc == 3))
                                    ins.then_inc(mm.h, 1)
                                mm.c += 1

                                @BV
                                def _(e, ti=ti, w3=w3, tbl=tbl, pt=pt):
                                    e.wait_ge(mm.h, mm.c)
                                    e.tensor_copy(out=tbl[:, w3 * 512:w3 * 512 + 512],
                                                  in_=pt[0:64, 0:512])
                                tmark[q % 2] = ev.c

                        mevk = [tmark[0], tmark[1]]
                        mevs = ev.c
                        for mi, (qk, tbl, ssl) in enumerate(((qth, tblk, s4),
                                                             (kth, tblq, s5))):
                            msbl = m1sb if mi == 0 else m2sb
                            wrm = [ssl.c, ssl.c]
                            for it in range(16):
                                msb = msbl[it % 2]
                                for w3 in range(3):
                                    if w3 == 2:
                                        pma = psS[0][:, 0:512]
                                        wv = mevs
                                    else:
                                        pma = psM2[it % 2][:, w3 * 512:w3 * 512 + 512]
                                        wv = mevk[it % 2]

                                    @block.tensor
                                    def _(e, mi=mi, it=it, w3=w3, qk=qk, tbl=tbl,
                                          pma=pma, w=sc.c, wv=wv):
                                        e.wait_ge(x0.h, x0.c)
                                        e.wait_ge(sc.h, w)
                                        e.wait_ge(ev.h, wv)
                                        e.matmul(pma, qk[:, it * 128:it * 128 + 128],
                                                 tbl[:, w3 * 512:w3 * 512 + 512],
                                                 start=True, stop=True).then_inc(mm.h, 1)
                                    mm.c += 1

                                    if w3 == 1:
                                        @BV
                                        def _(e, mi=mi, it=it, msb=msb,
                                              ssl=ssl, w=wrm[it % 2]):
                                            e.wait_ge(mm.h, mm.c)
                                            e.wait_ge(ssl.h, w)
                                            e.tensor_copy(out=msb[:, 0:1024],
                                                          in_=psM2[it % 2][:, 0:1024])
                                        mevk[it % 2] = ev.c
                                    elif w3 == 2:
                                        @BV
                                        def _(e, mi=mi, it=it, msb=msb,
                                              ssl=ssl, w=wrm[it % 2]):
                                            e.wait_ge(mm.h, mm.c)
                                            e.wait_ge(ssl.h, w)
                                            e.tensor_copy(out=msb[:, 1024:1536],
                                                          in_=psS[0][:, 0:512])
                                        mevs = ev.c

                                if mi == 1:
                                    @BV
                                    def _(e, it=it, msb=msb):
                                        e.tensor_scalar(out=m2bias[:, it * 2:it * 2 + 1],
                                                        in0=msb[:, 1276:1277],
                                                        scalar1=1.0 / SCALE,
                                                        scalar2=None, op0=ALU.mult)
                                        e.tensor_scalar(out=m2bias[:, it * 2 + 1:it * 2 + 2],
                                                        in0=msb[:, 254:255],
                                                        scalar1=1.0 / SCALE,
                                                        scalar2=None, op0=ALU.mult)

                                @block.sync
                                def _(e, mi=mi, it=it, msb=msb, ssl=ssl):
                                    wave(e, ssl,
                                         [(bass.AP((m1d if mi == 0 else m2d).tensor,
                                                   MGUARD + it * MBLK,
                                                   [[TBLP, 128], [1, TBLP]]), msb[:])],
                                         waits=[(ev, ev.c)])
                                wrm[it % 2] = ssl.c

                        for ic in range(4):
                            @block.sync
                            def _(e, h=h, ic=ic, w=mm.c):
                                wave(e, g3,
                                     [(g1sb[:].rearrange("p (b u) -> p b u", b=4),
                                       bass.AP(m1d.tensor, MGUARD + (4 * ic) * MBLK + 253,
                                               [[TBLP - 1, 128], [MBLK, 4], [1, 1152]]))],
                                     waits=[(mm, w), (s4, s4.c)])

                            @block.sync
                            def _(e, h=h, ic=ic, w=mm.c):
                                wave(e, x1,
                                     [(g2sb[:].rearrange("p (j u) -> p j u", j=16),
                                       bass.AP(m2d.tensor, MGUARD + HW + ic * 512,
                                               [[TBLP - 1, 128], [MBLK - 128, 16], [1, 512]]))],
                                     waits=[(mm, w), (s5, s5.c)])

                            for ei, (col, rbuf) in enumerate(((254, rowhi), (1276, rowlo))):
                                @block.tensor
                                def _(e, ic=ic, col=col, w=ev.c):
                                    e.wait_ge(ev.h, w)
                                    e.matmul(psE[:], tblk[:, col:col + 1],
                                             qth[:, ic * 512:ic * 512 + 512],
                                             start=True, stop=True).then_inc(mm.h, 1)
                                mm.c += 1

                                @BV
                                def _(e, rbuf=rbuf):
                                    e.wait_ge(mm.h, mm.c)
                                    e.tensor_copy(out=rbuf[:], in_=psE[:])

                            S0 = sc.c
                            jinfo = []
                            cum = [0]
                            for jt in range(16):
                                deltas = [4 * ic + ib - jt for ib in range(4)]
                                lo_ibs = [ib for ib in range(4) if deltas[ib] <= -(BANDR + 1)]
                                hi_ibs = [ib for ib in range(4) if deltas[ib] >= BANDR + 1]
                                band_ibs = [ib for ib in range(4) if abs(deltas[ib]) <= BANDR]
                                jinfo.append((lo_ibs, hi_ibs, band_ibs))
                                cum.append(cum[-1] + (1 if lo_ibs else 0) +
                                           (1 if band_ibs else 0) + (1 if hi_ibs else 0))
                            mmarks = []
                            for jt in range(16):
                                pb = psS[jt % 2]
                                lo_ibs, hi_ibs, band_ibs = jinfo[jt]

                                @block.tensor
                                def _(e, h=h, ic=ic, jt=jt, pb=pb,
                                      w=S0 + (cum[jt - 1] if jt >= 2 else 0),
                                      lo_ibs=lo_ibs, hi_ibs=hi_ibs, band_ibs=band_ibs):
                                    if jt == 0:
                                        e.wait_ge(x0.h, x0.c)
                                        e.wait_ge(x1.h, x1.c)
                                        e.wait_ge(g3.h, g3.c)
                                        e.wait_ge(c0.h, c0.c)
                                        e.wait_ge(ev.h, ev.c)
                                    ops = [("qk",)]
                                    if lo_ibs:
                                        ops.append(("lo",))
                                    if hi_ibs:
                                        ops.append(("hi",))
                                    if band_ibs:
                                        ops.append(("id",))
                                        ops.extend(("tr", ib) for ib in band_ibs)
                                    for oi, op in enumerate(ops):
                                        st_ = (oi == 0)
                                        sp_ = (oi == len(ops) - 1)
                                        if op[0] == "qk":
                                            ins = e.matmul(pb[:], kth[:, jt * 128:jt * 128 + 128],
                                                           qth[:, ic * 512:ic * 512 + 512],
                                                           start=st_, stop=sp_)
                                            ins._wait_ge(sc.h, w)
                                        elif op[0] == "lo":
                                            i0, i1 = lo_ibs[0] * 128, (lo_ibs[-1] + 1) * 128
                                            ins = e.matmul(pb[:, i0:i1], ones_row[:],
                                                           rowlo[0:1, i0:i1],
                                                           start=st_, stop=sp_)
                                        elif op[0] == "hi":
                                            i0, i1 = hi_ibs[0] * 128, (hi_ibs[-1] + 1) * 128
                                            ins = e.matmul(pb[:, i0:i1], ones_row[:],
                                                           rowhi[0:1, i0:i1],
                                                           start=st_, stop=sp_)
                                        elif op[0] == "id":
                                            i0, i1 = band_ibs[0] * 128, (band_ibs[-1] + 1) * 128
                                            ins = e.matmul(pb[:, i0:i1], identb[:],
                                                           g2sb[:, jt * 512 + i0:jt * 512 + i1],
                                                           start=st_, stop=sp_)
                                        else:
                                            ib = op[1]
                                            off = ib * 1152 + (jt - (4 * ic + ib) + BANDR) * 128
                                            ins = e.matmul(pb[:, ib * 128:ib * 128 + 128],
                                                           g1sb[:, off:off + 128], identb[:],
                                                           start=st_, stop=sp_)
                                    ins.then_inc(mm.h, 1)
                                mm.c += 1
                                mmarks.append(mm.c)

                            @BS
                            def _(e, h=h, ic=ic, jinfo=jinfo, mmarks=mmarks):
                                e.wait_ge(ev.h, ev.c)
                                for jt in range(16):
                                    lo_ibs, hi_ibs, band_ibs = jinfo[jt]
                                    pb = psS[jt % 2]
                                    e.wait_ge(mm.h, mmarks[jt])
                                    regions = []
                                    if lo_ibs:
                                        regions.append((lo_ibs[0] * 128, (lo_ibs[-1] + 1) * 128,
                                                        m2bias[:, jt * 2:jt * 2 + 1]))
                                    if band_ibs:
                                        regions.append((band_ibs[0] * 128,
                                                        (band_ibs[-1] + 1) * 128,
                                                        zcol[:, 0:1]))
                                    if hi_ibs:
                                        regions.append((hi_ibs[0] * 128, (hi_ibs[-1] + 1) * 128,
                                                        m2bias[:, jt * 2 + 1:jt * 2 + 2]))
                                    for (r0_, r1_, bias) in regions:
                                        e.activation(out=ptb[:, jt * 512 + r0_:jt * 512 + r1_],
                                                     in_=pb[:, r0_:r1_], func=AF.Exp,
                                                     scale=1.0 / SCALE, bias=bias)

                            @block.tensor
                            def _(e, h=h, ic=ic):
                                e.wait_ge(ev.h, ev.c)
                                for ck in range(16):
                                    ins = e.matmul(
                                        psC[:],
                                        vall[:, ck * 1040 + h * 65:ck * 1040 + h * 65 + 65],
                                        ptb[:, ck * 512:ck * 512 + 512],
                                        start=(ck == 0), stop=(ck == 15))
                                    if ck == 0:
                                        ins._wait_ge(sc.h, sc.c)
                                ins.then_inc(mm.h, 1)
                            mm.c += 1

                            @BV
                            def _(e, ic=ic):
                                e.wait_ge(mm.h, mm.c)
                                e.wait_ge(rw.h, rw.c)
                                e.tensor_copy(out=rows[0:1, R_DEN * T + ic * 512:
                                                  R_DEN * T + ic * 512 + 512],
                                              in_=psC[64:65, :])

                            @BV
                            def _(e, ic=ic):
                                e.wait_ge(mm.h, mm.c)
                                e.wait_ge(bcs.h, bcs.c)
                                e.tensor_copy(out=ctxsb[:, ic * 512:ic * 512 + 512],
                                              in_=psC[0:64, :])

                        @BS
                        def _(e, h=h):
                            e.wait_ge(ev.h, ev.c)
                            e.activation(out=R(R_DEN), in_=R(R_DEN), func=AF.Ln)
                            e.activation(out=R(R_DEN), in_=R(R_DEN), func=AF.Exp,
                                         scale=-1.0)
                        bcast_row([(dnb[:], R(R_DEN), 64)], waits=[(sc, sc.c)])

                        @block.sync
                        def _(e, h=h, w=s1.c):
                            wave(e, m0, [(xh[:], src[b, h * 64:(h + 1) * 64, :])],
                                 waits=[(ev, ev.c), (s1, w)])

                        @BV
                        def _(e, h=h):
                            e.wait_ge(m0.h, m0.c)
                            e.wait_ge(bcs.h, bcs.c)
                            e.tensor_mul(ctxsb[:], ctxsb[:], dnb[:])
                            e.tensor_add(xh[:], xh[:], ctxsb[:])

                        @block.sync
                        def _(e, h=h):
                            wave(e, s1, [(dst[b, h * 64:(h + 1) * 64, :], xh[:])],
                                 waits=[(ev, ev.c)])
                    barrier()

        first = True
        for sl in SUBLAYERS:
            src = xin if first else xs
            dst = outT if sl == SUBLAYERS[-1] else xs
            if sl == 0:
                for b in range(BPC):
                    emit_attn(b, src, dst)
            elif sl == 2:
                for b in range(BPC):
                    emit_ssm(b, src, dst)
            else:
                for b in range(BPC):
                    emit_ffn(b, src, dst)
            first = False

        @block.sync
        def _(e):
            for s_ in SYNC_SLOTS:
                e.wait_ge(s_.h, s_.c)

        @block.gpsimd
        def _(e):
            for s_ in GQ_SLOTS:
                e.wait_ge(s_.h, s_.c)

    gstack.close()
    return nc



def host_weights(inputs):
    g = lambda k: np.asarray(inputs[k])
    Ef, Er = build_E()
    d = {}
    d["relembT"] = np.ascontiguousarray(g("L0_attn_a_rel_embeddings").T)
    d["Wq"] = np.ascontiguousarray(g("L0_attn_a_Wq"))
    d["Wk"] = np.ascontiguousarray(g("L0_attn_a_Wk"))
    d["Wv"] = np.ascontiguousarray(g("L0_attn_a_Wv"))
    for nm, key in (("bq", "L0_attn_a_bq"), ("bk", "L0_attn_a_bk"),
                    ("bv", "L0_attn_a_bv")):
        v = g(key).astype(np.float32)
        d[nm + "_row"] = np.ascontiguousarray(v[None, :])
        d[nm + "_il"] = il(v, NDT)
    d["Erev"] = Er
    d["Efwd"] = Ef
    d["identf"] = np.eye(128, dtype=np.float32)
    d["identb"] = np.eye(128).astype(BF)
    d["ones_row"] = np.ones((1, 128), dtype=BF)
    d["w1T"] = np.ascontiguousarray(g("L0_ffn_w1").T)
    d["w2T"] = np.ascontiguousarray(g("L0_ffn_w2").T)
    cw = g("L0_ffn_conv_w").reshape(DH, 3)
    d["fconvw_il"] = np.ascontiguousarray(
        cw.reshape(32, 128, 3).transpose(1, 0, 2).reshape(128, 96)).astype(np.float32)
    d["fconvb_il"] = il(g("L0_ffn_conv_b").astype(np.float32), 32)
    d["WinT"] = np.ascontiguousarray(g("L0_attn_b_W_in").T)
    d["WxT"] = np.ascontiguousarray(g("L0_attn_b_W_x").T)
    d["WoutT"] = np.ascontiguousarray(g("L0_attn_b_W_out").T)
    d["bin_il"] = il(g("L0_attn_b_b_in").astype(np.float32), 32)
    bx = g("L0_attn_b_b_x").astype(np.float32)
    d["bxd_il"] = il(bx[:DI], 16)
    d["bx_bc"] = np.ascontiguousarray(bx[DI:][:, None])
    d["bout_il"] = il(g("L0_attn_b_b_out").astype(np.float32), 8)
    d["bconv_il"] = il(g("L0_attn_b_b_conv").astype(np.float32), 16)
    acw = g("L0_attn_b_W_conv").reshape(DI, 4)
    d["aconvw_il"] = np.ascontiguousarray(
        acw.reshape(16, 128, 4).transpose(1, 0, 2).reshape(128, 64)).astype(np.float32)
    Alog = g("L0_attn_b_A_log").astype(np.float32)
    d["A_il"] = np.ascontiguousarray(
        Alog.reshape(16, 128, 16).transpose(1, 0, 2).reshape(128, 256))
    d["D_il"] = il(g("L0_attn_b_D").astype(np.float32), 16)
    for k in d:
        if d[k].dtype == np.float64:
            d[k] = d[k].astype(np.float32)
    return d


def kernel(**inputs):
    x = np.asarray(inputs["L0_attn_a_hidden_states"])
    shared = host_weights(inputs)
    in_maps = []
    for c in range(NCORES):
        xc = x[c * BPC:(c + 1) * BPC]
        m = dict(shared)
        m["xT"] = np.ascontiguousarray(xc.transpose(0, 2, 1))
        in_maps.append(m)
    ncores = DEV_CORES if DEV_CORES else NCORES
    nc = build_program()
    res = run_bass_kernel_spmd(nc, in_maps[:ncores], core_ids=list(range(ncores)))
    outs = []
    for c in range(ncores):
        o = np.asarray(res.results[c]["outT"])
        outs.append(o.transpose(0, 2, 1))
    out = np.concatenate(outs, axis=0).astype(np.float32)
    if out.shape[0] < x.shape[0]:
        out = np.concatenate(
            [out, np.zeros((x.shape[0] - out.shape[0],) + out.shape[1:], np.float32)], 0)
    return out


# version 18
# speedup vs baseline: 1.0001x; 1.0001x over previous
import contextlib
import math
import os as _os
import numpy as np
import ml_dtypes
import concourse.bass as bass
import concourse.mybir as mybir
from concourse.bass_utils import run_bass_kernel_spmd

F32 = mybir.dt.float32
BF16 = mybir.dt.bfloat16
AF = mybir.ActivationFunctionType
ALU = mybir.AluOpType
AX = mybir.AxisListType
BF = ml_dtypes.bfloat16

NCORES = 8
BPC = 2
T = 2048
DM = 1024
NDT = DM // 128
H = 16
HD = 64
SCALE = math.sqrt(HD * 3.0)
DH = 4096
NCT = DH // 128
DI = 2048
NIT = DI // 128
NS = 16
EPS = 1e-5
MAGIC = float(3 << 22)

HW = 765
TBLW = 2 * HW + 1
TBLP = 1536
MBLK = 128 * TBLP
MGUARD = MBLK
BANDR = 4

SUBLAYERS = tuple(int(t) for t in _os.environ.get("SUBL", "0,1,2,3").split(","))
DEV_CORES = int(_os.environ.get("DEV_CORES", "0"))
TCH = 512
NCH = T // TCH

R_MEAN, R_RSTD, R_MAX, R_TMP, R_T2, R_DQ = range(6)
R_DEN = R_MAX
R_QS = R_T2


def _log_bucket_np(rel):
    mid = 128
    rel = np.clip(rel, -511, 511)
    sign = np.sign(rel).astype(np.float64)
    abs_pos = np.maximum(np.abs(rel), 1)
    log_base = math.log(511 / mid)
    log_pos = np.ceil(np.log(abs_pos.astype(np.float64) / mid) /
                      log_base * (mid - 1)) + mid
    bucket = np.where(abs_pos < mid, abs_pos.astype(np.float64), log_pos) * sign
    return bucket.astype(np.int64)


def _col_of_d(d):
    return np.clip(_log_bucket_np(np.asarray(d)) + 256, 0, 511)


def build_E():
    cols_f = _col_of_d(np.arange(TBLW) - HW)
    cols_r = _col_of_d(HW - np.arange(TBLW))
    Ef = np.zeros((512, TBLP), dtype=BF)
    Er = np.zeros((512, TBLP), dtype=BF)
    Ef[cols_f, np.arange(TBLW)] = BF(1.0)
    Er[cols_r, np.arange(TBLW)] = BF(1.0)
    return Ef, Er


def il(v, k):
    return np.ascontiguousarray(np.asarray(v).reshape(k, 128).T)


class SP:
    def __init__(self, h):
        self.h = h
        self.c = 0


_AUTO_OPS = frozenset((
    "tensor_copy", "tensor_tensor", "tensor_add", "tensor_sub", "tensor_mul",
    "tensor_max", "tensor_scalar", "tensor_single_scalar", "tensor_scalar_add",
    "tensor_scalar_sub", "tensor_scalar_mul", "tensor_scalar_max",
    "tensor_scalar_min", "tensor_relu", "tensor_reduce", "reduce_max",
    "reduce_sum", "scalar_tensor_tensor", "tensor_tensor_scan", "memset",
    "memzero", "reciprocal", "select", "copy_predicated",
    "activation", "copy", "add", "mul", "square", "sqrt", "sign",
))


class EP:

    def __init__(self, e, sp):
        self._e = e
        self._sp = sp

    def __getattr__(self, name):
        fn = getattr(self._e, name)
        if name in _AUTO_OPS:
            sp = self._sp
            e = self._e

            def call(*a, **k):
                ins = fn(*a, **k)
                ins._wait_ge(sp.h, sp.c)
                ins.then_inc(sp.h, 1)
                sp.c += 1
                return ins
            return call
        return fn


def build_program():
    nc = bass.Bass("TRN2", target_bir_lowering=False)
    gstack = contextlib.ExitStack()
    _uid = [0]

    def U(name):
        _uid[0] += 1
        return "%s_%d" % (name, _uid[0])

    def din(name, shape, dt):
        return nc.dram_tensor(name, list(shape), dt, kind="ExternalInput").ap()

    xin = din("xT", (BPC, DM, T), F32)
    outT = nc.dram_tensor("outT", [BPC, DM, T], F32, kind="ExternalOutput").ap()

    relembT_in = din("relembT", (DM, 512), F32)
    Wq_in = din("Wq", (DM, DM), F32)
    Wk_in = din("Wk", (DM, DM), F32)
    Wv_in = din("Wv", (DM, DM), F32)
    bq_row_in = din("bq_row", (1, DM), F32)
    bk_row_in = din("bk_row", (1, DM), F32)
    bv_row_in = din("bv_row", (1, DM), F32)
    bq_il_in = din("bq_il", (128, NDT), F32)
    bk_il_in = din("bk_il", (128, NDT), F32)
    Erev_in = din("Erev", (512, TBLP), BF16)
    Efwd_in = din("Efwd", (512, TBLP), BF16)
    identf_in = din("identf", (128, 128), F32)
    identb_in = din("identb", (128, 128), BF16)
    ones_row_in = din("ones_row", (1, 128), BF16)

    w1T_in = din("w1T", (DM, DH), F32)
    w2T_in = din("w2T", (DH, DM), F32)
    fconvw_in = din("fconvw_il", (128, 96), F32)
    fconvb_in = din("fconvb_il", (128, 32), F32)

    WinT_in = din("WinT", (DM, 2 * DI), F32)
    WxT_in = din("WxT", (DI, DI + 2 * NS), F32)
    WoutT_in = din("WoutT", (DI, DM), F32)
    bin_il_in = din("bin_il", (128, 32), F32)
    bxd_il_in = din("bxd_il", (128, 16), F32)
    bx_bc_in = din("bx_bc", (32, 1), F32)
    bout_il_in = din("bout_il", (128, 8), F32)
    bconv_il_in = din("bconv_il", (128, 16), F32)
    aconvw_in = din("aconvw_il", (128, 64), F32)
    A_il_in = din("A_il", (128, 256), F32)
    D_il_in = din("D_il", (128, 16), F32)

    xs = nc.dram_tensor("xs", [BPC, DM, T], F32, kind="Internal").ap()
    WinT_bf = nc.dram_tensor("WinT_bf", [DM, 2 * DI], BF16, kind="Internal").ap()
    WxT_bf = nc.dram_tensor("WxT_bf", [DI, DI + 2 * NS], BF16, kind="Internal").ap()
    WoutT_bf = nc.dram_tensor("WoutT_bf", [DI, DM], BF16, kind="Internal").ap()
    Wq_bf = nc.dram_tensor("Wq_bf", [DM, DM], BF16, kind="Internal").ap()
    Wk_bf = nc.dram_tensor("Wk_bf", [DM, DM], BF16, kind="Internal").ap()
    Wv_bf = nc.dram_tensor("Wv_bf", [DM, DM], BF16, kind="Internal").ap()
    relembT_bf = nc.dram_tensor("relembT_bf", [DM, 512], BF16, kind="Internal").ap()
    m1d = nc.dram_tensor("m1d", [2 * MGUARD + 16 * MBLK], BF16, kind="Internal").ap()
    m2d = nc.dram_tensor("m2d", [2 * MGUARD + 16 * MBLK], BF16, kind="Internal").ap()
    qt_d = nc.dram_tensor("qt_d", [DM, T], BF16, kind="Internal").ap()
    kt_d = nc.dram_tensor("kt_d", [DM, T], BF16, kind="Internal").ap()
    poskcd_d = nc.dram_tensor("poskcd_d", [512, DM], BF16, kind="Internal").ap()
    posqcd_d = nc.dram_tensor("posqcd_d", [512, DM], BF16, kind="Internal").ap()
    w1q_d = nc.dram_tensor("w1q_d", [DM, DH], BF16, kind="Internal").ap()
    w2q_d = nc.dram_tensor("w2q_d", [DH, DM], BF16, kind="Internal").ap()
    hsil_d = nc.dram_tensor("hsil_d", [DH, T], F32, kind="Internal").ap()
    xq2_d = nc.dram_tensor("xq2_d", [DH, T], BF16, kind="Internal").ap()
    xi_d = nc.dram_tensor("xi_d", [DI, T], F32, kind="Internal").ap()
    res_d = nc.dram_tensor("res_d", [DI, T], F32, kind="Internal").ap()
    dlt_d = nc.dram_tensor("dlt_d", [DI, T], F32, kind="Internal").ap()
    dltx_d = nc.dram_tensor("dltx_d", [DI, T], F32, kind="Internal").ap()
    yt_d = nc.dram_tensor("yt_d", [DI, T], BF16, kind="Internal").ap()
    row_d = nc.dram_tensor("row_d", [1, T], F32, kind="Internal").ap()
    bc_d = nc.dram_tensor("bc_d", [32, T], BF16, kind="Internal").ap()

    with nc.Block() as block:
        identf = gstack.enter_context(nc.sbuf_tensor("sb_identf", [128, 128], F32))
        identb = gstack.enter_context(nc.sbuf_tensor("sb_identb", [128, 128], BF16))
        ones_row = gstack.enter_context(nc.sbuf_tensor("sb_ones", [1, 128], BF16))
        rows = gstack.enter_context(nc.sbuf_tensor("sb_rows", [1, 6 * T], F32))
        rows2 = gstack.enter_context(nc.sbuf_tensor("sb_rows2", [1, 64], F32))
        red16 = gstack.enter_context(nc.sbuf_tensor("sb_red16", [128, 16], F32))
        r16t = gstack.enter_context(nc.sbuf_tensor("sb_r16t", [16, 128], F32))
        zcol = gstack.enter_context(nc.sbuf_tensor("sb_zcol", [128, 1], F32))

        sems = {}

        def sp(name):
            if name not in sems:
                sems[name] = SP(gstack.enter_context(nc.semaphore(name)))
            return sems[name]

        mm = sp("mm")
        ev = sp("ev")
        sc = sp("sc")
        SYNC_SLOTS = [sp(n) for n in
                      ("c0", "x0", "x1", "r0", "r1", "rw", "bc", "m0",
                       "s0", "s1", "s2", "s3", "s4", "s5", "s6", "s7")]
        GQ_SLOTS = [sp(n) for n in ("g0", "g1", "g2", "g3", "sw0")]
        (c0, x0, x1, r0, r1, rw, bcs, m0,
         s0, s1, s2, s3, s4, s5, s6, s7) = SYNC_SLOTS
        g0, g1, g2, g3, sw0 = GQ_SLOTS
        xsl = [x0, x1]
        rsl = [r0, r1]
        bsy = sp("bsy")
        bte = sp("bte")
        bve = sp("bve")
        bse = sp("bse")
        bge = sp("bge")

        def BV(f):
            @block.vector
            def _(e, f=f):
                f(EP(e, ev))

        def BS(f):
            @block.scalar
            def _(e, f=f):
                f(EP(e, sc))

        gq = sp("gq")

        def BG(f):
            @block.gpsimd
            def _(e, f=f):
                f(EP(e, gq))

        def R(i):
            return rows[0:1, i * T:(i + 1) * T]

        def wave(e, slot, pairs, waits=()):
            e.wait_ge(slot.h, slot.c)
            for s_, v_ in waits:
                e.wait_ge(s_.h, v_)
            for out, in_ in pairs:
                e.dma_start(out=out, in_=in_).then_inc(slot.h, 16)
                slot.c += 16

        def barrier():
            @block.sync
            def _(e):
                for s_ in SYNC_SLOTS:
                    e.wait_ge(s_.h, s_.c)
                e.sem_inc(bsy.h, 1)
            bsy.c += 1

            @block.gpsimd
            def _(e):
                for s_ in GQ_SLOTS:
                    e.wait_ge(s_.h, s_.c)
                e.sem_inc(bge.h, 1)
            bge.c += 1

            @block.tensor
            def _(e):
                e.sem_inc(bte.h, 1)
            bte.c += 1

            @block.vector
            def _(e):
                e.wait_ge(ev.h, ev.c)
                e.sem_inc(bve.h, 1)
            bve.c += 1

            @block.scalar
            def _(e):
                e.wait_ge(sc.h, sc.c)
                e.sem_inc(bse.h, 1)
            bse.c += 1

            for en, own in (("sync", bsy), ("gpsimd", bge), ("tensor", bte),
                            ("vector", bve), ("scalar", bse)):
                @getattr(block, en)
                def _(e, own=own):
                    for s_ in (bsy, bge, bte, bve, bse):
                        if s_ is not own:
                            e.wait_ge(s_.h, s_.c)

        @block.sync
        def _(e):
            wave(e, c0, [(identf[:], identf_in), (identb[:], identb_in),
                         (ones_row[:], ones_row_in)])

        @block.gpsimd
        def _(e):
            CH = 65536
            pairs = []
            for dst, src, n in ((WinT_bf, WinT_in, DM * 2 * DI),
                                (WxT_bf, WxT_in, DI * (DI + 2 * NS)),
                                (WoutT_bf, WoutT_in, DI * DM),
                                (Wq_bf, Wq_in, DM * DM),
                                (Wk_bf, Wk_in, DM * DM),
                                (Wv_bf, Wv_in, DM * DM),
                                (relembT_bf, relembT_in, DM * 512)):
                assert n % CH == 0, n
                pairs.append((bass.AP(dst.tensor, 0, [[CH, n // CH], [1, CH]]),
                              bass.AP(src.tensor, 0, [[CH, n // CH], [1, CH]])))
            wave(e, sw0, pairs)
        barrier()

        @BV
        def _(e):
            e.memset(zcol[:], 0.0)

        def tr_reduce(src_sb, op, pst):
            for k in range(16):
                @block.tensor
                def _(e, k=k):
                    e.wait_ge(ev.h, ev.c)
                    e.wait_ge(sc.h, sc.c)
                    e.wait_ge(c0.h, c0.c)
                    e.matmul(pst[:], src_sb[:, k * 128:(k + 1) * 128], identf[:],
                             start=True, stop=True,
                             is_transpose=True).then_inc(mm.h, 1)
                mm.c += 1

                @BV
                def _(e, k=k):
                    e.wait_ge(mm.h, mm.c)
                    e.tensor_reduce(out=red16[:, k:k + 1], in_=pst[:], axis=AX.X,
                                    op=op)

        def row_from_16(out_row, pst):
            @block.tensor
            def _(e):
                e.wait_ge(ev.h, ev.c)
                e.wait_ge(sc.h, sc.c)
                e.wait_ge(c0.h, c0.c)
                e.matmul(pst[0:16, 0:128], red16[:, 0:16], identf[:], start=True,
                         stop=True, is_transpose=True).then_inc(mm.h, 1)
            mm.c += 1

            @BS
            def _(e):
                e.wait_ge(mm.h, mm.c)
                e.wait_ge(rw.h, rw.c)
                e.activation(out=r16t[:], in_=pst[0:16, 0:128], func=AF.Copy)

            @block.sync
            def _(e):
                wave(e, rw, [(row_d[0:1, :], r16t[:])],
                     waits=[(sc, sc.c), (bcs, bcs.c)])
                wave(e, rw, [(out_row, row_d[0:1, :])])

        def bcast_row(dst_pairs, waits=()):
            @block.sync
            def _(e):
                for dst_sb, src_row, parts in dst_pairs:
                    w = src_row.shape[-1]
                    wave(e, rw, [(row_d[0:1, 0:w], src_row)],
                         waits=list(waits) + [(bcs, bcs.c)])
                    wave(e, bcs, [(dst_sb, row_d[0:1, 0:w].partition_broadcast(parts))],
                         waits=[(rw, rw.c)])

        def ln_stats(b, src, ph):
            for dt in range(NDT):
                buf = ph["xt2"][dt % 2]

                @block.sync
                def _(e, dt=dt, buf=buf):
                    wave(e, xsl[dt % 2], [(buf[:], src[b, dt * 128:(dt + 1) * 128, :])],
                         waits=[(ev, ev.c)])

                @BS
                def _(e, dt=dt, buf=buf):
                    e.wait_ge(xsl[dt % 2].h, xsl[dt % 2].c)
                    e.wait_ge(ev.h, ev.c)
                    e.activation(out=ph["tmp"][:], in_=buf[:], func=AF.Square)

                @BV
                def _(e, dt=dt, buf=buf):
                    e.wait_ge(xsl[dt % 2].h, xsl[dt % 2].c)
                    if dt == 0:
                        e.tensor_copy(out=ph["acc_s"][:], in_=buf[:])
                    else:
                        e.tensor_add(ph["acc_s"][:], ph["acc_s"][:], buf[:])
                    e.wait_ge(sc.h, sc.c)
                    if dt == 0:
                        e.tensor_copy(out=ph["acc_q"][:], in_=ph["tmp"][:])
                    else:
                        e.tensor_add(ph["acc_q"][:], ph["acc_q"][:], ph["tmp"][:])

            tr_reduce(ph["acc_s"], ALU.add, ph["pst"])
            row_from_16(R(R_MEAN), ph["pst"])
            tr_reduce(ph["acc_q"], ALU.add, ph["pst"])
            row_from_16(R(R_TMP), ph["pst"])

            @BV
            def _(e):
                e.wait_ge(rw.h, rw.c)
                e.tensor_scalar(out=R(R_MEAN), in0=R(R_MEAN), scalar1=1.0 / DM,
                                scalar2=None, op0=ALU.mult)
                e.tensor_mul(R(R_T2), R(R_MEAN), R(R_MEAN))
                e.tensor_scalar(out=R(R_TMP), in0=R(R_TMP), scalar1=1.0 / DM,
                                scalar2=None, op0=ALU.mult)
                e.tensor_sub(R(R_TMP), R(R_TMP), R(R_T2))
                e.tensor_scalar(out=R(R_TMP), in0=R(R_TMP), scalar1=EPS,
                                scalar2=None, op0=ALU.add)

            @BS
            def _(e):
                e.wait_ge(ev.h, ev.c)
                e.activation(out=R(R_TMP), in_=R(R_TMP), func=AF.Ln)
                e.activation(out=R(R_RSTD), in_=R(R_TMP), func=AF.Exp, scale=-0.5)

            pairs = [(ph["bc_mean"][:], R(R_MEAN), 128)]
            if ph.get("bc_rstd") is not None:
                pairs.append((ph["bc_rstd"][:], R(R_RSTD), 128))
            bcast_row(pairs, waits=[(sc, sc.c)])

        def ln_norm_bf16(b, src, out_sb, ph):
            for dt in range(NDT):
                buf = ph["xt2"][dt % 2]

                @block.sync
                def _(e, dt=dt, buf=buf):
                    wave(e, xsl[dt % 2], [(buf[:], src[b, dt * 128:(dt + 1) * 128, :])],
                         waits=[(ev, ev.c)])

                @BV
                def _(e, dt=dt, buf=buf):
                    e.wait_ge(xsl[dt % 2].h, xsl[dt % 2].c)
                    e.wait_ge(bcs.h, bcs.c)
                    e.tensor_sub(ph["tmp"][:], buf[:], ph["bc_mean"][:])
                    e.tensor_mul(out_sb[:, dt * T:(dt + 1) * T], ph["tmp"][:],
                                 ph["bc_rstd"][:])

        def ternarize(wT_ap, nt, cols, wq_ap, mw_slot, ph):
            for i in range(nt):
                buf = ph["xt2"][i % 2]

                @block.sync
                def _(e, i=i, buf=buf):
                    wave(e, xsl[i % 2], [(buf[:, 0:cols], wT_ap[i * 128:(i + 1) * 128, :])],
                         waits=[(ev, ev.c)])

                @BV
                def _(e, i=i, buf=buf):
                    e.wait_ge(xsl[i % 2].h, xsl[i % 2].c)
                    e.tensor_reduce(out=red16[:, 0:1], in_=buf[:, 0:cols], axis=AX.X,
                                    op=ALU.add, apply_absolute_value=True)
                    if i == 0:
                        e.tensor_copy(out=red16[:, 1:2], in_=red16[:, 0:1])
                    else:
                        e.tensor_add(red16[:, 1:2], red16[:, 1:2], red16[:, 0:1])

            @block.tensor
            def _(e):
                e.wait_ge(ev.h, ev.c)
                e.wait_ge(c0.h, c0.c)
                e.matmul(ph["pst"][0:1, 0:128], red16[:, 1:2], identf[:], start=True,
                         stop=True, is_transpose=True).then_inc(mm.h, 1)
            mm.c += 1

            @BV
            def _(e):
                e.wait_ge(mm.h, mm.c)
                e.tensor_reduce(out=rows2[0:1, mw_slot:mw_slot + 1],
                                in_=ph["pst"][0:1, 0:128], axis=AX.X, op=ALU.add)
                e.tensor_scalar(out=rows2[0:1, mw_slot:mw_slot + 1],
                                in0=rows2[0:1, mw_slot:mw_slot + 1],
                                scalar1=1.0 / (nt * 128 * cols), scalar2=1e-5,
                                op0=ALU.mult, op1=ALU.max)
                e.reciprocal(out=rows2[0:1, mw_slot + 8:mw_slot + 9],
                             in_=rows2[0:1, mw_slot:mw_slot + 1])

            bcast_row([(ph["swb"][:, 0:1], rows2[0:1, mw_slot + 8:mw_slot + 9], 128)],
                      waits=[(ev, ev.c)])

            for i in range(nt):
                buf = ph["xt2"][i % 2]

                @block.sync
                def _(e, i=i, buf=buf):
                    wave(e, xsl[i % 2], [(buf[:, 0:cols], wT_ap[i * 128:(i + 1) * 128, :])],
                         waits=[(ev, ev.c), (bcs, bcs.c)])

                @BV
                def _(e, i=i, buf=buf):
                    e.wait_ge(xsl[i % 2].h, xsl[i % 2].c)
                    e.wait_ge(bcs.h, bcs.c)
                    e.wait_ge(sw0.h, sw0.c)
                    e.tensor_scalar(out=ph["tmp"][:, 0:cols], in0=buf[:, 0:cols],
                                    scalar1=ph["swb"][:, 0:1], scalar2=MAGIC,
                                    op0=ALU.mult, op1=ALU.add)
                    e.tensor_scalar(out=ph["tmp"][:, 0:cols], in0=ph["tmp"][:, 0:cols],
                                    scalar1=MAGIC, scalar2=1.0,
                                    op0=ALU.subtract, op1=ALU.min)
                    e.tensor_scalar(out=ph["tmp"][:, 0:cols], in0=ph["tmp"][:, 0:cols],
                                    scalar1=-1.0, scalar2=None, op0=ALU.max)

                @block.gpsimd
                def _(e, i=i):
                    wave(e, sw0, [(wq_ap[i * 128:(i + 1) * 128, :], ph["tmp"][:, 0:cols])],
                         waits=[(ev, ev.c)])

        if 1 in SUBLAYERS or 3 in SUBLAYERS:
            with contextlib.ExitStack() as ps:
                ph = dict(
                    xt2=[ps.enter_context(nc.sbuf_tensor(U("tz_a"), [128, DH], F32)),
                         ps.enter_context(nc.sbuf_tensor(U("tz_b"), [128, DH], F32))],
                    tmp=ps.enter_context(nc.sbuf_tensor(U("tz_t"), [128, DH], F32)),
                    swb=ps.enter_context(nc.sbuf_tensor(U("tz_s"), [128, 1], F32)),
                    pst=ps.enter_context(nc.psum_tensor(U("tz_p"), [128, 128], F32)),
                )
                ternarize(w1T_in, NDT, DH, w1q_d, 0, ph)
                ternarize(w2T_in, NCT, DM, w2q_d, 1, ph)
                barrier()

        def emit_ffn(b, src, dst):
            with contextlib.ExitStack() as ps:
                xt2 = [ps.enter_context(nc.sbuf_tensor(U("f_xa"), [128, T], F32)),
                       ps.enter_context(nc.sbuf_tensor(U("f_xb"), [128, T], F32))]
                tmp = ps.enter_context(nc.sbuf_tensor(U("f_tmp"), [128, T], F32))
                acc_s = ps.enter_context(nc.sbuf_tensor(U("f_as"), [128, T], F32))
                acc_q = ps.enter_context(nc.sbuf_tensor(U("f_aq"), [128, T], F32))
                bc_mean = ps.enter_context(nc.sbuf_tensor(U("f_bm"), [128, T], F32))
                bc_b = ps.enter_context(nc.sbuf_tensor(U("f_bb"), [128, T], F32))
                bc_c = acc_s
                xq = ps.enter_context(nc.sbuf_tensor(U("f_xq"), [128, NDT * T], BF16))
                fwt = ps.enter_context(nc.sbuf_tensor(U("f_fwt"), [128, 96], F32))
                fbt = ps.enter_context(nc.sbuf_tensor(U("f_fbt"), [128, 32], F32))
                ph = dict(xt2=xt2, tmp=tmp, acc_s=acc_s, acc_q=acc_q,
                          bc_mean=bc_mean, bc_rstd=None)

                @block.sync
                def _(e):
                    wave(e, c0, [(fwt[:], fconvw_in), (fbt[:], fconvb_in)])

                with nc.psum_tensor(U("f_pst"), [128, 128], F32) as pst:
                    ph["pst"] = pst
                    ln_stats(b, src, ph)

                    for dt in range(NDT):
                        buf = xt2[dt % 2]

                        @block.sync
                        def _(e, dt=dt, buf=buf):
                            wave(e, xsl[dt % 2],
                                 [(buf[:], src[b, dt * 128:(dt + 1) * 128, :])],
                                 waits=[(ev, ev.c)])

                        @BV
                        def _(e, dt=dt, buf=buf):
                            e.wait_ge(xsl[dt % 2].h, xsl[dt % 2].c)
                            e.wait_ge(bcs.h, bcs.c)
                            e.tensor_sub(tmp[:], buf[:], bc_mean[:])
                            if dt == 0:
                                e.tensor_copy(out=acc_q[:], in_=tmp[:])
                                e.tensor_copy(out=bc_b[:], in_=tmp[:])
                            else:
                                e.tensor_tensor(out=acc_q[:], in0=acc_q[:], in1=tmp[:],
                                                op=ALU.max)
                                e.tensor_tensor(out=bc_b[:], in0=bc_b[:], in1=tmp[:],
                                                op=ALU.min)

                    tr_reduce(acc_q, ALU.max, pst)
                    row_from_16(R(R_MAX), pst)
                    tr_reduce(bc_b, ALU.min, pst)
                    row_from_16(R(R_TMP), pst)

                @BV
                def _(e):
                    e.wait_ge(rw.h, rw.c)
                    e.tensor_scalar(out=R(R_TMP), in0=R(R_TMP), scalar1=-1.0,
                                    scalar2=None, op0=ALU.mult)
                    e.tensor_tensor(out=R(R_MAX), in0=R(R_MAX), in1=R(R_TMP),
                                    op=ALU.max)
                    e.tensor_mul(R(R_TMP), R(R_RSTD), R(R_MAX))
                    e.tensor_scalar(out=R(R_TMP), in0=R(R_TMP), scalar1=1e-5,
                                    scalar2=None, op0=ALU.max)
                    e.tensor_scalar(out=R(R_DQ), in0=R(R_TMP),
                                    scalar1=rows2[0:1, 0:1], scalar2=1.0 / 127.0,
                                    op0=ALU.mult, op1=ALU.mult)
                    e.reciprocal(out=R(R_T2), in_=R(R_TMP))
                    e.tensor_scalar(out=R(R_T2), in0=R(R_T2), scalar1=127.0,
                                    scalar2=None, op0=ALU.mult)
                    e.tensor_mul(R(R_QS), R(R_RSTD), R(R_T2))

                bcast_row([(bc_b[:], R(R_QS), 128), (bc_c[:], R(R_DQ), 128)],
                          waits=[(ev, ev.c)])

                for dt in range(NDT):
                    buf = xt2[dt % 2]

                    @block.sync
                    def _(e, dt=dt, buf=buf):
                        wave(e, xsl[dt % 2], [(buf[:], src[b, dt * 128:(dt + 1) * 128, :])],
                             waits=[(ev, ev.c)])

                    @BV
                    def _(e, dt=dt, buf=buf):
                        e.wait_ge(xsl[dt % 2].h, xsl[dt % 2].c)
                        e.wait_ge(bcs.h, bcs.c)
                        e.tensor_sub(tmp[:], buf[:], bc_mean[:])
                        e.tensor_mul(tmp[:], tmp[:], bc_b[:])
                        e.tensor_scalar(out=xq[:, dt * T:(dt + 1) * T], in0=tmp[:],
                                        scalar1=MAGIC, scalar2=MAGIC,
                                        op0=ALU.add, op1=ALU.subtract)
                barrier()

                with contextlib.ExitStack() as p1:
                    mnb = p1.enter_context(nc.sbuf_tensor(U("f_mn"), [128, T], F32))
                    hrow = bc_mean
                    cacc = tmp
                    w1t = [p1.enter_context(nc.sbuf_tensor(U("f_w1t"), [128, 8 * 128], BF16))
                           for _ in range(2)]
                    psq = [p1.enter_context(nc.psum_tensor(U("f_p%d" % i), [128, 512], F32))
                           for i in range(2)]

                    evmark = [0, 0]
                    wmark = [0, 0]
                    for ct in range(NCT):
                        wb = w1t[ct % 2]

                        @block.sync
                        def _(e, ct=ct, wb=wb, w=wmark[ct % 2]):
                            wave(e, xsl[ct % 2],
                                 [(wb[:].rearrange("p (c u) -> p c u", c=8),
                                   w1q_d[:, ct * 128:(ct + 1) * 128].rearrange(
                                       "(c p) u -> p c u", p=128))],
                                 waits=[(mm, w)])
                        for tc in range(4):
                            pb = psq[tc % 2]

                            @block.tensor
                            def _(e, ct=ct, tc=tc, pb=pb, wb=wb, w=evmark[tc % 2]):
                                e.wait_ge(xsl[ct % 2].h, xsl[ct % 2].c)
                                e.wait_ge(ev.h, w)
                                for mc in range(8):
                                    ins = e.matmul(
                                        pb[:], wb[:, mc * 128:mc * 128 + 128],
                                        xq[:, mc * T + tc * 512:mc * T + tc * 512 + 512],
                                        start=(mc == 0), stop=(mc == 7))
                                ins.then_inc(mm.h, 1)
                            mm.c += 1
                            if tc == 3:
                                wmark[ct % 2] = mm.c

                            @BV
                            def _(e, ct=ct, tc=tc, pb=pb):
                                e.wait_ge(mm.h, mm.c)
                                e.wait_ge(bcs.h, bcs.c)
                                e.scalar_tensor_tensor(
                                    out=hrow[:, tc * 512:tc * 512 + 512],
                                    in0=pb[:], scalar=1.0,
                                    in1=bc_c[:, tc * 512:tc * 512 + 512],
                                    op0=ALU.mult, op1=ALU.mult)
                            evmark[tc % 2] = ev.c

                        @BV
                        def _(e, ct=ct):
                            e.wait_ge(c0.h, c0.c)
                            e.wait_ge(s0.h, s0.c)
                            e.tensor_scalar(out=cacc[:], in0=hrow[:, 0:T],
                                            scalar1=fwt[:, ct * 3 + 1:ct * 3 + 2],
                                            scalar2=None, op0=ALU.mult)
                            e.scalar_tensor_tensor(out=cacc[:, 1:T], in0=hrow[:, 0:T - 1],
                                                   scalar=fwt[:, ct * 3:ct * 3 + 1],
                                                   in1=cacc[:, 1:T], op0=ALU.mult,
                                                   op1=ALU.add)
                            e.scalar_tensor_tensor(out=cacc[:, 0:T - 1], in0=hrow[:, 1:T],
                                                   scalar=fwt[:, ct * 3 + 2:ct * 3 + 3],
                                                   in1=cacc[:, 0:T - 1], op0=ALU.mult,
                                                   op1=ALU.add)

                        @BS
                        def _(e, ct=ct):
                            e.wait_ge(ev.h, ev.c)
                            e.wait_ge(s0.h, s0.c)
                            e.wait_ge(c0.h, c0.c)
                            e.activation(out=cacc[:], in_=cacc[:], func=AF.Silu,
                                         bias=fbt[:, ct:ct + 1])

                        @BV
                        def _(e, ct=ct):
                            e.wait_ge(sc.h, sc.c)
                            if ct == 0:
                                e.tensor_copy(out=acc_q[:], in_=cacc[:])
                                e.tensor_copy(out=mnb[:], in_=cacc[:])
                            else:
                                e.tensor_tensor(out=acc_q[:], in0=acc_q[:], in1=cacc[:],
                                                op=ALU.max)
                                e.tensor_tensor(out=mnb[:], in0=mnb[:], in1=cacc[:],
                                                op=ALU.min)

                        @block.sync
                        def _(e, ct=ct):
                            wave(e, s0, [(hsil_d[ct * 128:(ct + 1) * 128, :], cacc[:])],
                                 waits=[(ev, ev.c)])

                    with nc.psum_tensor(U("fh_pst"), [128, 128], F32) as pst2:
                        tr_reduce(acc_q, ALU.max, pst2)
                        row_from_16(R(R_MAX), pst2)
                        tr_reduce(mnb, ALU.min, pst2)
                        row_from_16(R(R_TMP), pst2)

                    @BV
                    def _(e):
                        e.wait_ge(rw.h, rw.c)
                        e.tensor_scalar(out=R(R_TMP), in0=R(R_TMP), scalar1=-1.0,
                                        scalar2=None, op0=ALU.mult)
                        e.tensor_tensor(out=R(R_MAX), in0=R(R_MAX), in1=R(R_TMP),
                                        op=ALU.max)
                        e.tensor_scalar(out=R(R_TMP), in0=R(R_MAX), scalar1=1e-5,
                                        scalar2=None, op0=ALU.max)
                        e.tensor_scalar(out=R(R_DQ), in0=R(R_TMP),
                                        scalar1=rows2[0:1, 1:2], scalar2=1.0 / 127.0,
                                        op0=ALU.mult, op1=ALU.mult)
                        e.reciprocal(out=R(R_QS), in_=R(R_TMP))
                        e.tensor_scalar(out=R(R_QS), in0=R(R_QS), scalar1=127.0,
                                        scalar2=None, op0=ALU.mult)

                    bcast_row([(bc_b[:], R(R_QS), 128), (bc_c[:], R(R_DQ), 128)],
                              waits=[(ev, ev.c)])

                    for ct in range(NCT):
                        buf = xt2[ct % 2]

                        @block.sync
                        def _(e, ct=ct, buf=buf):
                            wave(e, xsl[ct % 2],
                                 [(buf[:], hsil_d[ct * 128:(ct + 1) * 128, :])],
                                 waits=[(ev, ev.c), (s0, s0.c)])

                        @BV
                        def _(e, ct=ct, buf=buf):
                            e.wait_ge(xsl[ct % 2].h, xsl[ct % 2].c)
                            e.wait_ge(bcs.h, bcs.c)
                            e.wait_ge(s1.h, s1.c)
                            e.tensor_mul(cacc[:], buf[:], bc_b[:])
                            e.tensor_scalar(out=hrow[:, 0:T].bitcast(BF16)[:, 0:T],
                                            in0=cacc[:], scalar1=MAGIC, scalar2=MAGIC,
                                            op0=ALU.add, op1=ALU.subtract)

                        @block.sync
                        def _(e, ct=ct):
                            wave(e, s1, [(xq2_d[ct * 128:(ct + 1) * 128, :],
                                          hrow[:, 0:T].bitcast(BF16)[:, 0:T])],
                                 waits=[(ev, ev.c)])
                    barrier()

                with contextlib.ExitStack() as p2:
                    w2 = p2.enter_context(nc.sbuf_tensor(U("f_w2"), [128, NCT * DM], BF16))
                    r2 = [p2.enter_context(nc.sbuf_tensor(U("f_r2"), [128, 512], BF16))
                          for _ in range(2)]
                    psd = [p2.enter_context(nc.psum_tensor(U("f_pd%d" % i), [128, 512], F32))
                           for i in range(8)]

                    @block.sync
                    def _(e):
                        wave(e, c0, [(w2[:, cc * DM:(cc + 1) * DM],
                                      w2q_d[cc * 128:(cc + 1) * 128, :])
                                     for cc in range(NCT)])

                    for tc in range(4):
                        for cc in range(NCT):
                            buf = r2[cc % 2]

                            @block.sync
                            def _(e, tc=tc, cc=cc, buf=buf, w=mm.c):
                                wave(e, rsl[cc % 2],
                                     [(buf[:], xq2_d[cc * 128:(cc + 1) * 128,
                                                     tc * 512:tc * 512 + 512])],
                                     waits=[(mm, w)])

                            @block.tensor
                            def _(e, tc=tc, cc=cc, buf=buf):
                                e.wait_ge(rsl[cc % 2].h, rsl[cc % 2].c)
                                e.wait_ge(c0.h, c0.c)
                                if cc == 0:
                                    e.wait_ge(ev.h, ev.c)
                                for dm_ in range(8):
                                    ins = e.matmul(
                                        psd[dm_][:],
                                        w2[:, cc * DM + dm_ * 128:cc * DM + dm_ * 128 + 128],
                                        buf[:], start=(cc == 0), stop=(cc == NCT - 1))
                                ins.then_inc(mm.h, 1)
                            mm.c += 1

                        for dm_ in range(8):
                            @block.sync
                            def _(e, tc=tc, dm_=dm_):
                                wave(e, m0, [(xt2[0][:, 0:512],
                                              src[b, dm_ * 128:(dm_ + 1) * 128,
                                                  tc * 512:tc * 512 + 512])],
                                     waits=[(ev, ev.c)])

                            @BV
                            def _(e, tc=tc, dm_=dm_):
                                e.wait_ge(mm.h, mm.c)
                                e.wait_ge(m0.h, m0.c)
                                e.wait_ge(s1.h, s1.c)
                                e.tensor_mul(tmp[:, 0:512], psd[dm_][:],
                                             bc_c[:, tc * 512:tc * 512 + 512])
                                e.tensor_add(tmp[:, 0:512], tmp[:, 0:512],
                                             xt2[0][:, 0:512])

                            @block.sync
                            def _(e, tc=tc, dm_=dm_):
                                wave(e, s1, [(dst[b, dm_ * 128:(dm_ + 1) * 128,
                                                  tc * 512:tc * 512 + 512],
                                              tmp[:, 0:512])],
                                     waits=[(ev, ev.c)])
                    barrier()

        def emit_ssm(b, src, dst):
            with contextlib.ExitStack() as ps:
                xibf = ps.enter_context(nc.sbuf_tensor(U("s_xibf"), [128, NIT * T], BF16))
                with contextlib.ExitStack() as p1:
                    lnx = p1.enter_context(nc.sbuf_tensor(U("s_lnx"), [128, NDT * T], BF16))
                    with contextlib.ExitStack() as pln:
                        xt2 = [pln.enter_context(nc.sbuf_tensor(U("s_xa"), [128, T], F32)),
                               pln.enter_context(nc.sbuf_tensor(U("s_xb"), [128, T], F32))]
                        tmp = pln.enter_context(nc.sbuf_tensor(U("s_tmp"), [128, T], F32))
                        acc_s = pln.enter_context(nc.sbuf_tensor(U("s_as"), [128, T], F32))
                        acc_q = pln.enter_context(nc.sbuf_tensor(U("s_aq"), [128, T], F32))
                        bc_mean = pln.enter_context(nc.sbuf_tensor(U("s_bm"), [128, T], F32))
                        bc_rstd = pln.enter_context(nc.sbuf_tensor(U("s_br"), [128, T], F32))
                        pst = pln.enter_context(nc.psum_tensor(U("s_pst"), [128, 128], F32))
                        ph = dict(xt2=xt2, tmp=tmp, acc_s=acc_s, acc_q=acc_q,
                                  bc_mean=bc_mean, bc_rstd=bc_rstd, pst=pst)
                        ln_stats(b, src, ph)
                        ln_norm_bf16(b, src, lnx, ph)
                        barrier()

                    wint = [p1.enter_context(nc.sbuf_tensor(U("s_wint"), [128, 8 * 128], BF16))
                            for _ in range(2)]
                    awt = p1.enter_context(nc.sbuf_tensor(U("s_awt"), [128, 64], F32))
                    abt = p1.enter_context(nc.sbuf_tensor(U("s_abt"), [128, 16], F32))
                    bint = p1.enter_context(nc.sbuf_tensor(U("s_bint"), [128, 32], F32))
                    hrow = p1.enter_context(nc.sbuf_tensor(U("s_hr"), [128, T], F32))
                    cacc = p1.enter_context(nc.sbuf_tensor(U("s_ca"), [128, T], F32))
                    psq = [p1.enter_context(nc.psum_tensor(U("s_p%d" % i), [128, 512], F32))
                           for i in range(2)]

                    @block.sync
                    def _(e):
                        wave(e, c0, [(awt[:], aconvw_in), (abt[:], bconv_il_in),
                                     (bint[:], bin_il_in)])

                    evmark = [0, 0]
                    wmark = [0, 0]
                    gsl = [g1, g2]
                    for ct in range(2 * NIT):
                        is_xi = ct < NIT
                        wb = wint[ct % 2]

                        @block.sync
                        def _(e, ct=ct, wb=wb, w=wmark[ct % 2]):
                            wave(e, gsl[ct % 2],
                                 [(wb[:].rearrange("p (c u) -> p c u", c=8),
                                   WinT_bf[:, ct * 128:(ct + 1) * 128].rearrange(
                                       "(c p) u -> p c u", p=128))],
                                 waits=[(mm, w)])
                        for tc in range(4):
                            pb = psq[tc % 2]

                            @block.tensor
                            def _(e, ct=ct, tc=tc, pb=pb, wb=wb, w=evmark[tc % 2]):
                                e.wait_ge(gsl[ct % 2].h, gsl[ct % 2].c)
                                e.wait_ge(c0.h, c0.c)
                                e.wait_ge(ev.h, w)
                                for mc in range(8):
                                    ins = e.matmul(
                                        pb[:], wb[:, mc * 128:mc * 128 + 128],
                                        lnx[:, mc * T + tc * 512:mc * T + tc * 512 + 512],
                                        start=(mc == 0), stop=(mc == 7))
                                ins.then_inc(mm.h, 1)
                            mm.c += 1
                            if tc == 3:
                                wmark[ct % 2] = mm.c

                            @BV
                            def _(e, ct=ct, tc=tc, pb=pb):
                                e.wait_ge(mm.h, mm.c)
                                e.wait_ge(s6.h, s6.c)
                                e.wait_ge(c0.h, c0.c)
                                e.tensor_scalar(
                                    out=hrow[:, tc * 512:tc * 512 + 512], in0=pb[:],
                                    scalar1=bint[:, ct:ct + 1], scalar2=None,
                                    op0=ALU.add)
                            evmark[tc % 2] = ev.c

                        if is_xi:
                            @BV
                            def _(e, ct=ct):
                                e.wait_ge(c0.h, c0.c)
                                e.tensor_scalar(out=cacc[:], in0=hrow[:, 0:T],
                                                scalar1=awt[:, ct * 4 + 3:ct * 4 + 4],
                                                scalar2=None, op0=ALU.mult)
                                for kk in range(3):
                                    sh = 3 - kk
                                    e.scalar_tensor_tensor(
                                        out=cacc[:, sh:T], in0=hrow[:, 0:T - sh],
                                        scalar=awt[:, ct * 4 + kk:ct * 4 + kk + 1],
                                        in1=cacc[:, sh:T], op0=ALU.mult, op1=ALU.add)

                            @BS
                            def _(e, ct=ct):
                                e.wait_ge(ev.h, ev.c)
                                e.wait_ge(s6.h, s6.c)
                                e.wait_ge(c0.h, c0.c)
                                e.activation(out=cacc[:], in_=cacc[:], func=AF.Silu,
                                             bias=abt[:, ct:ct + 1])

                            @BV
                            def _(e, ct=ct):
                                e.wait_ge(sc.h, sc.c)
                                e.tensor_copy(out=xibf[:, ct * T:(ct + 1) * T],
                                              in_=cacc[:])

                            @block.sync
                            def _(e, ct=ct):
                                wave(e, s6, [(xi_d[ct * 128:(ct + 1) * 128, :], cacc[:])],
                                     waits=[(sc, sc.c)])
                        else:
                            @BS
                            def _(e, ct=ct):
                                e.wait_ge(ev.h, ev.c)
                                e.wait_ge(s6.h, s6.c)
                                e.activation(out=cacc[:], in_=hrow[:], func=AF.Silu)

                            @block.sync
                            def _(e, ct=ct):
                                wave(e, s6,
                                     [(res_d[(ct - NIT) * 128:(ct - NIT + 1) * 128, :],
                                       cacc[:])],
                                     waits=[(sc, sc.c)])
                    barrier()

                with contextlib.ExitStack() as p2:
                    wxt = [p2.enter_context(nc.sbuf_tensor(U("s_wxt"), [128, NIT * 128], BF16))
                           for _ in range(2)]
                    wxbc = p2.enter_context(nc.sbuf_tensor(U("s_wxbc"), [128, NIT * 32], BF16))
                    bxd = p2.enter_context(nc.sbuf_tensor(U("s_bxd"), [128, 16], F32))
                    bxbc = p2.enter_context(nc.sbuf_tensor(U("s_bxbc"), [32, 1], F32))
                    drow = p2.enter_context(nc.sbuf_tensor(U("s_dr"), [128, T], F32))
                    dxrow = p2.enter_context(nc.sbuf_tensor(U("s_dxr"), [128, T], F32))
                    xif = p2.enter_context(nc.sbuf_tensor(U("s_xif"), [128, T], F32))
                    bcsb = p2.enter_context(nc.sbuf_tensor(U("s_bc"), [32, T], BF16))
                    psq = [p2.enter_context(nc.psum_tensor(U("s_q%d" % i), [128, 512], F32))
                           for i in range(2)]

                    @block.sync
                    def _(e):
                        wave(e, c0, [(bxd[:], bxd_il_in), (bxbc[:], bx_bc_in)])

                    @block.sync
                    def _(e):
                        wave(e, g3, [(wxbc[:].rearrange("p (c u) -> p c u", c=NIT),
                                      WxT_bf[:, 2048:2080].rearrange(
                                          "(c p) u -> p c u", p=128))])

                    wmark = [0, 0]
                    gsl = [g1, g2]
                    for pc in range(NIT):
                        wb = wxt[pc % 2]

                        @block.sync
                        def _(e, pc=pc, wb=wb, w=wmark[pc % 2]):
                            wave(e, gsl[pc % 2],
                                 [(wb[:].rearrange("p (c u) -> p c u", c=NIT),
                                   WxT_bf[:, pc * 128:(pc + 1) * 128].rearrange(
                                       "(c p) u -> p c u", p=128))],
                                 waits=[(mm, w)])
                        @block.sync
                        def _(e, pc=pc):
                            wave(e, m0, [(xif[:], xi_d[pc * 128:(pc + 1) * 128, :])],
                                 waits=[(ev, ev.c), (s6, s6.c)])
                        for tc in range(4):
                            pb = psq[tc % 2]

                            @block.tensor
                            def _(e, pc=pc, tc=tc, pb=pb, wb=wb, w=sc.c):
                                e.wait_ge(gsl[pc % 2].h, gsl[pc % 2].c)
                                e.wait_ge(sc.h, w)
                                for cc in range(NIT):
                                    ins = e.matmul(
                                        pb[:], wb[:, cc * 128:cc * 128 + 128],
                                        xibf[:, cc * T + tc * 512:cc * T + tc * 512 + 512],
                                        start=(cc == 0), stop=(cc == NIT - 1))
                                ins.then_inc(mm.h, 1)
                            mm.c += 1
                            if tc == 3:
                                wmark[pc % 2] = mm.c

                            @BS
                            def _(e, pc=pc, tc=tc, pb=pb):
                                e.wait_ge(mm.h, mm.c)
                                e.wait_ge(s2.h, s2.c)
                                e.wait_ge(c0.h, c0.c)
                                e.activation(out=drow[:, tc * 512:tc * 512 + 512],
                                             in_=pb[:], func=AF.Exp,
                                             bias=bxd[:, pc:pc + 1])
                                e.activation(out=drow[:, tc * 512:tc * 512 + 512],
                                             in_=drow[:, tc * 512:tc * 512 + 512],
                                             func=AF.Ln, bias=1.0)

                        @BV
                        def _(e, pc=pc):
                            e.wait_ge(sc.h, sc.c)
                            e.wait_ge(m0.h, m0.c)
                            e.wait_ge(s2.h, s2.c)
                            e.tensor_mul(dxrow[:], drow[:], xif[:])

                        @block.sync
                        def _(e, pc=pc):
                            wave(e, s2, [(dlt_d[pc * 128:(pc + 1) * 128, :], drow[:]),
                                         (dltx_d[pc * 128:(pc + 1) * 128, :], dxrow[:])],
                                 waits=[(ev, ev.c)])

                    for tc in range(4):
                        pb = psq[tc % 2]

                        @block.tensor
                        def _(e, tc=tc, pb=pb, w=ev.c):
                            e.wait_ge(ev.h, w)
                            e.wait_ge(g3.h, g3.c)
                            for cc in range(NIT):
                                ins = e.matmul(
                                    pb[0:32, :],
                                    wxbc[:, cc * 32:cc * 32 + 32],
                                    xibf[:, cc * T + tc * 512:cc * T + tc * 512 + 512],
                                    start=(cc == 0), stop=(cc == NIT - 1))
                            ins.then_inc(mm.h, 1)
                        mm.c += 1

                        @BV
                        def _(e, tc=tc, pb=pb):
                            e.wait_ge(mm.h, mm.c)
                            e.wait_ge(c0.h, c0.c)
                            e.tensor_scalar(out=bcsb[:, tc * 512:tc * 512 + 512],
                                            in0=pb[0:32, :], scalar1=bxbc[:, 0:1],
                                            scalar2=None, op0=ALU.add)

                    @block.sync
                    def _(e):
                        wave(e, s7, [(bc_d, bcsb[:])], waits=[(ev, ev.c)])
                    barrier()

            with contextlib.ExitStack() as p3:
                Bb2 = [p3.enter_context(nc.sbuf_tensor(U("s_Bb"), [128, NS * TCH], BF16))
                       for _ in range(2)]
                Cb2 = [p3.enter_context(nc.sbuf_tensor(U("s_Cb"), [128, NS * TCH], BF16))
                       for _ in range(2)]
                dch2 = [p3.enter_context(nc.sbuf_tensor(U("s_dch"), [128, TCH], F32))
                        for _ in range(2)]
                dxch2 = [p3.enter_context(nc.sbuf_tensor(U("s_dxch"), [128, TCH], F32))
                         for _ in range(2)]
                dA2 = [p3.enter_context(nc.sbuf_tensor(U("s_dA"), [128, TCH], F32))
                       for _ in range(2)]
                dBx2 = [p3.enter_context(nc.sbuf_tensor(U("s_dBx"), [128, TCH], F32))
                        for _ in range(2)]
                hbuf = p3.enter_context(nc.sbuf_tensor(U("s_h"), [128, TCH], F32))
                ytmp = p3.enter_context(nc.sbuf_tensor(U("s_yt"), [128, TCH], F32))
                yacc = p3.enter_context(nc.sbuf_tensor(U("s_ya"), [128, TCH], F32))
                ybf = p3.enter_context(nc.sbuf_tensor(U("s_ybf"), [128, TCH], BF16))
                stt = p3.enter_context(nc.sbuf_tensor(U("s_st"), [128, NIT * NS], F32))
                Aneg = p3.enter_context(nc.sbuf_tensor(U("s_An"), [128, 256], F32))
                Dil = p3.enter_context(nc.sbuf_tensor(U("s_Dil"), [128, 16], F32))
                xich2 = [p3.enter_context(nc.sbuf_tensor(U("s_xich"), [128, TCH], F32))
                         for _ in range(2)]
                rsch2 = [p3.enter_context(nc.sbuf_tensor(U("s_rsch"), [128, TCH], F32))
                         for _ in range(2)]

                @block.sync
                def _(e):
                    wave(e, c0, [(Aneg[:], A_il_in), (Dil[:], D_il_in)])

                @BS
                def _(e):
                    e.wait_ge(c0.h, c0.c)
                    e.activation(out=Aneg[:], in_=Aneg[:], func=AF.Exp)

                @BV
                def _(e):
                    e.wait_ge(sc.h, sc.c)
                    e.tensor_scalar(out=Aneg[:], in0=Aneg[:], scalar1=-1.0,
                                    scalar2=None, op0=ALU.mult)

                def issue_bc(ch, wev, wgq):
                    @block.sync
                    def _(e, ch=ch, wev=wev, wgq=wgq):
                        wave(e, bcs,
                             [(Bb2[ch % 2][:, n * TCH:(n + 1) * TCH],
                               bc_d[n:n + 1, ch * TCH:(ch + 1) * TCH].partition_broadcast(128))
                              for n in range(NS)] +
                             [(Cb2[ch % 2][:, n * TCH:(n + 1) * TCH],
                               bc_d[NS + n:NS + n + 1, ch * TCH:(ch + 1) * TCH].partition_broadcast(128))
                              for n in range(NS)],
                             waits=[(ev, wev), (s7, s7.c), (gq, wgq)])
                    return bcs.c

                save_ev = [ev.c, ev.c]
                save_gq = [gq.c, gq.c]
                bwa = [0] * NCH
                bwa[0] = issue_bc(0, ev.c, gq.c)
                for ch in range(NCH):
                    Bb = Bb2[ch % 2]
                    Cb = Cb2[ch % 2]
                    bw = bwa[ch]

                    chm_ev = [0, 0]
                    chm_sc = [0, 0]
                    chm_gq = [0, 0]
                    csl = [m0, s0]
                    for g in range(NIT):
                        gp = g % 2
                        dch, dxch = dch2[gp], dxch2[gp]
                        xich, rsch = xich2[gp], rsch2[gp]

                        @block.sync
                        def _(e, ch=ch, g=g, gp=gp, dch=dch, dxch=dxch,
                              xich=xich, rsch=rsch,
                              wv=chm_ev[gp], ws=chm_sc[gp], wg=chm_gq[gp]):
                            wave(e, csl[gp],
                                 [(dch[:], dlt_d[g * 128:(g + 1) * 128,
                                                 ch * TCH:(ch + 1) * TCH]),
                                  (dxch[:], dltx_d[g * 128:(g + 1) * 128,
                                                   ch * TCH:(ch + 1) * TCH]),
                                  (xich[:], xi_d[g * 128:(g + 1) * 128,
                                                 ch * TCH:(ch + 1) * TCH]),
                                  (rsch[:], res_d[g * 128:(g + 1) * 128,
                                                  ch * TCH:(ch + 1) * TCH])],
                                 waits=[(ev, wv), (sc, ws), (gq, wg)])

                        S0 = sc.c
                        E0 = ev.c
                        G0 = gq.c
                        opsn = 4 if ch < NCH - 1 else 3

                        @BS
                        def _(e, ch=ch, g=g, gp=gp, dch=dch, E0=E0, opsn=opsn):
                            e.wait_ge(csl[gp].h, csl[gp].c)
                            e.wait_ge(ev.h, E0)
                            for n in range(NS):
                                if n >= 2:
                                    e.wait_ge(ev.h, E0 + (n - 1) * opsn)
                                e.activation(out=dA2[n % 2][:], in_=dch[:],
                                             func=AF.Exp,
                                             scale=Aneg[:, g * NS + n:g * NS + n + 1])

                        @BG
                        def _(e, ch=ch, g=g, gp=gp, dxch=dxch, E0=E0, opsn=opsn,
                              bw=bw):
                            e.wait_ge(csl[gp].h, csl[gp].c)
                            e.wait_ge(bcs.h, bw)
                            e.wait_ge(ev.h, E0)
                            for n in range(NS):
                                if n >= 2:
                                    e.wait_ge(ev.h, E0 + (n - 1) * opsn)
                                e.tensor_mul(dBx2[n % 2][:], dxch[:],
                                             Bb[:, n * TCH:(n + 1) * TCH])

                        @BV
                        def _(e, ch=ch, g=g, gp=gp, w=s3.c, S0=S0, G0=G0, bw=bw):
                            e.wait_ge(bcs.h, bw)
                            e.wait_ge(s3.h, w)
                            for n in range(NS):
                                dA = dA2[n % 2]
                                e.wait_ge(sc.h, S0 + n + 1)
                                e.wait_ge(gq.h, G0 + n + 1)
                                e.tensor_tensor_scan(
                                    out=hbuf[:], data0=dA[:], data1=dBx2[n % 2][:],
                                    initial=(0.0 if ch == 0 else
                                             stt[:, g * NS + n:g * NS + n + 1]),
                                    op0=ALU.mult, op1=ALU.add)
                                if ch < NCH - 1:
                                    e.tensor_copy(out=stt[:, g * NS + n:g * NS + n + 1],
                                                  in_=hbuf[:, TCH - 1:TCH])
                                e.tensor_mul(ytmp[:], hbuf[:],
                                             Cb[:, n * TCH:(n + 1) * TCH])
                                if n == 0:
                                    e.tensor_copy(out=yacc[:], in_=ytmp[:])
                                else:
                                    e.tensor_add(yacc[:], yacc[:], ytmp[:])

                        @BV
                        def _(e, ch=ch, g=g, gp=gp, xich=xich, rsch=rsch, w=s3.c):
                            e.wait_ge(csl[gp].h, csl[gp].c)
                            e.wait_ge(s3.h, w)
                            e.wait_ge(c0.h, c0.c)
                            e.scalar_tensor_tensor(out=yacc[:], in0=xich[:],
                                                   scalar=Dil[:, g:g + 1],
                                                   in1=yacc[:], op0=ALU.mult,
                                                   op1=ALU.add)
                            e.tensor_mul(ybf[:], yacc[:], rsch[:])
                        chm_ev[gp] = ev.c
                        chm_sc[gp] = sc.c
                        chm_gq[gp] = gq.c

                        @block.sync
                        def _(e, ch=ch, g=g):
                            wave(e, s3, [(yt_d[g * 128:(g + 1) * 128,
                                               ch * TCH:(ch + 1) * TCH], ybf[:])],
                                 waits=[(ev, ev.c)])
                    save_ev[ch % 2] = ev.c
                    save_gq[ch % 2] = gq.c
                    if ch + 1 < NCH:
                        bwa[ch + 1] = issue_bc(ch + 1, save_ev[(ch + 1) % 2],
                                               save_gq[(ch + 1) % 2])
                barrier()

            with contextlib.ExitStack() as p4:
                wo = p4.enter_context(nc.sbuf_tensor(U("s_wo"), [128, NIT * DM], BF16))
                yt2 = [p4.enter_context(nc.sbuf_tensor(U("s_yt2"), [128, 512], BF16))
                       for _ in range(2)]
                bout = p4.enter_context(nc.sbuf_tensor(U("s_bo"), [128, 8], F32))
                xres = p4.enter_context(nc.sbuf_tensor(U("s_xr2"), [128, 512], F32))
                otmp = p4.enter_context(nc.sbuf_tensor(U("s_ot"), [128, 512], F32))
                psd = [p4.enter_context(nc.psum_tensor(U("s_pd%d" % i), [128, 512], F32))
                       for i in range(8)]

                @block.sync
                def _(e):
                    wave(e, c0, [(bout[:], bout_il_in)])

                @block.sync
                def _(e):
                    wave(e, g1, [(wo[:, cc * DM:(cc + 1) * DM],
                                  WoutT_bf[cc * 128:(cc + 1) * 128, :])
                                 for cc in range(NIT)])

                for tc in range(4):
                    for cc in range(NIT):
                        buf = yt2[cc % 2]

                        @block.sync
                        def _(e, tc=tc, cc=cc, buf=buf, w=mm.c):
                            wave(e, rsl[cc % 2],
                                 [(buf[:], yt_d[cc * 128:(cc + 1) * 128,
                                                tc * 512:tc * 512 + 512])],
                                 waits=[(mm, w), (s3, s3.c)])

                        @block.tensor
                        def _(e, tc=tc, cc=cc, buf=buf):
                            e.wait_ge(rsl[cc % 2].h, rsl[cc % 2].c)
                            e.wait_ge(g1.h, g1.c)
                            if cc == 0:
                                e.wait_ge(ev.h, ev.c)
                            for dm_ in range(8):
                                ins = e.matmul(
                                    psd[dm_][:],
                                    wo[:, cc * DM + dm_ * 128:cc * DM + dm_ * 128 + 128],
                                    buf[:], start=(cc == 0), stop=(cc == NIT - 1))
                            ins.then_inc(mm.h, 1)
                        mm.c += 1

                    for dm_ in range(8):
                        @block.sync
                        def _(e, tc=tc, dm_=dm_):
                            wave(e, m0, [(xres[:], src[b, dm_ * 128:(dm_ + 1) * 128,
                                                       tc * 512:tc * 512 + 512])],
                                 waits=[(ev, ev.c)])

                        @BV
                        def _(e, tc=tc, dm_=dm_):
                            e.wait_ge(mm.h, mm.c)
                            e.wait_ge(m0.h, m0.c)
                            e.wait_ge(c0.h, c0.c)
                            e.wait_ge(s1.h, s1.c)
                            e.scalar_tensor_tensor(out=otmp[:], in0=psd[dm_][:],
                                                   scalar=bout[:, dm_:dm_ + 1],
                                                   in1=xres[:], op0=ALU.add,
                                                   op1=ALU.add)

                        @block.sync
                        def _(e, tc=tc, dm_=dm_):
                            wave(e, s1, [(dst[b, dm_ * 128:(dm_ + 1) * 128,
                                              tc * 512:tc * 512 + 512], otmp[:])],
                                 waits=[(ev, ev.c)])
                barrier()

        def emit_attn(b, src, dst):
            with contextlib.ExitStack() as pa:
                vall = pa.enter_context(nc.sbuf_tensor(U("a_vall"), [128, 16 * 1040], BF16))
                with contextlib.ExitStack() as p1:
                    lnx = p1.enter_context(nc.sbuf_tensor(U("a_lnx"), [128, NDT * T], BF16))
                    with contextlib.ExitStack() as pln:
                        xt2 = [pln.enter_context(nc.sbuf_tensor(U("a_xa"), [128, T], F32)),
                               pln.enter_context(nc.sbuf_tensor(U("a_xb"), [128, T], F32))]
                        tmp = pln.enter_context(nc.sbuf_tensor(U("a_tmp"), [128, T], F32))
                        acc_s = pln.enter_context(nc.sbuf_tensor(U("a_as"), [128, T], F32))
                        acc_q = pln.enter_context(nc.sbuf_tensor(U("a_aq"), [128, T], F32))
                        bc_mean = pln.enter_context(nc.sbuf_tensor(U("a_bm"), [128, T], F32))
                        bc_rstd = pln.enter_context(nc.sbuf_tensor(U("a_br"), [128, T], F32))
                        pst = pln.enter_context(nc.psum_tensor(U("a_pst"), [128, 128], F32))
                        ph = dict(xt2=xt2, tmp=tmp, acc_s=acc_s, acc_q=acc_q,
                                  bc_mean=bc_mean, bc_rstd=bc_rstd, pst=pst)
                        ln_stats(b, src, ph)
                        ln_norm_bf16(b, src, lnx, ph)
                        barrier()

                    wsb = p1.enter_context(nc.sbuf_tensor(U("a_wsb"), [128, 8 * DM], BF16))
                    bil = p1.enter_context(nc.sbuf_tensor(U("a_bil"), [128, 2 * NDT], F32))
                    brow = p1.enter_context(nc.sbuf_tensor(U("a_brow"), [1, 3 * DM], BF16))
                    stg = p1.enter_context(nc.sbuf_tensor(U("a_stg"), [128, 512], BF16))
                    psq = [p1.enter_context(nc.psum_tensor(U("a_p%d" % i), [128, 512], F32))
                           for i in range(2)]

                    @block.sync
                    def _(e):
                        wave(e, c0, [(bil[:, 0:NDT], bq_il_in),
                                     (bil[:, NDT:2 * NDT], bk_il_in)])

                    @block.gpsimd
                    def _(e):
                        wave(e, sw0, [(brow[0:1, i * DM:(i + 1) * DM], srcrow)
                                     for i, srcrow in enumerate(
                                         (bq_row_in, bk_row_in, bv_row_in))])

                    def load_w(W_in_ap):
                        @block.sync
                        def _(e, W_in_ap=W_in_ap, w=mm.c):
                            wave(e, g1, [(wsb[:, mc * DM:(mc + 1) * DM],
                                          W_in_ap[mc * 128:(mc + 1) * 128, :])
                                         for mc in range(8)],
                                 waits=[(mm, w)])

                    for wi, (W_in_ap, dram, bslot) in enumerate(
                            ((Wq_bf, qt_d, 0), (Wk_bf, kt_d, 1))):
                        load_w(W_in_ap)
                        for hdc in range(NDT):
                            for tc in range(4):
                                pb = psq[tc % 2]

                                @block.tensor
                                def _(e, hdc=hdc, tc=tc, pb=pb, w=sc.c):
                                    e.wait_ge(g1.h, g1.c)
                                    e.wait_ge(sc.h, w)
                                    for mc in range(8):
                                        ins = e.matmul(
                                            pb[:],
                                            wsb[:, mc * DM + hdc * 128:mc * DM + hdc * 128 + 128],
                                            lnx[:, mc * T + tc * 512:mc * T + tc * 512 + 512],
                                            start=(mc == 0), stop=(mc == 7))
                                    ins.then_inc(mm.h, 1)
                                mm.c += 1

                                @BS
                                def _(e, hdc=hdc, tc=tc, pb=pb, bslot=bslot):
                                    e.wait_ge(mm.h, mm.c)
                                    e.wait_ge(s4.h, s4.c)
                                    e.wait_ge(c0.h, c0.c)
                                    e.activation(out=stg[:], in_=pb[:], func=AF.Identity,
                                                 bias=bil[:, bslot * NDT + hdc:bslot * NDT + hdc + 1])

                                @block.sync
                                def _(e, hdc=hdc, tc=tc, dram=dram):
                                    wave(e, s4, [(dram[hdc * 128:(hdc + 1) * 128,
                                                       tc * 512:tc * 512 + 512], stg[:])],
                                         waits=[(sc, sc.c)])

                    load_w(Wv_bf)

                    @BV
                    def _(e):
                        e.memset(vall[:].rearrange("p (c h u) -> p c h u",
                                                   c=16, h=16)[:, :, :, 64:65], 1.0)
                    for tt in range(16):
                        for half in range(2):
                            pb = psq[half]

                            @block.tensor
                            def _(e, tt=tt, half=half, pb=pb, w=ev.c):
                                e.wait_ge(g1.h, g1.c)
                                e.wait_ge(g2.h, g2.c)
                                e.wait_ge(sw0.h, sw0.c)
                                e.wait_ge(c0.h, c0.c)
                                e.wait_ge(sc.h, sc.c)
                                e.wait_ge(ev.h, w)
                                for mc in range(8):
                                    e.matmul(pb[:],
                                             lnx[:, mc * T + tt * 128:mc * T + tt * 128 + 128],
                                             wsb[:, mc * DM + half * 512:mc * DM + half * 512 + 512],
                                             start=(mc == 0), stop=False)
                                ins = e.matmul(
                                    pb[:], ones_row[:],
                                    brow[0:1, 2 * DM + half * 512:2 * DM + half * 512 + 512],
                                    start=False, stop=True)
                                ins.then_inc(mm.h, 1)
                            mm.c += 1

                            @BV
                            def _(e, tt=tt, half=half, pb=pb):
                                e.wait_ge(mm.h, mm.c)
                                base = tt * 1040 + half * 520
                                e.tensor_copy(
                                    out=vall[:, base:base + 520].rearrange(
                                        "p (h u) -> p h u", h=8)[:, :, 0:64],
                                    in_=pb[:].rearrange("p (h u) -> p h u", h=8))

                    if b == 0:
                        relsb = p1.enter_context(
                            nc.sbuf_tensor(U("a_rel"), [128, 8 * 512], BF16))

                        @block.sync
                        def _(e, w=mm.c):
                            wave(e, g2, [(relsb[:, mc * 512:(mc + 1) * 512],
                                          relembT_bf[mc * 128:(mc + 1) * 128, :])
                                         for mc in range(8)],
                                 waits=[(mm, w)])
                        for wi, (W_in_ap, dram, bro) in enumerate(
                                ((Wk_bf, poskcd_d, 1), (Wq_bf, posqcd_d, 0))):
                            load_w(W_in_ap)
                            for cc in range(4):
                                for half in range(2):
                                    pb = psq[half]

                                    @block.tensor
                                    def _(e, cc=cc, half=half, pb=pb, bro=bro, w=sc.c):
                                        e.wait_ge(g1.h, g1.c)
                                        e.wait_ge(g2.h, g2.c)
                                        e.wait_ge(sw0.h, sw0.c)
                                        e.wait_ge(c0.h, c0.c)
                                        e.wait_ge(ev.h, ev.c)
                                        e.wait_ge(sc.h, w)
                                        for mc in range(8):
                                            e.matmul(
                                                pb[:],
                                                relsb[:, mc * 512 + cc * 128:mc * 512 + cc * 128 + 128],
                                                wsb[:, mc * DM + half * 512:mc * DM + half * 512 + 512],
                                                start=(mc == 0), stop=False)
                                        ins = e.matmul(
                                            pb[:], ones_row[:],
                                            brow[0:1, bro * DM + half * 512:bro * DM + half * 512 + 512],
                                            start=False, stop=True)
                                        ins.then_inc(mm.h, 1)
                                    mm.c += 1

                                    @BS
                                    def _(e, cc=cc, half=half, pb=pb):
                                        e.wait_ge(mm.h, mm.c)
                                        e.wait_ge(s4.h, s4.c)
                                        e.activation(out=stg[:], in_=pb[:], func=AF.Copy)

                                    @block.sync
                                    def _(e, cc=cc, half=half, dram=dram):
                                        wave(e, s4,
                                             [(dram[cc * 128:(cc + 1) * 128,
                                                    half * 512:half * 512 + 512], stg[:])],
                                             waits=[(sc, sc.c)])
                    barrier()

                with contextlib.ExitStack() as p2:
                    esb = p2.enter_context(nc.sbuf_tensor(U("a_esb"), [128, 2 * 4 * TBLP], BF16))
                    pk4 = p2.enter_context(nc.sbuf_tensor(U("a_pk4"), [128, 2 * 256], BF16))
                    tblk = p2.enter_context(nc.sbuf_tensor(U("a_tbk"), [64, TBLP], BF16))
                    tblq = p2.enter_context(nc.sbuf_tensor(U("a_tbq"), [64, TBLP], BF16))
                    qth = p2.enter_context(nc.sbuf_tensor(U("a_qth"), [64, T], BF16))
                    kth = p2.enter_context(nc.sbuf_tensor(U("a_kth"), [64, T], BF16))
                    m1sb = [p2.enter_context(nc.sbuf_tensor(U("a_m1"), [128, TBLP], BF16))
                            for _ in range(2)]
                    m2sb = [p2.enter_context(nc.sbuf_tensor(U("a_m2"), [128, TBLP], BF16))
                            for _ in range(2)]
                    m2bias = p2.enter_context(nc.sbuf_tensor(U("a_m2b"), [128, 32], F32))
                    g1sb = p2.enter_context(nc.sbuf_tensor(U("a_g1"), [128, 4 * 1152], BF16))
                    g2sb = p2.enter_context(nc.sbuf_tensor(U("a_g2"), [128, 16 * 512], BF16))
                    ptb = p2.enter_context(nc.sbuf_tensor(U("a_pt"), [128, 16 * 512], BF16))
                    rowlo = p2.enter_context(nc.sbuf_tensor(U("a_rlo"), [1, 512], BF16))
                    rowhi = p2.enter_context(nc.sbuf_tensor(U("a_rhi"), [1, 512], BF16))
                    ctxsb = p2.enter_context(nc.sbuf_tensor(U("a_ctx"), [64, T], F32))
                    dnb = p2.enter_context(nc.sbuf_tensor(U("a_dnb"), [64, T], F32))
                    xh = p2.enter_context(nc.sbuf_tensor(U("a_xh"), [64, T], F32))
                    psS = [p2.enter_context(nc.psum_tensor(U("a_s%d" % i), [128, 512], F32))
                           for i in range(2)]
                    psM2 = [p2.enter_context(nc.psum_tensor(U("a_m%d" % i), [128, 1024], F32))
                            for i in range(2)]
                    psC = p2.enter_context(nc.psum_tensor(U("a_c"), [65, 512], F32))
                    psE = p2.enter_context(nc.psum_tensor(U("a_e"), [1, 512], F32))

                    @block.sync
                    def _(e):
                        wave(e, c0,
                             [(esb[:, 0:4 * TBLP],
                               Erev_in.rearrange("(c p) w -> p c w", p=128)),
                              (esb[:, 4 * TBLP:8 * TBLP],
                               Efwd_in.rearrange("(c p) w -> p c w", p=128))])

                    for h in range(H):
                        @block.sync
                        def _(e, h=h, w=mm.c):
                            wave(e, m0,
                                 [(pk4[:, ti * 256:(ti + 1) * 256].rearrange(
                                     "p (c u) -> p c u", c=4),
                                   bass.AP(dram.tensor, h * 64,
                                           [[DM, 128], [128 * DM, 4], [1, 64]]))
                                  for ti, dram in enumerate((poskcd_d, posqcd_d))],
                                 waits=[(mm, w), (s4, s4.c)])
                            wave(e, x0, [(qth[:], qt_d[h * 64:(h + 1) * 64, :]),
                                         (kth[:], kt_d[h * 64:(h + 1) * 64, :])],
                                 waits=[(mm, w), (s4, s4.c)])

                        tmark = [ev.c, ev.c]
                        for ti, tbl in enumerate((tblk, tblq)):
                            for w3 in range(3):
                                q = ti * 3 + w3
                                pt = psM2[q % 2]

                                @block.tensor
                                def _(e, h=h, ti=ti, w3=w3, pt=pt, w=tmark[q % 2]):
                                    e.wait_ge(m0.h, m0.c)
                                    e.wait_ge(c0.h, c0.c)
                                    e.wait_ge(ev.h, w)
                                    for cc in range(4):
                                        ins = e.matmul(
                                            pt[0:64, 0:512],
                                            pk4[:, ti * 256 + cc * 64:ti * 256 + cc * 64 + 64],
                                            esb[:, ti * 4 * TBLP + cc * TBLP + w3 * 512:
                                                ti * 4 * TBLP + cc * TBLP + w3 * 512 + 512],
                                            start=(cc == 0), stop=(cc == 3))
                                    ins.then_inc(mm.h, 1)
                                mm.c += 1

                                @BV
                                def _(e, ti=ti, w3=w3, tbl=tbl, pt=pt):
                                    e.wait_ge(mm.h, mm.c)
                                    e.tensor_copy(out=tbl[:, w3 * 512:w3 * 512 + 512],
                                                  in_=pt[0:64, 0:512])
                                tmark[q % 2] = ev.c

                        mevk = [tmark[0], tmark[1]]
                        mevs = ev.c
                        for mi, (qk, tbl, ssl) in enumerate(((qth, tblk, s4),
                                                             (kth, tblq, s5))):
                            msbl = m1sb if mi == 0 else m2sb
                            wrm = [ssl.c, ssl.c]
                            for it in range(16):
                                msb = msbl[it % 2]
                                for w3 in range(3):
                                    if w3 == 2:
                                        pma = psS[0][:, 0:512]
                                        wv = mevs
                                    else:
                                        pma = psM2[it % 2][:, w3 * 512:w3 * 512 + 512]
                                        wv = mevk[it % 2]

                                    @block.tensor
                                    def _(e, mi=mi, it=it, w3=w3, qk=qk, tbl=tbl,
                                          pma=pma, w=sc.c, wv=wv):
                                        e.wait_ge(x0.h, x0.c)
                                        e.wait_ge(sc.h, w)
                                        e.wait_ge(ev.h, wv)
                                        e.matmul(pma, qk[:, it * 128:it * 128 + 128],
                                                 tbl[:, w3 * 512:w3 * 512 + 512],
                                                 start=True, stop=True).then_inc(mm.h, 1)
                                    mm.c += 1

                                    if w3 == 1:
                                        @BV
                                        def _(e, mi=mi, it=it, msb=msb,
                                              ssl=ssl, w=wrm[it % 2]):
                                            e.wait_ge(mm.h, mm.c)
                                            e.wait_ge(ssl.h, w)
                                            e.tensor_copy(out=msb[:, 0:1024],
                                                          in_=psM2[it % 2][:, 0:1024])
                                        mevk[it % 2] = ev.c
                                    elif w3 == 2:
                                        @BV
                                        def _(e, mi=mi, it=it, msb=msb,
                                              ssl=ssl, w=wrm[it % 2]):
                                            e.wait_ge(mm.h, mm.c)
                                            e.wait_ge(ssl.h, w)
                                            e.tensor_copy(out=msb[:, 1024:1536],
                                                          in_=psS[0][:, 0:512])
                                        mevs = ev.c

                                if mi == 1:
                                    @BV
                                    def _(e, it=it, msb=msb):
                                        e.tensor_scalar(out=m2bias[:, it * 2:it * 2 + 1],
                                                        in0=msb[:, 1276:1277],
                                                        scalar1=1.0 / SCALE,
                                                        scalar2=None, op0=ALU.mult)
                                        e.tensor_scalar(out=m2bias[:, it * 2 + 1:it * 2 + 2],
                                                        in0=msb[:, 254:255],
                                                        scalar1=1.0 / SCALE,
                                                        scalar2=None, op0=ALU.mult)

                                @block.sync
                                def _(e, mi=mi, it=it, msb=msb, ssl=ssl):
                                    wave(e, ssl,
                                         [(bass.AP((m1d if mi == 0 else m2d).tensor,
                                                   MGUARD + it * MBLK,
                                                   [[TBLP, 128], [1, TBLP]]), msb[:])],
                                         waits=[(ev, ev.c)])
                                wrm[it % 2] = ssl.c

                        for ic in range(4):
                            @block.sync
                            def _(e, h=h, ic=ic, w=mm.c):
                                wave(e, g3,
                                     [(g1sb[:].rearrange("p (b u) -> p b u", b=4),
                                       bass.AP(m1d.tensor, MGUARD + (4 * ic) * MBLK + 253,
                                               [[TBLP - 1, 128], [MBLK, 4], [1, 1152]]))],
                                     waits=[(mm, w), (s4, s4.c)])

                            @block.sync
                            def _(e, h=h, ic=ic, w=mm.c):
                                wave(e, x1,
                                     [(g2sb[:].rearrange("p (j u) -> p j u", j=16),
                                       bass.AP(m2d.tensor, MGUARD + HW + ic * 512,
                                               [[TBLP - 1, 128], [MBLK - 128, 16], [1, 512]]))],
                                     waits=[(mm, w), (s5, s5.c)])

                            for ei, (col, rbuf) in enumerate(((254, rowhi), (1276, rowlo))):
                                @block.tensor
                                def _(e, ic=ic, col=col, w=ev.c):
                                    e.wait_ge(ev.h, w)
                                    e.matmul(psE[:], tblk[:, col:col + 1],
                                             qth[:, ic * 512:ic * 512 + 512],
                                             start=True, stop=True).then_inc(mm.h, 1)
                                mm.c += 1

                                @BV
                                def _(e, rbuf=rbuf):
                                    e.wait_ge(mm.h, mm.c)
                                    e.tensor_copy(out=rbuf[:], in_=psE[:])

                            S0 = sc.c
                            jinfo = []
                            cum = [0]
                            for jt in range(16):
                                deltas = [4 * ic + ib - jt for ib in range(4)]
                                lo_ibs = [ib for ib in range(4) if deltas[ib] <= -(BANDR + 1)]
                                hi_ibs = [ib for ib in range(4) if deltas[ib] >= BANDR + 1]
                                band_ibs = [ib for ib in range(4) if abs(deltas[ib]) <= BANDR]
                                jinfo.append((lo_ibs, hi_ibs, band_ibs))
                                cum.append(cum[-1] + (1 if lo_ibs else 0) +
                                           (1 if band_ibs else 0) + (1 if hi_ibs else 0))
                            mmarks = []
                            for jt in range(16):
                                pb = psS[jt % 2]
                                lo_ibs, hi_ibs, band_ibs = jinfo[jt]

                                @block.tensor
                                def _(e, h=h, ic=ic, jt=jt, pb=pb,
                                      w=S0 + (cum[jt - 1] if jt >= 2 else 0),
                                      lo_ibs=lo_ibs, hi_ibs=hi_ibs, band_ibs=band_ibs):
                                    if jt == 0:
                                        e.wait_ge(x0.h, x0.c)
                                        e.wait_ge(x1.h, x1.c)
                                        e.wait_ge(g3.h, g3.c)
                                        e.wait_ge(c0.h, c0.c)
                                        e.wait_ge(ev.h, ev.c)
                                    ops = [("qk",)]
                                    if lo_ibs:
                                        ops.append(("lo",))
                                    if hi_ibs:
                                        ops.append(("hi",))
                                    if band_ibs:
                                        ops.append(("id",))
                                        ops.extend(("tr", ib) for ib in band_ibs)
                                    for oi, op in enumerate(ops):
                                        st_ = (oi == 0)
                                        sp_ = (oi == len(ops) - 1)
                                        if op[0] == "qk":
                                            ins = e.matmul(pb[:], kth[:, jt * 128:jt * 128 + 128],
                                                           qth[:, ic * 512:ic * 512 + 512],
                                                           start=st_, stop=sp_)
                                            ins._wait_ge(sc.h, w)
                                        elif op[0] == "lo":
                                            i0, i1 = lo_ibs[0] * 128, (lo_ibs[-1] + 1) * 128
                                            ins = e.matmul(pb[:, i0:i1], ones_row[:],
                                                           rowlo[0:1, i0:i1],
                                                           start=st_, stop=sp_)
                                        elif op[0] == "hi":
                                            i0, i1 = hi_ibs[0] * 128, (hi_ibs[-1] + 1) * 128
                                            ins = e.matmul(pb[:, i0:i1], ones_row[:],
                                                           rowhi[0:1, i0:i1],
                                                           start=st_, stop=sp_)
                                        elif op[0] == "id":
                                            i0, i1 = band_ibs[0] * 128, (band_ibs[-1] + 1) * 128
                                            ins = e.matmul(pb[:, i0:i1], identb[:],
                                                           g2sb[:, jt * 512 + i0:jt * 512 + i1],
                                                           start=st_, stop=sp_)
                                        else:
                                            ib = op[1]
                                            off = ib * 1152 + (jt - (4 * ic + ib) + BANDR) * 128
                                            ins = e.matmul(pb[:, ib * 128:ib * 128 + 128],
                                                           g1sb[:, off:off + 128], identb[:],
                                                           start=st_, stop=sp_)
                                    ins.then_inc(mm.h, 1)
                                mm.c += 1
                                mmarks.append(mm.c)

                            @BS
                            def _(e, h=h, ic=ic, jinfo=jinfo, mmarks=mmarks):
                                e.wait_ge(ev.h, ev.c)
                                for jt in range(16):
                                    lo_ibs, hi_ibs, band_ibs = jinfo[jt]
                                    pb = psS[jt % 2]
                                    e.wait_ge(mm.h, mmarks[jt])
                                    regions = []
                                    if lo_ibs:
                                        regions.append((lo_ibs[0] * 128, (lo_ibs[-1] + 1) * 128,
                                                        m2bias[:, jt * 2:jt * 2 + 1]))
                                    if band_ibs:
                                        regions.append((band_ibs[0] * 128,
                                                        (band_ibs[-1] + 1) * 128,
                                                        zcol[:, 0:1]))
                                    if hi_ibs:
                                        regions.append((hi_ibs[0] * 128, (hi_ibs[-1] + 1) * 128,
                                                        m2bias[:, jt * 2 + 1:jt * 2 + 2]))
                                    for (r0_, r1_, bias) in regions:
                                        e.activation(out=ptb[:, jt * 512 + r0_:jt * 512 + r1_],
                                                     in_=pb[:, r0_:r1_], func=AF.Exp,
                                                     scale=1.0 / SCALE, bias=bias)

                            @block.tensor
                            def _(e, h=h, ic=ic):
                                e.wait_ge(ev.h, ev.c)
                                for ck in range(16):
                                    ins = e.matmul(
                                        psC[:],
                                        vall[:, ck * 1040 + h * 65:ck * 1040 + h * 65 + 65],
                                        ptb[:, ck * 512:ck * 512 + 512],
                                        start=(ck == 0), stop=(ck == 15))
                                    if ck == 0:
                                        ins._wait_ge(sc.h, sc.c)
                                ins.then_inc(mm.h, 1)
                            mm.c += 1

                            @BV
                            def _(e, ic=ic):
                                e.wait_ge(mm.h, mm.c)
                                e.wait_ge(rw.h, rw.c)
                                e.tensor_copy(out=rows[0:1, R_DEN * T + ic * 512:
                                                  R_DEN * T + ic * 512 + 512],
                                              in_=psC[64:65, :])

                            @BV
                            def _(e, ic=ic):
                                e.wait_ge(mm.h, mm.c)
                                e.wait_ge(bcs.h, bcs.c)
                                e.tensor_copy(out=ctxsb[:, ic * 512:ic * 512 + 512],
                                              in_=psC[0:64, :])

                        @BS
                        def _(e, h=h):
                            e.wait_ge(ev.h, ev.c)
                            e.activation(out=R(R_DEN), in_=R(R_DEN), func=AF.Ln)
                            e.activation(out=R(R_DEN), in_=R(R_DEN), func=AF.Exp,
                                         scale=-1.0)
                        bcast_row([(dnb[:], R(R_DEN), 64)], waits=[(sc, sc.c)])

                        @block.sync
                        def _(e, h=h, w=s1.c):
                            wave(e, m0, [(xh[:], src[b, h * 64:(h + 1) * 64, :])],
                                 waits=[(ev, ev.c), (s1, w)])

                        @BV
                        def _(e, h=h):
                            e.wait_ge(m0.h, m0.c)
                            e.wait_ge(bcs.h, bcs.c)
                            e.tensor_mul(ctxsb[:], ctxsb[:], dnb[:])
                            e.tensor_add(xh[:], xh[:], ctxsb[:])

                        @block.sync
                        def _(e, h=h):
                            wave(e, s1, [(dst[b, h * 64:(h + 1) * 64, :], xh[:])],
                                 waits=[(ev, ev.c)])
                    barrier()

        first = True
        for sl in SUBLAYERS:
            src = xin if first else xs
            dst = outT if sl == SUBLAYERS[-1] else xs
            if sl == 0:
                for b in range(BPC):
                    emit_attn(b, src, dst)
            elif sl == 2:
                for b in range(BPC):
                    emit_ssm(b, src, dst)
            else:
                for b in range(BPC):
                    emit_ffn(b, src, dst)
            first = False

        @block.sync
        def _(e):
            for s_ in SYNC_SLOTS:
                e.wait_ge(s_.h, s_.c)

        @block.gpsimd
        def _(e):
            for s_ in GQ_SLOTS:
                e.wait_ge(s_.h, s_.c)

    gstack.close()
    return nc



def host_weights(inputs):
    g = lambda k: np.asarray(inputs[k])
    Ef, Er = build_E()
    d = {}
    d["relembT"] = np.ascontiguousarray(g("L0_attn_a_rel_embeddings").T)
    d["Wq"] = np.ascontiguousarray(g("L0_attn_a_Wq"))
    d["Wk"] = np.ascontiguousarray(g("L0_attn_a_Wk"))
    d["Wv"] = np.ascontiguousarray(g("L0_attn_a_Wv"))
    for nm, key in (("bq", "L0_attn_a_bq"), ("bk", "L0_attn_a_bk"),
                    ("bv", "L0_attn_a_bv")):
        v = g(key).astype(np.float32)
        d[nm + "_row"] = np.ascontiguousarray(v[None, :])
        d[nm + "_il"] = il(v, NDT)
    d["Erev"] = Er
    d["Efwd"] = Ef
    d["identf"] = np.eye(128, dtype=np.float32)
    d["identb"] = np.eye(128).astype(BF)
    d["ones_row"] = np.ones((1, 128), dtype=BF)
    d["w1T"] = np.ascontiguousarray(g("L0_ffn_w1").T)
    d["w2T"] = np.ascontiguousarray(g("L0_ffn_w2").T)
    cw = g("L0_ffn_conv_w").reshape(DH, 3)
    d["fconvw_il"] = np.ascontiguousarray(
        cw.reshape(32, 128, 3).transpose(1, 0, 2).reshape(128, 96)).astype(np.float32)
    d["fconvb_il"] = il(g("L0_ffn_conv_b").astype(np.float32), 32)
    d["WinT"] = np.ascontiguousarray(g("L0_attn_b_W_in").T)
    d["WxT"] = np.ascontiguousarray(g("L0_attn_b_W_x").T)
    d["WoutT"] = np.ascontiguousarray(g("L0_attn_b_W_out").T)
    d["bin_il"] = il(g("L0_attn_b_b_in").astype(np.float32), 32)
    bx = g("L0_attn_b_b_x").astype(np.float32)
    d["bxd_il"] = il(bx[:DI], 16)
    d["bx_bc"] = np.ascontiguousarray(bx[DI:][:, None])
    d["bout_il"] = il(g("L0_attn_b_b_out").astype(np.float32), 8)
    d["bconv_il"] = il(g("L0_attn_b_b_conv").astype(np.float32), 16)
    acw = g("L0_attn_b_W_conv").reshape(DI, 4)
    d["aconvw_il"] = np.ascontiguousarray(
        acw.reshape(16, 128, 4).transpose(1, 0, 2).reshape(128, 64)).astype(np.float32)
    Alog = g("L0_attn_b_A_log").astype(np.float32)
    d["A_il"] = np.ascontiguousarray(
        Alog.reshape(16, 128, 16).transpose(1, 0, 2).reshape(128, 256))
    d["D_il"] = il(g("L0_attn_b_D").astype(np.float32), 16)
    for k in d:
        if d[k].dtype == np.float64:
            d[k] = d[k].astype(np.float32)
    return d


def kernel(**inputs):
    x = np.asarray(inputs["L0_attn_a_hidden_states"])
    shared = host_weights(inputs)
    in_maps = []
    for c in range(NCORES):
        xc = x[c * BPC:(c + 1) * BPC]
        m = dict(shared)
        m["xT"] = np.ascontiguousarray(xc.transpose(0, 2, 1))
        in_maps.append(m)
    ncores = DEV_CORES if DEV_CORES else NCORES
    nc = build_program()
    res = run_bass_kernel_spmd(nc, in_maps[:ncores], core_ids=list(range(ncores)))
    outs = []
    for c in range(ncores):
        o = np.asarray(res.results[c]["outT"])
        outs.append(o.transpose(0, 2, 1))
    out = np.concatenate(outs, axis=0).astype(np.float32)
    if out.shape[0] < x.shape[0]:
        out = np.concatenate(
            [out, np.zeros((x.shape[0] - out.shape[0],) + out.shape[1:], np.float32)], 0)
    return out


# version 19
# speedup vs baseline: 1.0667x; 1.0666x over previous
import contextlib
import math
import os as _os
import numpy as np
import ml_dtypes
import concourse.bass as bass
import concourse.mybir as mybir
from concourse.bass_utils import run_bass_kernel_spmd

F32 = mybir.dt.float32
BF16 = mybir.dt.bfloat16
AF = mybir.ActivationFunctionType
ALU = mybir.AluOpType
AX = mybir.AxisListType
BF = ml_dtypes.bfloat16

NCORES = 8
BPC = 2
T = 2048
DM = 1024
NDT = DM // 128
H = 16
HD = 64
SCALE = math.sqrt(HD * 3.0)
DH = 4096
NCT = DH // 128
DI = 2048
NIT = DI // 128
NS = 16
EPS = 1e-5
MAGIC = float(3 << 22)

HW = 765
TBLW = 2 * HW + 1
TBLP = 1536
MBLK = 128 * TBLP
MGUARD = MBLK
BANDR = 4

SUBLAYERS = tuple(int(t) for t in _os.environ.get("SUBL", "0,1,2,3").split(","))
DEV_CORES = int(_os.environ.get("DEV_CORES", "0"))
TCH = 512
NCH = T // TCH

R_MEAN, R_RSTD, R_MAX, R_TMP, R_T2, R_DQ = range(6)
R_DEN = R_MAX
R_QS = R_T2


def _log_bucket_np(rel):
    mid = 128
    rel = np.clip(rel, -511, 511)
    sign = np.sign(rel).astype(np.float64)
    abs_pos = np.maximum(np.abs(rel), 1)
    log_base = math.log(511 / mid)
    log_pos = np.ceil(np.log(abs_pos.astype(np.float64) / mid) /
                      log_base * (mid - 1)) + mid
    bucket = np.where(abs_pos < mid, abs_pos.astype(np.float64), log_pos) * sign
    return bucket.astype(np.int64)


def _col_of_d(d):
    return np.clip(_log_bucket_np(np.asarray(d)) + 256, 0, 511)


def build_E():
    cols_f = _col_of_d(np.arange(TBLW) - HW)
    cols_r = _col_of_d(HW - np.arange(TBLW))
    Ef = np.zeros((512, TBLP), dtype=BF)
    Er = np.zeros((512, TBLP), dtype=BF)
    Ef[cols_f, np.arange(TBLW)] = BF(1.0)
    Er[cols_r, np.arange(TBLW)] = BF(1.0)
    return Ef, Er


def il(v, k):
    return np.ascontiguousarray(np.asarray(v).reshape(k, 128).T)


class SP:
    def __init__(self, h):
        self.h = h
        self.c = 0


_AUTO_OPS = frozenset((
    "tensor_copy", "tensor_tensor", "tensor_add", "tensor_sub", "tensor_mul",
    "tensor_max", "tensor_scalar", "tensor_single_scalar", "tensor_scalar_add",
    "tensor_scalar_sub", "tensor_scalar_mul", "tensor_scalar_max",
    "tensor_scalar_min", "tensor_relu", "tensor_reduce", "reduce_max",
    "reduce_sum", "scalar_tensor_tensor", "tensor_tensor_scan", "memset",
    "memzero", "reciprocal", "select", "copy_predicated",
    "activation", "copy", "add", "mul", "square", "sqrt", "sign",
))


class EP:

    def __init__(self, e, sp):
        self._e = e
        self._sp = sp

    def __getattr__(self, name):
        fn = getattr(self._e, name)
        if name in _AUTO_OPS:
            sp = self._sp
            e = self._e

            def call(*a, **k):
                ins = fn(*a, **k)
                ins._wait_ge(sp.h, sp.c)
                ins.then_inc(sp.h, 1)
                sp.c += 1
                return ins
            return call
        return fn


def build_program():
    nc = bass.Bass("TRN2", target_bir_lowering=False)
    gstack = contextlib.ExitStack()
    _uid = [0]

    def U(name):
        _uid[0] += 1
        return "%s_%d" % (name, _uid[0])

    def din(name, shape, dt):
        return nc.dram_tensor(name, list(shape), dt, kind="ExternalInput").ap()

    xin = din("xT", (BPC, DM, T), F32)
    outT = nc.dram_tensor("outT", [BPC, DM, T], F32, kind="ExternalOutput").ap()

    relembT_in = din("relembT", (DM, 512), F32)
    Wq_in = din("Wq", (DM, DM), F32)
    Wk_in = din("Wk", (DM, DM), F32)
    Wv_in = din("Wv", (DM, DM), F32)
    bq_row_in = din("bq_row", (1, DM), F32)
    bk_row_in = din("bk_row", (1, DM), F32)
    bv_row_in = din("bv_row", (1, DM), F32)
    bq_il_in = din("bq_il", (128, NDT), F32)
    bk_il_in = din("bk_il", (128, NDT), F32)
    Erev_in = din("Erev", (512, TBLP), BF16)
    Efwd_in = din("Efwd", (512, TBLP), BF16)
    identf_in = din("identf", (128, 128), F32)
    identb_in = din("identb", (128, 128), BF16)
    ones_row_in = din("ones_row", (1, 128), BF16)

    w1T_in = din("w1T", (DM, DH), F32)
    w2T_in = din("w2T", (DH, DM), F32)
    fconvw_in = din("fconvw_il", (128, 96), F32)
    fconvb_in = din("fconvb_il", (128, 32), F32)

    WinT_in = din("WinT", (DM, 2 * DI), F32)
    WxT_in = din("WxT", (DI, DI + 2 * NS), F32)
    WoutT_in = din("WoutT", (DI, DM), F32)
    bin_il_in = din("bin_il", (128, 32), F32)
    bxd_il_in = din("bxd_il", (128, 16), F32)
    bx_bc_in = din("bx_bc", (32, 1), F32)
    bout_il_in = din("bout_il", (128, 8), F32)
    bconv_il_in = din("bconv_il", (128, 16), F32)
    aconvw_in = din("aconvw_il", (128, 64), F32)
    A_il_in = din("A_il", (128, 256), F32)
    D_il_in = din("D_il", (128, 16), F32)

    xs = nc.dram_tensor("xs", [BPC, DM, T], F32, kind="Internal").ap()
    WinT_bf = nc.dram_tensor("WinT_bf", [DM, 2 * DI], BF16, kind="Internal").ap()
    WxT_bf = nc.dram_tensor("WxT_bf", [DI, DI + 2 * NS], BF16, kind="Internal").ap()
    WoutT_bf = nc.dram_tensor("WoutT_bf", [DI, DM], BF16, kind="Internal").ap()
    Wq_bf = nc.dram_tensor("Wq_bf", [DM, DM], BF16, kind="Internal").ap()
    Wk_bf = nc.dram_tensor("Wk_bf", [DM, DM], BF16, kind="Internal").ap()
    Wv_bf = nc.dram_tensor("Wv_bf", [DM, DM], BF16, kind="Internal").ap()
    relembT_bf = nc.dram_tensor("relembT_bf", [DM, 512], BF16, kind="Internal").ap()
    m1d = nc.dram_tensor("m1d", [2 * MGUARD + 16 * MBLK], BF16, kind="Internal").ap()
    m2d = nc.dram_tensor("m2d", [2 * MGUARD + 16 * MBLK], BF16, kind="Internal").ap()
    qt_d = nc.dram_tensor("qt_d", [DM, T], BF16, kind="Internal").ap()
    kt_d = nc.dram_tensor("kt_d", [DM, T], BF16, kind="Internal").ap()
    poskcd_d = nc.dram_tensor("poskcd_d", [512, DM], BF16, kind="Internal").ap()
    posqcd_d = nc.dram_tensor("posqcd_d", [512, DM], BF16, kind="Internal").ap()
    w1q_d = nc.dram_tensor("w1q_d", [DM, DH], BF16, kind="Internal").ap()
    w2q_d = nc.dram_tensor("w2q_d", [DH, DM], BF16, kind="Internal").ap()
    hsil_d = nc.dram_tensor("hsil_d", [DH, T], F32, kind="Internal").ap()
    xq2_d = nc.dram_tensor("xq2_d", [DH, T], BF16, kind="Internal").ap()
    xi_d = nc.dram_tensor("xi_d", [DI, T], F32, kind="Internal").ap()
    res_d = nc.dram_tensor("res_d", [DI, T], F32, kind="Internal").ap()
    dlt_d = nc.dram_tensor("dlt_d", [DI, T], F32, kind="Internal").ap()
    dltx_d = nc.dram_tensor("dltx_d", [DI, T], F32, kind="Internal").ap()
    yt_d = nc.dram_tensor("yt_d", [DI, T], BF16, kind="Internal").ap()
    row_d = nc.dram_tensor("row_d", [1, T], F32, kind="Internal").ap()
    bc_d = nc.dram_tensor("bc_d", [32, T], BF16, kind="Internal").ap()

    with nc.Block() as block:
        identf = gstack.enter_context(nc.sbuf_tensor("sb_identf", [128, 128], F32))
        identb = gstack.enter_context(nc.sbuf_tensor("sb_identb", [128, 128], BF16))
        ones_row = gstack.enter_context(nc.sbuf_tensor("sb_ones", [1, 128], BF16))
        rows = gstack.enter_context(nc.sbuf_tensor("sb_rows", [1, 6 * T], F32))
        rows2 = gstack.enter_context(nc.sbuf_tensor("sb_rows2", [1, 64], F32))
        red16 = gstack.enter_context(nc.sbuf_tensor("sb_red16", [128, 16], F32))
        r16t = gstack.enter_context(nc.sbuf_tensor("sb_r16t", [16, 128], F32))
        zcol = gstack.enter_context(nc.sbuf_tensor("sb_zcol", [128, 1], F32))

        sems = {}

        def sp(name):
            if name not in sems:
                sems[name] = SP(gstack.enter_context(nc.semaphore(name)))
            return sems[name]

        mm = sp("mm")
        ev = sp("ev")
        sc = sp("sc")
        SYNC_SLOTS = [sp(n) for n in
                      ("c0", "x0", "x1", "r0", "r1", "rw", "bc", "m0",
                       "s0", "s1", "s2", "s3", "s4", "s5", "s6", "s7")]
        GQ_SLOTS = [sp(n) for n in ("g0", "g1", "g2", "g3", "sw0")]
        (c0, x0, x1, r0, r1, rw, bcs, m0,
         s0, s1, s2, s3, s4, s5, s6, s7) = SYNC_SLOTS
        g0, g1, g2, g3, sw0 = GQ_SLOTS
        xsl = [x0, x1]
        rsl = [r0, r1]
        bsy = sp("bsy")
        bte = sp("bte")
        bve = sp("bve")
        bse = sp("bse")
        bge = sp("bge")

        def BV(f):
            @block.vector
            def _(e, f=f):
                f(EP(e, ev))

        def BS(f):
            @block.scalar
            def _(e, f=f):
                f(EP(e, sc))

        gq = sp("gq")

        def BG(f):
            @block.gpsimd
            def _(e, f=f):
                f(EP(e, gq))

        def R(i):
            return rows[0:1, i * T:(i + 1) * T]

        def wave(e, slot, pairs, waits=()):
            e.wait_ge(slot.h, slot.c)
            for s_, v_ in waits:
                e.wait_ge(s_.h, v_)
            for out, in_ in pairs:
                e.dma_start(out=out, in_=in_).then_inc(slot.h, 16)
                slot.c += 16

        def barrier():
            @block.sync
            def _(e):
                for s_ in SYNC_SLOTS:
                    e.wait_ge(s_.h, s_.c)
                e.sem_inc(bsy.h, 1)
            bsy.c += 1

            @block.gpsimd
            def _(e):
                for s_ in GQ_SLOTS:
                    e.wait_ge(s_.h, s_.c)
                e.sem_inc(bge.h, 1)
            bge.c += 1

            @block.tensor
            def _(e):
                e.sem_inc(bte.h, 1)
            bte.c += 1

            @block.vector
            def _(e):
                e.wait_ge(ev.h, ev.c)
                e.sem_inc(bve.h, 1)
            bve.c += 1

            @block.scalar
            def _(e):
                e.wait_ge(sc.h, sc.c)
                e.sem_inc(bse.h, 1)
            bse.c += 1

            for en, own in (("sync", bsy), ("gpsimd", bge), ("tensor", bte),
                            ("vector", bve), ("scalar", bse)):
                @getattr(block, en)
                def _(e, own=own):
                    for s_ in (bsy, bge, bte, bve, bse):
                        if s_ is not own:
                            e.wait_ge(s_.h, s_.c)

        @block.sync
        def _(e):
            wave(e, c0, [(identf[:], identf_in), (identb[:], identb_in),
                         (ones_row[:], ones_row_in)])

        @block.gpsimd
        def _(e):
            CH = 65536
            pairs = []
            for dst, src, n in ((WinT_bf, WinT_in, DM * 2 * DI),
                                (WxT_bf, WxT_in, DI * (DI + 2 * NS)),
                                (WoutT_bf, WoutT_in, DI * DM),
                                (Wq_bf, Wq_in, DM * DM),
                                (Wk_bf, Wk_in, DM * DM),
                                (Wv_bf, Wv_in, DM * DM),
                                (relembT_bf, relembT_in, DM * 512)):
                assert n % CH == 0, n
                pairs.append((bass.AP(dst.tensor, 0, [[CH, n // CH], [1, CH]]),
                              bass.AP(src.tensor, 0, [[CH, n // CH], [1, CH]])))
            wave(e, sw0, pairs)
        barrier()

        @BV
        def _(e):
            e.memset(zcol[:], 0.0)

        def tr_reduce(src_sb, op, pst):
            for k in range(16):
                @block.tensor
                def _(e, k=k):
                    e.wait_ge(ev.h, ev.c)
                    e.wait_ge(sc.h, sc.c)
                    e.wait_ge(c0.h, c0.c)
                    e.matmul(pst[:], src_sb[:, k * 128:(k + 1) * 128], identf[:],
                             start=True, stop=True,
                             is_transpose=True).then_inc(mm.h, 1)
                mm.c += 1

                @BV
                def _(e, k=k):
                    e.wait_ge(mm.h, mm.c)
                    e.tensor_reduce(out=red16[:, k:k + 1], in_=pst[:], axis=AX.X,
                                    op=op)

        def row_from_16(out_row, pst):
            @block.tensor
            def _(e):
                e.wait_ge(ev.h, ev.c)
                e.wait_ge(sc.h, sc.c)
                e.wait_ge(c0.h, c0.c)
                e.matmul(pst[0:16, 0:128], red16[:, 0:16], identf[:], start=True,
                         stop=True, is_transpose=True).then_inc(mm.h, 1)
            mm.c += 1

            @BS
            def _(e):
                e.wait_ge(mm.h, mm.c)
                e.wait_ge(rw.h, rw.c)
                e.activation(out=r16t[:], in_=pst[0:16, 0:128], func=AF.Copy)

            @block.sync
            def _(e):
                wave(e, rw, [(row_d[0:1, :], r16t[:])],
                     waits=[(sc, sc.c), (bcs, bcs.c)])
                wave(e, rw, [(out_row, row_d[0:1, :])])

        def bcast_row(dst_pairs, waits=()):
            @block.sync
            def _(e):
                for dst_sb, src_row, parts in dst_pairs:
                    w = src_row.shape[-1]
                    wave(e, rw, [(row_d[0:1, 0:w], src_row)],
                         waits=list(waits) + [(bcs, bcs.c)])
                    wave(e, bcs, [(dst_sb, row_d[0:1, 0:w].partition_broadcast(parts))],
                         waits=[(rw, rw.c)])

        def ln_stats(b, src, ph):
            for dt in range(NDT):
                buf = ph["xt2"][dt % 2]

                @block.sync
                def _(e, dt=dt, buf=buf):
                    wave(e, xsl[dt % 2], [(buf[:], src[b, dt * 128:(dt + 1) * 128, :])],
                         waits=[(ev, ev.c)])

                @BS
                def _(e, dt=dt, buf=buf):
                    e.wait_ge(xsl[dt % 2].h, xsl[dt % 2].c)
                    e.wait_ge(ev.h, ev.c)
                    e.activation(out=ph["tmp"][:], in_=buf[:], func=AF.Square)

                @BV
                def _(e, dt=dt, buf=buf):
                    e.wait_ge(xsl[dt % 2].h, xsl[dt % 2].c)
                    if dt == 0:
                        e.tensor_copy(out=ph["acc_s"][:], in_=buf[:])
                    else:
                        e.tensor_add(ph["acc_s"][:], ph["acc_s"][:], buf[:])
                    e.wait_ge(sc.h, sc.c)
                    if dt == 0:
                        e.tensor_copy(out=ph["acc_q"][:], in_=ph["tmp"][:])
                    else:
                        e.tensor_add(ph["acc_q"][:], ph["acc_q"][:], ph["tmp"][:])

            tr_reduce(ph["acc_s"], ALU.add, ph["pst"])
            row_from_16(R(R_MEAN), ph["pst"])
            tr_reduce(ph["acc_q"], ALU.add, ph["pst"])
            row_from_16(R(R_TMP), ph["pst"])

            @BV
            def _(e):
                e.wait_ge(rw.h, rw.c)
                e.tensor_scalar(out=R(R_MEAN), in0=R(R_MEAN), scalar1=1.0 / DM,
                                scalar2=None, op0=ALU.mult)
                e.tensor_mul(R(R_T2), R(R_MEAN), R(R_MEAN))
                e.tensor_scalar(out=R(R_TMP), in0=R(R_TMP), scalar1=1.0 / DM,
                                scalar2=None, op0=ALU.mult)
                e.tensor_sub(R(R_TMP), R(R_TMP), R(R_T2))
                e.tensor_scalar(out=R(R_TMP), in0=R(R_TMP), scalar1=EPS,
                                scalar2=None, op0=ALU.add)

            @BS
            def _(e):
                e.wait_ge(ev.h, ev.c)
                e.activation(out=R(R_TMP), in_=R(R_TMP), func=AF.Ln)
                e.activation(out=R(R_RSTD), in_=R(R_TMP), func=AF.Exp, scale=-0.5)

            pairs = [(ph["bc_mean"][:], R(R_MEAN), 128)]
            if ph.get("bc_rstd") is not None:
                pairs.append((ph["bc_rstd"][:], R(R_RSTD), 128))
            bcast_row(pairs, waits=[(sc, sc.c)])

        def ln_norm_bf16(b, src, out_sb, ph):
            for dt in range(NDT):
                buf = ph["xt2"][dt % 2]

                @block.sync
                def _(e, dt=dt, buf=buf):
                    wave(e, xsl[dt % 2], [(buf[:], src[b, dt * 128:(dt + 1) * 128, :])],
                         waits=[(ev, ev.c)])

                @BV
                def _(e, dt=dt, buf=buf):
                    e.wait_ge(xsl[dt % 2].h, xsl[dt % 2].c)
                    e.wait_ge(bcs.h, bcs.c)
                    e.tensor_sub(ph["tmp"][:], buf[:], ph["bc_mean"][:])
                    e.tensor_mul(out_sb[:, dt * T:(dt + 1) * T], ph["tmp"][:],
                                 ph["bc_rstd"][:])

        def ternarize(wT_ap, nt, cols, wq_ap, mw_slot, ph):
            for i in range(nt):
                buf = ph["xt2"][i % 2]

                @block.sync
                def _(e, i=i, buf=buf):
                    wave(e, xsl[i % 2], [(buf[:, 0:cols], wT_ap[i * 128:(i + 1) * 128, :])],
                         waits=[(ev, ev.c)])

                @BV
                def _(e, i=i, buf=buf):
                    e.wait_ge(xsl[i % 2].h, xsl[i % 2].c)
                    e.tensor_reduce(out=red16[:, 0:1], in_=buf[:, 0:cols], axis=AX.X,
                                    op=ALU.add, apply_absolute_value=True)
                    if i == 0:
                        e.tensor_copy(out=red16[:, 1:2], in_=red16[:, 0:1])
                    else:
                        e.tensor_add(red16[:, 1:2], red16[:, 1:2], red16[:, 0:1])

            @block.tensor
            def _(e):
                e.wait_ge(ev.h, ev.c)
                e.wait_ge(c0.h, c0.c)
                e.matmul(ph["pst"][0:1, 0:128], red16[:, 1:2], identf[:], start=True,
                         stop=True, is_transpose=True).then_inc(mm.h, 1)
            mm.c += 1

            @BV
            def _(e):
                e.wait_ge(mm.h, mm.c)
                e.tensor_reduce(out=rows2[0:1, mw_slot:mw_slot + 1],
                                in_=ph["pst"][0:1, 0:128], axis=AX.X, op=ALU.add)
                e.tensor_scalar(out=rows2[0:1, mw_slot:mw_slot + 1],
                                in0=rows2[0:1, mw_slot:mw_slot + 1],
                                scalar1=1.0 / (nt * 128 * cols), scalar2=1e-5,
                                op0=ALU.mult, op1=ALU.max)
                e.reciprocal(out=rows2[0:1, mw_slot + 8:mw_slot + 9],
                             in_=rows2[0:1, mw_slot:mw_slot + 1])

            bcast_row([(ph["swb"][:, 0:1], rows2[0:1, mw_slot + 8:mw_slot + 9], 128)],
                      waits=[(ev, ev.c)])

            for i in range(nt):
                buf = ph["xt2"][i % 2]

                @block.sync
                def _(e, i=i, buf=buf):
                    wave(e, xsl[i % 2], [(buf[:, 0:cols], wT_ap[i * 128:(i + 1) * 128, :])],
                         waits=[(ev, ev.c), (bcs, bcs.c)])

                @BV
                def _(e, i=i, buf=buf):
                    e.wait_ge(xsl[i % 2].h, xsl[i % 2].c)
                    e.wait_ge(bcs.h, bcs.c)
                    e.wait_ge(sw0.h, sw0.c)
                    e.tensor_scalar(out=ph["tmp"][:, 0:cols], in0=buf[:, 0:cols],
                                    scalar1=ph["swb"][:, 0:1], scalar2=MAGIC,
                                    op0=ALU.mult, op1=ALU.add)
                    e.tensor_scalar(out=ph["tmp"][:, 0:cols], in0=ph["tmp"][:, 0:cols],
                                    scalar1=MAGIC, scalar2=1.0,
                                    op0=ALU.subtract, op1=ALU.min)
                    e.tensor_scalar(out=ph["tmp"][:, 0:cols], in0=ph["tmp"][:, 0:cols],
                                    scalar1=-1.0, scalar2=None, op0=ALU.max)

                @block.gpsimd
                def _(e, i=i):
                    wave(e, sw0, [(wq_ap[i * 128:(i + 1) * 128, :], ph["tmp"][:, 0:cols])],
                         waits=[(ev, ev.c)])

        if 1 in SUBLAYERS or 3 in SUBLAYERS:
            with contextlib.ExitStack() as ps:
                ph = dict(
                    xt2=[ps.enter_context(nc.sbuf_tensor(U("tz_a"), [128, DH], F32)),
                         ps.enter_context(nc.sbuf_tensor(U("tz_b"), [128, DH], F32))],
                    tmp=ps.enter_context(nc.sbuf_tensor(U("tz_t"), [128, DH], F32)),
                    swb=ps.enter_context(nc.sbuf_tensor(U("tz_s"), [128, 1], F32)),
                    pst=ps.enter_context(nc.psum_tensor(U("tz_p"), [128, 128], F32)),
                )
                ternarize(w1T_in, NDT, DH, w1q_d, 0, ph)
                ternarize(w2T_in, NCT, DM, w2q_d, 1, ph)
                barrier()

        def emit_ffn(b, src, dst):
            with contextlib.ExitStack() as ps:
                xt2 = [ps.enter_context(nc.sbuf_tensor(U("f_xa"), [128, T], F32)),
                       ps.enter_context(nc.sbuf_tensor(U("f_xb"), [128, T], F32))]
                tmp = ps.enter_context(nc.sbuf_tensor(U("f_tmp"), [128, T], F32))
                acc_s = ps.enter_context(nc.sbuf_tensor(U("f_as"), [128, T], F32))
                acc_q = ps.enter_context(nc.sbuf_tensor(U("f_aq"), [128, T], F32))
                bc_mean = ps.enter_context(nc.sbuf_tensor(U("f_bm"), [128, T], F32))
                bc_b = ps.enter_context(nc.sbuf_tensor(U("f_bb"), [128, T], F32))
                bc_c = acc_s
                xq = ps.enter_context(nc.sbuf_tensor(U("f_xq"), [128, NDT * T], BF16))
                fwt = ps.enter_context(nc.sbuf_tensor(U("f_fwt"), [128, 96], F32))
                fbt = ps.enter_context(nc.sbuf_tensor(U("f_fbt"), [128, 32], F32))
                ph = dict(xt2=xt2, tmp=tmp, acc_s=acc_s, acc_q=acc_q,
                          bc_mean=bc_mean, bc_rstd=None)

                @block.sync
                def _(e):
                    wave(e, c0, [(fwt[:], fconvw_in), (fbt[:], fconvb_in)])

                with nc.psum_tensor(U("f_pst"), [128, 128], F32) as pst:
                    ph["pst"] = pst
                    ln_stats(b, src, ph)

                    for dt in range(NDT):
                        buf = xt2[dt % 2]

                        @block.sync
                        def _(e, dt=dt, buf=buf):
                            wave(e, xsl[dt % 2],
                                 [(buf[:], src[b, dt * 128:(dt + 1) * 128, :])],
                                 waits=[(ev, ev.c)])

                        @BV
                        def _(e, dt=dt, buf=buf):
                            e.wait_ge(xsl[dt % 2].h, xsl[dt % 2].c)
                            e.wait_ge(bcs.h, bcs.c)
                            e.tensor_sub(tmp[:], buf[:], bc_mean[:])
                            if dt == 0:
                                e.tensor_copy(out=acc_q[:], in_=tmp[:])
                                e.tensor_copy(out=bc_b[:], in_=tmp[:])
                            else:
                                e.tensor_tensor(out=acc_q[:], in0=acc_q[:], in1=tmp[:],
                                                op=ALU.max)
                                e.tensor_tensor(out=bc_b[:], in0=bc_b[:], in1=tmp[:],
                                                op=ALU.min)

                    tr_reduce(acc_q, ALU.max, pst)
                    row_from_16(R(R_MAX), pst)
                    tr_reduce(bc_b, ALU.min, pst)
                    row_from_16(R(R_TMP), pst)

                @BV
                def _(e):
                    e.wait_ge(rw.h, rw.c)
                    e.tensor_scalar(out=R(R_TMP), in0=R(R_TMP), scalar1=-1.0,
                                    scalar2=None, op0=ALU.mult)
                    e.tensor_tensor(out=R(R_MAX), in0=R(R_MAX), in1=R(R_TMP),
                                    op=ALU.max)
                    e.tensor_mul(R(R_TMP), R(R_RSTD), R(R_MAX))
                    e.tensor_scalar(out=R(R_TMP), in0=R(R_TMP), scalar1=1e-5,
                                    scalar2=None, op0=ALU.max)
                    e.tensor_scalar(out=R(R_DQ), in0=R(R_TMP),
                                    scalar1=rows2[0:1, 0:1], scalar2=1.0 / 127.0,
                                    op0=ALU.mult, op1=ALU.mult)
                    e.reciprocal(out=R(R_T2), in_=R(R_TMP))
                    e.tensor_scalar(out=R(R_T2), in0=R(R_T2), scalar1=127.0,
                                    scalar2=None, op0=ALU.mult)
                    e.tensor_mul(R(R_QS), R(R_RSTD), R(R_T2))

                bcast_row([(bc_b[:], R(R_QS), 128), (bc_c[:], R(R_DQ), 128)],
                          waits=[(ev, ev.c)])

                for dt in range(NDT):
                    buf = xt2[dt % 2]

                    @block.sync
                    def _(e, dt=dt, buf=buf):
                        wave(e, xsl[dt % 2], [(buf[:], src[b, dt * 128:(dt + 1) * 128, :])],
                             waits=[(ev, ev.c)])

                    @BV
                    def _(e, dt=dt, buf=buf):
                        e.wait_ge(xsl[dt % 2].h, xsl[dt % 2].c)
                        e.wait_ge(bcs.h, bcs.c)
                        e.tensor_sub(tmp[:], buf[:], bc_mean[:])
                        e.tensor_mul(tmp[:], tmp[:], bc_b[:])
                        e.tensor_scalar(out=xq[:, dt * T:(dt + 1) * T], in0=tmp[:],
                                        scalar1=MAGIC, scalar2=MAGIC,
                                        op0=ALU.add, op1=ALU.subtract)
                barrier()

                with contextlib.ExitStack() as p1:
                    mnb = p1.enter_context(nc.sbuf_tensor(U("f_mn"), [128, T], F32))
                    hrow = bc_mean
                    cacc = tmp
                    w1t = [p1.enter_context(nc.sbuf_tensor(U("f_w1t"), [128, 8 * 128], BF16))
                           for _ in range(2)]
                    psq = [p1.enter_context(nc.psum_tensor(U("f_p%d" % i), [128, 512], F32))
                           for i in range(2)]

                    evmark = [0, 0]
                    wmark = [0, 0]
                    for ct in range(NCT):
                        wb = w1t[ct % 2]

                        @block.sync
                        def _(e, ct=ct, wb=wb, w=wmark[ct % 2]):
                            wave(e, xsl[ct % 2],
                                 [(wb[:].rearrange("p (c u) -> p c u", c=8),
                                   w1q_d[:, ct * 128:(ct + 1) * 128].rearrange(
                                       "(c p) u -> p c u", p=128))],
                                 waits=[(mm, w)])
                        for tc in range(4):
                            pb = psq[tc % 2]

                            @block.tensor
                            def _(e, ct=ct, tc=tc, pb=pb, wb=wb, w=evmark[tc % 2]):
                                e.wait_ge(xsl[ct % 2].h, xsl[ct % 2].c)
                                e.wait_ge(ev.h, w)
                                for mc in range(8):
                                    ins = e.matmul(
                                        pb[:], wb[:, mc * 128:mc * 128 + 128],
                                        xq[:, mc * T + tc * 512:mc * T + tc * 512 + 512],
                                        start=(mc == 0), stop=(mc == 7))
                                ins.then_inc(mm.h, 1)
                            mm.c += 1
                            if tc == 3:
                                wmark[ct % 2] = mm.c

                            @BV
                            def _(e, ct=ct, tc=tc, pb=pb):
                                e.wait_ge(mm.h, mm.c)
                                e.wait_ge(bcs.h, bcs.c)
                                e.scalar_tensor_tensor(
                                    out=hrow[:, tc * 512:tc * 512 + 512],
                                    in0=pb[:], scalar=1.0,
                                    in1=bc_c[:, tc * 512:tc * 512 + 512],
                                    op0=ALU.mult, op1=ALU.mult)
                            evmark[tc % 2] = ev.c

                        @BV
                        def _(e, ct=ct):
                            e.wait_ge(c0.h, c0.c)
                            e.wait_ge(s0.h, s0.c)
                            e.tensor_scalar(out=cacc[:], in0=hrow[:, 0:T],
                                            scalar1=fwt[:, ct * 3 + 1:ct * 3 + 2],
                                            scalar2=None, op0=ALU.mult)
                            e.scalar_tensor_tensor(out=cacc[:, 1:T], in0=hrow[:, 0:T - 1],
                                                   scalar=fwt[:, ct * 3:ct * 3 + 1],
                                                   in1=cacc[:, 1:T], op0=ALU.mult,
                                                   op1=ALU.add)
                            e.scalar_tensor_tensor(out=cacc[:, 0:T - 1], in0=hrow[:, 1:T],
                                                   scalar=fwt[:, ct * 3 + 2:ct * 3 + 3],
                                                   in1=cacc[:, 0:T - 1], op0=ALU.mult,
                                                   op1=ALU.add)

                        @BS
                        def _(e, ct=ct):
                            e.wait_ge(ev.h, ev.c)
                            e.wait_ge(s0.h, s0.c)
                            e.wait_ge(c0.h, c0.c)
                            e.activation(out=cacc[:], in_=cacc[:], func=AF.Silu,
                                         bias=fbt[:, ct:ct + 1])

                        @BV
                        def _(e, ct=ct):
                            e.wait_ge(sc.h, sc.c)
                            if ct == 0:
                                e.tensor_copy(out=acc_q[:], in_=cacc[:])
                                e.tensor_copy(out=mnb[:], in_=cacc[:])
                            else:
                                e.tensor_tensor(out=acc_q[:], in0=acc_q[:], in1=cacc[:],
                                                op=ALU.max)
                                e.tensor_tensor(out=mnb[:], in0=mnb[:], in1=cacc[:],
                                                op=ALU.min)

                        @block.sync
                        def _(e, ct=ct):
                            wave(e, s0, [(hsil_d[ct * 128:(ct + 1) * 128, :], cacc[:])],
                                 waits=[(ev, ev.c)])

                    with nc.psum_tensor(U("fh_pst"), [128, 128], F32) as pst2:
                        tr_reduce(acc_q, ALU.max, pst2)
                        row_from_16(R(R_MAX), pst2)
                        tr_reduce(mnb, ALU.min, pst2)
                        row_from_16(R(R_TMP), pst2)

                    @BV
                    def _(e):
                        e.wait_ge(rw.h, rw.c)
                        e.tensor_scalar(out=R(R_TMP), in0=R(R_TMP), scalar1=-1.0,
                                        scalar2=None, op0=ALU.mult)
                        e.tensor_tensor(out=R(R_MAX), in0=R(R_MAX), in1=R(R_TMP),
                                        op=ALU.max)
                        e.tensor_scalar(out=R(R_TMP), in0=R(R_MAX), scalar1=1e-5,
                                        scalar2=None, op0=ALU.max)
                        e.tensor_scalar(out=R(R_DQ), in0=R(R_TMP),
                                        scalar1=rows2[0:1, 1:2], scalar2=1.0 / 127.0,
                                        op0=ALU.mult, op1=ALU.mult)
                        e.reciprocal(out=R(R_QS), in_=R(R_TMP))
                        e.tensor_scalar(out=R(R_QS), in0=R(R_QS), scalar1=127.0,
                                        scalar2=None, op0=ALU.mult)

                    bcast_row([(bc_b[:], R(R_QS), 128), (bc_c[:], R(R_DQ), 128)],
                              waits=[(ev, ev.c)])

                    for ct in range(NCT):
                        buf = xt2[ct % 2]

                        @block.sync
                        def _(e, ct=ct, buf=buf):
                            wave(e, xsl[ct % 2],
                                 [(buf[:], hsil_d[ct * 128:(ct + 1) * 128, :])],
                                 waits=[(ev, ev.c), (s0, s0.c)])

                        @BV
                        def _(e, ct=ct, buf=buf):
                            e.wait_ge(xsl[ct % 2].h, xsl[ct % 2].c)
                            e.wait_ge(bcs.h, bcs.c)
                            e.wait_ge(s1.h, s1.c)
                            e.tensor_mul(cacc[:], buf[:], bc_b[:])
                            e.tensor_scalar(out=hrow[:, 0:T].bitcast(BF16)[:, 0:T],
                                            in0=cacc[:], scalar1=MAGIC, scalar2=MAGIC,
                                            op0=ALU.add, op1=ALU.subtract)

                        @block.sync
                        def _(e, ct=ct):
                            wave(e, s1, [(xq2_d[ct * 128:(ct + 1) * 128, :],
                                          hrow[:, 0:T].bitcast(BF16)[:, 0:T])],
                                 waits=[(ev, ev.c)])
                    barrier()

                with contextlib.ExitStack() as p2:
                    w2 = p2.enter_context(nc.sbuf_tensor(U("f_w2"), [128, NCT * DM], BF16))
                    r2 = [p2.enter_context(nc.sbuf_tensor(U("f_r2"), [128, 512], BF16))
                          for _ in range(2)]
                    psd = [p2.enter_context(nc.psum_tensor(U("f_pd%d" % i), [128, 512], F32))
                           for i in range(8)]

                    @block.sync
                    def _(e):
                        wave(e, c0, [(w2[:, cc * DM:(cc + 1) * DM],
                                      w2q_d[cc * 128:(cc + 1) * 128, :])
                                     for cc in range(NCT)])

                    wm2 = [mm.c, mm.c]
                    for tc in range(4):
                        for cc in range(NCT):
                            buf = r2[cc % 2]

                            @block.sync
                            def _(e, tc=tc, cc=cc, buf=buf, w=wm2[cc % 2]):
                                wave(e, rsl[cc % 2],
                                     [(buf[:], xq2_d[cc * 128:(cc + 1) * 128,
                                                     tc * 512:tc * 512 + 512])],
                                     waits=[(mm, w)])

                            @block.tensor
                            def _(e, tc=tc, cc=cc, buf=buf):
                                e.wait_ge(rsl[cc % 2].h, rsl[cc % 2].c)
                                e.wait_ge(c0.h, c0.c)
                                if cc == 0:
                                    e.wait_ge(ev.h, ev.c)
                                for dm_ in range(8):
                                    ins = e.matmul(
                                        psd[dm_][:],
                                        w2[:, cc * DM + dm_ * 128:cc * DM + dm_ * 128 + 128],
                                        buf[:], start=(cc == 0), stop=(cc == NCT - 1))
                                ins.then_inc(mm.h, 1)
                            mm.c += 1
                            wm2[cc % 2] = mm.c

                        for dm_ in range(8):
                            @block.sync
                            def _(e, tc=tc, dm_=dm_):
                                wave(e, m0, [(xt2[0][:, 0:512],
                                              src[b, dm_ * 128:(dm_ + 1) * 128,
                                                  tc * 512:tc * 512 + 512])],
                                     waits=[(ev, ev.c)])

                            @BV
                            def _(e, tc=tc, dm_=dm_):
                                e.wait_ge(mm.h, mm.c)
                                e.wait_ge(m0.h, m0.c)
                                e.wait_ge(s1.h, s1.c)
                                e.tensor_mul(tmp[:, 0:512], psd[dm_][:],
                                             bc_c[:, tc * 512:tc * 512 + 512])
                                e.tensor_add(tmp[:, 0:512], tmp[:, 0:512],
                                             xt2[0][:, 0:512])

                            @block.sync
                            def _(e, tc=tc, dm_=dm_):
                                wave(e, s1, [(dst[b, dm_ * 128:(dm_ + 1) * 128,
                                                  tc * 512:tc * 512 + 512],
                                              tmp[:, 0:512])],
                                     waits=[(ev, ev.c)])
                    barrier()

        def emit_ssm(b, src, dst):
            with contextlib.ExitStack() as ps:
                xibf = ps.enter_context(nc.sbuf_tensor(U("s_xibf"), [128, NIT * T], BF16))
                with contextlib.ExitStack() as p1:
                    lnx = p1.enter_context(nc.sbuf_tensor(U("s_lnx"), [128, NDT * T], BF16))
                    with contextlib.ExitStack() as pln:
                        xt2 = [pln.enter_context(nc.sbuf_tensor(U("s_xa"), [128, T], F32)),
                               pln.enter_context(nc.sbuf_tensor(U("s_xb"), [128, T], F32))]
                        tmp = pln.enter_context(nc.sbuf_tensor(U("s_tmp"), [128, T], F32))
                        acc_s = pln.enter_context(nc.sbuf_tensor(U("s_as"), [128, T], F32))
                        acc_q = pln.enter_context(nc.sbuf_tensor(U("s_aq"), [128, T], F32))
                        bc_mean = pln.enter_context(nc.sbuf_tensor(U("s_bm"), [128, T], F32))
                        bc_rstd = pln.enter_context(nc.sbuf_tensor(U("s_br"), [128, T], F32))
                        pst = pln.enter_context(nc.psum_tensor(U("s_pst"), [128, 128], F32))
                        ph = dict(xt2=xt2, tmp=tmp, acc_s=acc_s, acc_q=acc_q,
                                  bc_mean=bc_mean, bc_rstd=bc_rstd, pst=pst)
                        ln_stats(b, src, ph)
                        ln_norm_bf16(b, src, lnx, ph)
                        barrier()

                    wint = [p1.enter_context(nc.sbuf_tensor(U("s_wint"), [128, 8 * 128], BF16))
                            for _ in range(2)]
                    awt = p1.enter_context(nc.sbuf_tensor(U("s_awt"), [128, 64], F32))
                    abt = p1.enter_context(nc.sbuf_tensor(U("s_abt"), [128, 16], F32))
                    bint = p1.enter_context(nc.sbuf_tensor(U("s_bint"), [128, 32], F32))
                    hrow = p1.enter_context(nc.sbuf_tensor(U("s_hr"), [128, T], F32))
                    cacc = p1.enter_context(nc.sbuf_tensor(U("s_ca"), [128, T], F32))
                    psq = [p1.enter_context(nc.psum_tensor(U("s_p%d" % i), [128, 512], F32))
                           for i in range(2)]

                    @block.sync
                    def _(e):
                        wave(e, c0, [(awt[:], aconvw_in), (abt[:], bconv_il_in),
                                     (bint[:], bin_il_in)])

                    evmark = [0, 0]
                    wmark = [0, 0]
                    gsl = [g1, g2]
                    for ct in range(2 * NIT):
                        is_xi = ct < NIT
                        wb = wint[ct % 2]

                        @block.sync
                        def _(e, ct=ct, wb=wb, w=wmark[ct % 2]):
                            wave(e, gsl[ct % 2],
                                 [(wb[:].rearrange("p (c u) -> p c u", c=8),
                                   WinT_bf[:, ct * 128:(ct + 1) * 128].rearrange(
                                       "(c p) u -> p c u", p=128))],
                                 waits=[(mm, w)])
                        for tc in range(4):
                            pb = psq[tc % 2]

                            @block.tensor
                            def _(e, ct=ct, tc=tc, pb=pb, wb=wb, w=evmark[tc % 2]):
                                e.wait_ge(gsl[ct % 2].h, gsl[ct % 2].c)
                                e.wait_ge(c0.h, c0.c)
                                e.wait_ge(ev.h, w)
                                for mc in range(8):
                                    ins = e.matmul(
                                        pb[:], wb[:, mc * 128:mc * 128 + 128],
                                        lnx[:, mc * T + tc * 512:mc * T + tc * 512 + 512],
                                        start=(mc == 0), stop=(mc == 7))
                                ins.then_inc(mm.h, 1)
                            mm.c += 1
                            if tc == 3:
                                wmark[ct % 2] = mm.c

                            @BV
                            def _(e, ct=ct, tc=tc, pb=pb):
                                e.wait_ge(mm.h, mm.c)
                                e.wait_ge(s6.h, s6.c)
                                e.wait_ge(c0.h, c0.c)
                                e.tensor_scalar(
                                    out=hrow[:, tc * 512:tc * 512 + 512], in0=pb[:],
                                    scalar1=bint[:, ct:ct + 1], scalar2=None,
                                    op0=ALU.add)
                            evmark[tc % 2] = ev.c

                        if is_xi:
                            @BV
                            def _(e, ct=ct):
                                e.wait_ge(c0.h, c0.c)
                                e.tensor_scalar(out=cacc[:], in0=hrow[:, 0:T],
                                                scalar1=awt[:, ct * 4 + 3:ct * 4 + 4],
                                                scalar2=None, op0=ALU.mult)
                                for kk in range(3):
                                    sh = 3 - kk
                                    e.scalar_tensor_tensor(
                                        out=cacc[:, sh:T], in0=hrow[:, 0:T - sh],
                                        scalar=awt[:, ct * 4 + kk:ct * 4 + kk + 1],
                                        in1=cacc[:, sh:T], op0=ALU.mult, op1=ALU.add)

                            @BS
                            def _(e, ct=ct):
                                e.wait_ge(ev.h, ev.c)
                                e.wait_ge(s6.h, s6.c)
                                e.wait_ge(c0.h, c0.c)
                                e.activation(out=cacc[:], in_=cacc[:], func=AF.Silu,
                                             bias=abt[:, ct:ct + 1])

                            @BV
                            def _(e, ct=ct):
                                e.wait_ge(sc.h, sc.c)
                                e.tensor_copy(out=xibf[:, ct * T:(ct + 1) * T],
                                              in_=cacc[:])

                            @block.sync
                            def _(e, ct=ct):
                                wave(e, s6, [(xi_d[ct * 128:(ct + 1) * 128, :], cacc[:])],
                                     waits=[(sc, sc.c)])
                        else:
                            @BS
                            def _(e, ct=ct):
                                e.wait_ge(ev.h, ev.c)
                                e.wait_ge(s6.h, s6.c)
                                e.activation(out=cacc[:], in_=hrow[:], func=AF.Silu)

                            @block.sync
                            def _(e, ct=ct):
                                wave(e, s6,
                                     [(res_d[(ct - NIT) * 128:(ct - NIT + 1) * 128, :],
                                       cacc[:])],
                                     waits=[(sc, sc.c)])
                    barrier()

                with contextlib.ExitStack() as p2:
                    wxt = [p2.enter_context(nc.sbuf_tensor(U("s_wxt"), [128, NIT * 128], BF16))
                           for _ in range(2)]
                    wxbc = p2.enter_context(nc.sbuf_tensor(U("s_wxbc"), [128, NIT * 32], BF16))
                    bxd = p2.enter_context(nc.sbuf_tensor(U("s_bxd"), [128, 16], F32))
                    bxbc = p2.enter_context(nc.sbuf_tensor(U("s_bxbc"), [32, 1], F32))
                    drow = p2.enter_context(nc.sbuf_tensor(U("s_dr"), [128, T], F32))
                    dxrow = p2.enter_context(nc.sbuf_tensor(U("s_dxr"), [128, T], F32))
                    xif = p2.enter_context(nc.sbuf_tensor(U("s_xif"), [128, T], F32))
                    bcsb = p2.enter_context(nc.sbuf_tensor(U("s_bc"), [32, T], BF16))
                    psq = [p2.enter_context(nc.psum_tensor(U("s_q%d" % i), [128, 512], F32))
                           for i in range(2)]

                    @block.sync
                    def _(e):
                        wave(e, c0, [(bxd[:], bxd_il_in), (bxbc[:], bx_bc_in)])

                    @block.sync
                    def _(e):
                        wave(e, g3, [(wxbc[:].rearrange("p (c u) -> p c u", c=NIT),
                                      WxT_bf[:, 2048:2080].rearrange(
                                          "(c p) u -> p c u", p=128))])

                    wmark = [0, 0]
                    gsl = [g1, g2]
                    for pc in range(NIT):
                        wb = wxt[pc % 2]

                        @block.sync
                        def _(e, pc=pc, wb=wb, w=wmark[pc % 2]):
                            wave(e, gsl[pc % 2],
                                 [(wb[:].rearrange("p (c u) -> p c u", c=NIT),
                                   WxT_bf[:, pc * 128:(pc + 1) * 128].rearrange(
                                       "(c p) u -> p c u", p=128))],
                                 waits=[(mm, w)])
                        @block.sync
                        def _(e, pc=pc):
                            wave(e, m0, [(xif[:], xi_d[pc * 128:(pc + 1) * 128, :])],
                                 waits=[(ev, ev.c), (s6, s6.c)])
                        for tc in range(4):
                            pb = psq[tc % 2]

                            @block.tensor
                            def _(e, pc=pc, tc=tc, pb=pb, wb=wb, w=sc.c):
                                e.wait_ge(gsl[pc % 2].h, gsl[pc % 2].c)
                                e.wait_ge(sc.h, w)
                                for cc in range(NIT):
                                    ins = e.matmul(
                                        pb[:], wb[:, cc * 128:cc * 128 + 128],
                                        xibf[:, cc * T + tc * 512:cc * T + tc * 512 + 512],
                                        start=(cc == 0), stop=(cc == NIT - 1))
                                ins.then_inc(mm.h, 1)
                            mm.c += 1
                            if tc == 3:
                                wmark[pc % 2] = mm.c

                            @BS
                            def _(e, pc=pc, tc=tc, pb=pb):
                                e.wait_ge(mm.h, mm.c)
                                e.wait_ge(s2.h, s2.c)
                                e.wait_ge(c0.h, c0.c)
                                e.activation(out=drow[:, tc * 512:tc * 512 + 512],
                                             in_=pb[:], func=AF.Exp,
                                             bias=bxd[:, pc:pc + 1])
                                e.activation(out=drow[:, tc * 512:tc * 512 + 512],
                                             in_=drow[:, tc * 512:tc * 512 + 512],
                                             func=AF.Ln, bias=1.0)

                        @BV
                        def _(e, pc=pc):
                            e.wait_ge(sc.h, sc.c)
                            e.wait_ge(m0.h, m0.c)
                            e.wait_ge(s2.h, s2.c)
                            e.tensor_mul(dxrow[:], drow[:], xif[:])

                        @block.sync
                        def _(e, pc=pc):
                            wave(e, s2, [(dlt_d[pc * 128:(pc + 1) * 128, :], drow[:]),
                                         (dltx_d[pc * 128:(pc + 1) * 128, :], dxrow[:])],
                                 waits=[(ev, ev.c)])

                    for tc in range(4):
                        pb = psq[tc % 2]

                        @block.tensor
                        def _(e, tc=tc, pb=pb, w=ev.c):
                            e.wait_ge(ev.h, w)
                            e.wait_ge(g3.h, g3.c)
                            for cc in range(NIT):
                                ins = e.matmul(
                                    pb[0:32, :],
                                    wxbc[:, cc * 32:cc * 32 + 32],
                                    xibf[:, cc * T + tc * 512:cc * T + tc * 512 + 512],
                                    start=(cc == 0), stop=(cc == NIT - 1))
                            ins.then_inc(mm.h, 1)
                        mm.c += 1

                        @BV
                        def _(e, tc=tc, pb=pb):
                            e.wait_ge(mm.h, mm.c)
                            e.wait_ge(c0.h, c0.c)
                            e.tensor_scalar(out=bcsb[:, tc * 512:tc * 512 + 512],
                                            in0=pb[0:32, :], scalar1=bxbc[:, 0:1],
                                            scalar2=None, op0=ALU.add)

                    @block.sync
                    def _(e):
                        wave(e, s7, [(bc_d, bcsb[:])], waits=[(ev, ev.c)])
                    barrier()

            with contextlib.ExitStack() as p3:
                Bb2 = [p3.enter_context(nc.sbuf_tensor(U("s_Bb"), [128, NS * TCH], BF16))
                       for _ in range(2)]
                Cb2 = [p3.enter_context(nc.sbuf_tensor(U("s_Cb"), [128, NS * TCH], BF16))
                       for _ in range(2)]
                dch2 = [p3.enter_context(nc.sbuf_tensor(U("s_dch"), [128, TCH], F32))
                        for _ in range(2)]
                dxch2 = [p3.enter_context(nc.sbuf_tensor(U("s_dxch"), [128, TCH], F32))
                         for _ in range(2)]
                dA2 = [p3.enter_context(nc.sbuf_tensor(U("s_dA"), [128, TCH], F32))
                       for _ in range(2)]
                dBx2 = [p3.enter_context(nc.sbuf_tensor(U("s_dBx"), [128, TCH], F32))
                        for _ in range(2)]
                hbuf = p3.enter_context(nc.sbuf_tensor(U("s_h"), [128, TCH], F32))
                ytmp = p3.enter_context(nc.sbuf_tensor(U("s_yt"), [128, TCH], F32))
                yacc = p3.enter_context(nc.sbuf_tensor(U("s_ya"), [128, TCH], F32))
                ybf = p3.enter_context(nc.sbuf_tensor(U("s_ybf"), [128, TCH], BF16))
                stt = p3.enter_context(nc.sbuf_tensor(U("s_st"), [128, NIT * NS], F32))
                Aneg = p3.enter_context(nc.sbuf_tensor(U("s_An"), [128, 256], F32))
                Dil = p3.enter_context(nc.sbuf_tensor(U("s_Dil"), [128, 16], F32))
                xich2 = [p3.enter_context(nc.sbuf_tensor(U("s_xich"), [128, TCH], F32))
                         for _ in range(2)]
                rsch2 = [p3.enter_context(nc.sbuf_tensor(U("s_rsch"), [128, TCH], F32))
                         for _ in range(2)]

                @block.sync
                def _(e):
                    wave(e, c0, [(Aneg[:], A_il_in), (Dil[:], D_il_in)])

                @BS
                def _(e):
                    e.wait_ge(c0.h, c0.c)
                    e.activation(out=Aneg[:], in_=Aneg[:], func=AF.Exp)

                @BV
                def _(e):
                    e.wait_ge(sc.h, sc.c)
                    e.tensor_scalar(out=Aneg[:], in0=Aneg[:], scalar1=-1.0,
                                    scalar2=None, op0=ALU.mult)

                def issue_bc(ch, wev, wgq):
                    @block.sync
                    def _(e, ch=ch, wev=wev, wgq=wgq):
                        wave(e, bcs,
                             [(Bb2[ch % 2][:, n * TCH:(n + 1) * TCH],
                               bc_d[n:n + 1, ch * TCH:(ch + 1) * TCH].partition_broadcast(128))
                              for n in range(NS)] +
                             [(Cb2[ch % 2][:, n * TCH:(n + 1) * TCH],
                               bc_d[NS + n:NS + n + 1, ch * TCH:(ch + 1) * TCH].partition_broadcast(128))
                              for n in range(NS)],
                             waits=[(ev, wev), (s7, s7.c), (gq, wgq)])
                    return bcs.c

                save_ev = [ev.c, ev.c]
                save_gq = [gq.c, gq.c]
                bwa = [0] * NCH
                bwa[0] = issue_bc(0, ev.c, gq.c)
                for ch in range(NCH):
                    Bb = Bb2[ch % 2]
                    Cb = Cb2[ch % 2]
                    bw = bwa[ch]

                    chm_ev = [0, 0]
                    chm_sc = [0, 0]
                    chm_gq = [0, 0]
                    csl = [m0, s0]
                    for g in range(NIT):
                        gp = g % 2
                        dch, dxch = dch2[gp], dxch2[gp]
                        xich, rsch = xich2[gp], rsch2[gp]

                        @block.sync
                        def _(e, ch=ch, g=g, gp=gp, dch=dch, dxch=dxch,
                              xich=xich, rsch=rsch,
                              wv=chm_ev[gp], ws=chm_sc[gp], wg=chm_gq[gp]):
                            wave(e, csl[gp],
                                 [(dch[:], dlt_d[g * 128:(g + 1) * 128,
                                                 ch * TCH:(ch + 1) * TCH]),
                                  (dxch[:], dltx_d[g * 128:(g + 1) * 128,
                                                   ch * TCH:(ch + 1) * TCH]),
                                  (xich[:], xi_d[g * 128:(g + 1) * 128,
                                                 ch * TCH:(ch + 1) * TCH]),
                                  (rsch[:], res_d[g * 128:(g + 1) * 128,
                                                  ch * TCH:(ch + 1) * TCH])],
                                 waits=[(ev, wv), (sc, ws), (gq, wg)])

                        S0 = sc.c
                        E0 = ev.c
                        G0 = gq.c
                        opsn = 4 if ch < NCH - 1 else 3

                        @BS
                        def _(e, ch=ch, g=g, gp=gp, dch=dch, E0=E0, opsn=opsn):
                            e.wait_ge(csl[gp].h, csl[gp].c)
                            e.wait_ge(ev.h, E0)
                            for n in range(NS):
                                if n >= 2:
                                    e.wait_ge(ev.h, E0 + (n - 1) * opsn)
                                e.activation(out=dA2[n % 2][:], in_=dch[:],
                                             func=AF.Exp,
                                             scale=Aneg[:, g * NS + n:g * NS + n + 1])

                        @BG
                        def _(e, ch=ch, g=g, gp=gp, dxch=dxch, E0=E0, opsn=opsn,
                              bw=bw):
                            e.wait_ge(csl[gp].h, csl[gp].c)
                            e.wait_ge(bcs.h, bw)
                            e.wait_ge(ev.h, E0)
                            for n in range(NS):
                                if n >= 2:
                                    e.wait_ge(ev.h, E0 + (n - 1) * opsn)
                                e.tensor_mul(dBx2[n % 2][:], dxch[:],
                                             Bb[:, n * TCH:(n + 1) * TCH])

                        @BV
                        def _(e, ch=ch, g=g, gp=gp, w=s3.c, S0=S0, G0=G0, bw=bw):
                            e.wait_ge(bcs.h, bw)
                            e.wait_ge(s3.h, w)
                            for n in range(NS):
                                dA = dA2[n % 2]
                                e.wait_ge(sc.h, S0 + n + 1)
                                e.wait_ge(gq.h, G0 + n + 1)
                                e.tensor_tensor_scan(
                                    out=hbuf[:], data0=dA[:], data1=dBx2[n % 2][:],
                                    initial=(0.0 if ch == 0 else
                                             stt[:, g * NS + n:g * NS + n + 1]),
                                    op0=ALU.mult, op1=ALU.add)
                                if ch < NCH - 1:
                                    e.tensor_copy(out=stt[:, g * NS + n:g * NS + n + 1],
                                                  in_=hbuf[:, TCH - 1:TCH])
                                e.tensor_mul(ytmp[:], hbuf[:],
                                             Cb[:, n * TCH:(n + 1) * TCH])
                                if n == 0:
                                    e.tensor_copy(out=yacc[:], in_=ytmp[:])
                                else:
                                    e.tensor_add(yacc[:], yacc[:], ytmp[:])

                        @BV
                        def _(e, ch=ch, g=g, gp=gp, xich=xich, rsch=rsch, w=s3.c):
                            e.wait_ge(csl[gp].h, csl[gp].c)
                            e.wait_ge(s3.h, w)
                            e.wait_ge(c0.h, c0.c)
                            e.scalar_tensor_tensor(out=yacc[:], in0=xich[:],
                                                   scalar=Dil[:, g:g + 1],
                                                   in1=yacc[:], op0=ALU.mult,
                                                   op1=ALU.add)
                            e.tensor_mul(ytmp[:], yacc[:], rsch[:])
                            e.tensor_copy(out=ybf[:], in_=ytmp[:])
                        chm_ev[gp] = ev.c
                        chm_sc[gp] = sc.c
                        chm_gq[gp] = gq.c

                        @block.sync
                        def _(e, ch=ch, g=g):
                            wave(e, s3, [(yt_d[g * 128:(g + 1) * 128,
                                               ch * TCH:(ch + 1) * TCH], ybf[:])],
                                 waits=[(ev, ev.c)])
                    save_ev[ch % 2] = ev.c
                    save_gq[ch % 2] = gq.c
                    if ch + 1 < NCH:
                        bwa[ch + 1] = issue_bc(ch + 1, save_ev[(ch + 1) % 2],
                                               save_gq[(ch + 1) % 2])
                barrier()

            with contextlib.ExitStack() as p4:
                wo = p4.enter_context(nc.sbuf_tensor(U("s_wo"), [128, NIT * DM], BF16))
                yt2 = [p4.enter_context(nc.sbuf_tensor(U("s_yt2"), [128, 512], BF16))
                       for _ in range(2)]
                bout = p4.enter_context(nc.sbuf_tensor(U("s_bo"), [128, 8], F32))
                xres = p4.enter_context(nc.sbuf_tensor(U("s_xr2"), [128, 512], F32))
                otmp = p4.enter_context(nc.sbuf_tensor(U("s_ot"), [128, 512], F32))
                psd = [p4.enter_context(nc.psum_tensor(U("s_pd%d" % i), [128, 512], F32))
                       for i in range(8)]

                @block.sync
                def _(e):
                    wave(e, c0, [(bout[:], bout_il_in)])

                @block.sync
                def _(e):
                    wave(e, g1, [(wo[:, cc * DM:(cc + 1) * DM],
                                  WoutT_bf[cc * 128:(cc + 1) * 128, :])
                                 for cc in range(NIT)])

                wy2 = [mm.c, mm.c]
                for tc in range(4):
                    for cc in range(NIT):
                        buf = yt2[cc % 2]

                        @block.sync
                        def _(e, tc=tc, cc=cc, buf=buf, w=wy2[cc % 2]):
                            wave(e, rsl[cc % 2],
                                 [(buf[:], yt_d[cc * 128:(cc + 1) * 128,
                                                tc * 512:tc * 512 + 512])],
                                 waits=[(mm, w), (s3, s3.c)])

                        @block.tensor
                        def _(e, tc=tc, cc=cc, buf=buf):
                            e.wait_ge(rsl[cc % 2].h, rsl[cc % 2].c)
                            e.wait_ge(g1.h, g1.c)
                            if cc == 0:
                                e.wait_ge(ev.h, ev.c)
                            for dm_ in range(8):
                                ins = e.matmul(
                                    psd[dm_][:],
                                    wo[:, cc * DM + dm_ * 128:cc * DM + dm_ * 128 + 128],
                                    buf[:], start=(cc == 0), stop=(cc == NIT - 1))
                            ins.then_inc(mm.h, 1)
                        mm.c += 1
                        wy2[cc % 2] = mm.c

                    for dm_ in range(8):
                        @block.sync
                        def _(e, tc=tc, dm_=dm_):
                            wave(e, m0, [(xres[:], src[b, dm_ * 128:(dm_ + 1) * 128,
                                                       tc * 512:tc * 512 + 512])],
                                 waits=[(ev, ev.c)])

                        @BV
                        def _(e, tc=tc, dm_=dm_):
                            e.wait_ge(mm.h, mm.c)
                            e.wait_ge(m0.h, m0.c)
                            e.wait_ge(c0.h, c0.c)
                            e.wait_ge(s1.h, s1.c)
                            e.scalar_tensor_tensor(out=otmp[:], in0=psd[dm_][:],
                                                   scalar=bout[:, dm_:dm_ + 1],
                                                   in1=xres[:], op0=ALU.add,
                                                   op1=ALU.add)

                        @block.sync
                        def _(e, tc=tc, dm_=dm_):
                            wave(e, s1, [(dst[b, dm_ * 128:(dm_ + 1) * 128,
                                              tc * 512:tc * 512 + 512], otmp[:])],
                                 waits=[(ev, ev.c)])
                barrier()

        def emit_attn(b, src, dst):
            with contextlib.ExitStack() as pa:
                vall = pa.enter_context(nc.sbuf_tensor(U("a_vall"), [128, 16 * 1040], BF16))
                with contextlib.ExitStack() as p1:
                    lnx = p1.enter_context(nc.sbuf_tensor(U("a_lnx"), [128, NDT * T], BF16))
                    with contextlib.ExitStack() as pln:
                        xt2 = [pln.enter_context(nc.sbuf_tensor(U("a_xa"), [128, T], F32)),
                               pln.enter_context(nc.sbuf_tensor(U("a_xb"), [128, T], F32))]
                        tmp = pln.enter_context(nc.sbuf_tensor(U("a_tmp"), [128, T], F32))
                        acc_s = pln.enter_context(nc.sbuf_tensor(U("a_as"), [128, T], F32))
                        acc_q = pln.enter_context(nc.sbuf_tensor(U("a_aq"), [128, T], F32))
                        bc_mean = pln.enter_context(nc.sbuf_tensor(U("a_bm"), [128, T], F32))
                        bc_rstd = pln.enter_context(nc.sbuf_tensor(U("a_br"), [128, T], F32))
                        pst = pln.enter_context(nc.psum_tensor(U("a_pst"), [128, 128], F32))
                        ph = dict(xt2=xt2, tmp=tmp, acc_s=acc_s, acc_q=acc_q,
                                  bc_mean=bc_mean, bc_rstd=bc_rstd, pst=pst)
                        ln_stats(b, src, ph)
                        ln_norm_bf16(b, src, lnx, ph)
                        barrier()

                    wsb = p1.enter_context(nc.sbuf_tensor(U("a_wsb"), [128, 8 * DM], BF16))
                    bil = p1.enter_context(nc.sbuf_tensor(U("a_bil"), [128, 2 * NDT], F32))
                    brow = p1.enter_context(nc.sbuf_tensor(U("a_brow"), [1, 3 * DM], BF16))
                    stg = p1.enter_context(nc.sbuf_tensor(U("a_stg"), [128, 512], BF16))
                    psq = [p1.enter_context(nc.psum_tensor(U("a_p%d" % i), [128, 512], F32))
                           for i in range(2)]

                    @block.sync
                    def _(e):
                        wave(e, c0, [(bil[:, 0:NDT], bq_il_in),
                                     (bil[:, NDT:2 * NDT], bk_il_in)])

                    @block.gpsimd
                    def _(e):
                        wave(e, sw0, [(brow[0:1, i * DM:(i + 1) * DM], srcrow)
                                     for i, srcrow in enumerate(
                                         (bq_row_in, bk_row_in, bv_row_in))])

                    def load_w(W_in_ap):
                        @block.sync
                        def _(e, W_in_ap=W_in_ap, w=mm.c):
                            wave(e, g1, [(wsb[:, mc * DM:(mc + 1) * DM],
                                          W_in_ap[mc * 128:(mc + 1) * 128, :])
                                         for mc in range(8)],
                                 waits=[(mm, w)])

                    for wi, (W_in_ap, dram, bslot) in enumerate(
                            ((Wq_bf, qt_d, 0), (Wk_bf, kt_d, 1))):
                        load_w(W_in_ap)
                        for hdc in range(NDT):
                            for tc in range(4):
                                pb = psq[tc % 2]

                                @block.tensor
                                def _(e, hdc=hdc, tc=tc, pb=pb, w=sc.c):
                                    e.wait_ge(g1.h, g1.c)
                                    e.wait_ge(sc.h, w)
                                    for mc in range(8):
                                        ins = e.matmul(
                                            pb[:],
                                            wsb[:, mc * DM + hdc * 128:mc * DM + hdc * 128 + 128],
                                            lnx[:, mc * T + tc * 512:mc * T + tc * 512 + 512],
                                            start=(mc == 0), stop=(mc == 7))
                                    ins.then_inc(mm.h, 1)
                                mm.c += 1

                                @BS
                                def _(e, hdc=hdc, tc=tc, pb=pb, bslot=bslot):
                                    e.wait_ge(mm.h, mm.c)
                                    e.wait_ge(s4.h, s4.c)
                                    e.wait_ge(c0.h, c0.c)
                                    e.activation(out=stg[:], in_=pb[:], func=AF.Identity,
                                                 bias=bil[:, bslot * NDT + hdc:bslot * NDT + hdc + 1])

                                @block.sync
                                def _(e, hdc=hdc, tc=tc, dram=dram):
                                    wave(e, s4, [(dram[hdc * 128:(hdc + 1) * 128,
                                                       tc * 512:tc * 512 + 512], stg[:])],
                                         waits=[(sc, sc.c)])

                    load_w(Wv_bf)

                    @BV
                    def _(e):
                        e.memset(vall[:].rearrange("p (c h u) -> p c h u",
                                                   c=16, h=16)[:, :, :, 64:65], 1.0)
                    for tt in range(16):
                        for half in range(2):
                            pb = psq[half]

                            @block.tensor
                            def _(e, tt=tt, half=half, pb=pb, w=ev.c):
                                e.wait_ge(g1.h, g1.c)
                                e.wait_ge(g2.h, g2.c)
                                e.wait_ge(sw0.h, sw0.c)
                                e.wait_ge(c0.h, c0.c)
                                e.wait_ge(sc.h, sc.c)
                                e.wait_ge(ev.h, w)
                                for mc in range(8):
                                    e.matmul(pb[:],
                                             lnx[:, mc * T + tt * 128:mc * T + tt * 128 + 128],
                                             wsb[:, mc * DM + half * 512:mc * DM + half * 512 + 512],
                                             start=(mc == 0), stop=False)
                                ins = e.matmul(
                                    pb[:], ones_row[:],
                                    brow[0:1, 2 * DM + half * 512:2 * DM + half * 512 + 512],
                                    start=False, stop=True)
                                ins.then_inc(mm.h, 1)
                            mm.c += 1

                            @BV
                            def _(e, tt=tt, half=half, pb=pb):
                                e.wait_ge(mm.h, mm.c)
                                base = tt * 1040 + half * 520
                                e.tensor_copy(
                                    out=vall[:, base:base + 520].rearrange(
                                        "p (h u) -> p h u", h=8)[:, :, 0:64],
                                    in_=pb[:].rearrange("p (h u) -> p h u", h=8))

                    if b == 0:
                        relsb = p1.enter_context(
                            nc.sbuf_tensor(U("a_rel"), [128, 8 * 512], BF16))

                        @block.sync
                        def _(e, w=mm.c):
                            wave(e, g2, [(relsb[:, mc * 512:(mc + 1) * 512],
                                          relembT_bf[mc * 128:(mc + 1) * 128, :])
                                         for mc in range(8)],
                                 waits=[(mm, w)])
                        for wi, (W_in_ap, dram, bro) in enumerate(
                                ((Wk_bf, poskcd_d, 1), (Wq_bf, posqcd_d, 0))):
                            load_w(W_in_ap)
                            for cc in range(4):
                                for half in range(2):
                                    pb = psq[half]

                                    @block.tensor
                                    def _(e, cc=cc, half=half, pb=pb, bro=bro, w=sc.c):
                                        e.wait_ge(g1.h, g1.c)
                                        e.wait_ge(g2.h, g2.c)
                                        e.wait_ge(sw0.h, sw0.c)
                                        e.wait_ge(c0.h, c0.c)
                                        e.wait_ge(ev.h, ev.c)
                                        e.wait_ge(sc.h, w)
                                        for mc in range(8):
                                            e.matmul(
                                                pb[:],
                                                relsb[:, mc * 512 + cc * 128:mc * 512 + cc * 128 + 128],
                                                wsb[:, mc * DM + half * 512:mc * DM + half * 512 + 512],
                                                start=(mc == 0), stop=False)
                                        ins = e.matmul(
                                            pb[:], ones_row[:],
                                            brow[0:1, bro * DM + half * 512:bro * DM + half * 512 + 512],
                                            start=False, stop=True)
                                        ins.then_inc(mm.h, 1)
                                    mm.c += 1

                                    @BS
                                    def _(e, cc=cc, half=half, pb=pb):
                                        e.wait_ge(mm.h, mm.c)
                                        e.wait_ge(s4.h, s4.c)
                                        e.activation(out=stg[:], in_=pb[:], func=AF.Copy)

                                    @block.sync
                                    def _(e, cc=cc, half=half, dram=dram):
                                        wave(e, s4,
                                             [(dram[cc * 128:(cc + 1) * 128,
                                                    half * 512:half * 512 + 512], stg[:])],
                                             waits=[(sc, sc.c)])
                    barrier()

                with contextlib.ExitStack() as p2:
                    esb = p2.enter_context(nc.sbuf_tensor(U("a_esb"), [128, 2 * 4 * TBLP], BF16))
                    pk4 = p2.enter_context(nc.sbuf_tensor(U("a_pk4"), [128, 2 * 256], BF16))
                    tblk = p2.enter_context(nc.sbuf_tensor(U("a_tbk"), [64, TBLP], BF16))
                    tblq = p2.enter_context(nc.sbuf_tensor(U("a_tbq"), [64, TBLP], BF16))
                    qth = p2.enter_context(nc.sbuf_tensor(U("a_qth"), [64, T], BF16))
                    kth = p2.enter_context(nc.sbuf_tensor(U("a_kth"), [64, T], BF16))
                    m1sb = [p2.enter_context(nc.sbuf_tensor(U("a_m1"), [128, TBLP], BF16))
                            for _ in range(2)]
                    m2sb = [p2.enter_context(nc.sbuf_tensor(U("a_m2"), [128, TBLP], BF16))
                            for _ in range(2)]
                    m2bias = p2.enter_context(nc.sbuf_tensor(U("a_m2b"), [128, 32], F32))
                    g1sb = p2.enter_context(nc.sbuf_tensor(U("a_g1"), [128, 4 * 1152], BF16))
                    g2sb = p2.enter_context(nc.sbuf_tensor(U("a_g2"), [128, 16 * 512], BF16))
                    ptb = p2.enter_context(nc.sbuf_tensor(U("a_pt"), [128, 16 * 512], BF16))
                    rowlo = p2.enter_context(nc.sbuf_tensor(U("a_rlo"), [1, 512], BF16))
                    rowhi = p2.enter_context(nc.sbuf_tensor(U("a_rhi"), [1, 512], BF16))
                    ctxsb = p2.enter_context(nc.sbuf_tensor(U("a_ctx"), [64, T], F32))
                    dnb = p2.enter_context(nc.sbuf_tensor(U("a_dnb"), [64, T], F32))
                    xh = p2.enter_context(nc.sbuf_tensor(U("a_xh"), [64, T], F32))
                    psS = [p2.enter_context(nc.psum_tensor(U("a_s%d" % i), [128, 512], F32))
                           for i in range(2)]
                    psM2 = [p2.enter_context(nc.psum_tensor(U("a_m%d" % i), [128, 1024], F32))
                            for i in range(2)]
                    psC = p2.enter_context(nc.psum_tensor(U("a_c"), [65, 512], F32))
                    psE = p2.enter_context(nc.psum_tensor(U("a_e"), [1, 512], F32))

                    @block.sync
                    def _(e):
                        wave(e, c0,
                             [(esb[:, 0:4 * TBLP],
                               Erev_in.rearrange("(c p) w -> p c w", p=128)),
                              (esb[:, 4 * TBLP:8 * TBLP],
                               Efwd_in.rearrange("(c p) w -> p c w", p=128))])

                    for h in range(H):
                        @block.sync
                        def _(e, h=h, w=mm.c):
                            wave(e, m0,
                                 [(pk4[:, ti * 256:(ti + 1) * 256].rearrange(
                                     "p (c u) -> p c u", c=4),
                                   bass.AP(dram.tensor, h * 64,
                                           [[DM, 128], [128 * DM, 4], [1, 64]]))
                                  for ti, dram in enumerate((poskcd_d, posqcd_d))],
                                 waits=[(mm, w), (s4, s4.c)])
                            wave(e, x0, [(qth[:], qt_d[h * 64:(h + 1) * 64, :]),
                                         (kth[:], kt_d[h * 64:(h + 1) * 64, :])],
                                 waits=[(mm, w), (s4, s4.c)])

                        tmark = [ev.c, ev.c]
                        for ti, tbl in enumerate((tblk, tblq)):
                            for w3 in range(3):
                                q = ti * 3 + w3
                                pt = psM2[q % 2]

                                @block.tensor
                                def _(e, h=h, ti=ti, w3=w3, pt=pt, w=tmark[q % 2]):
                                    e.wait_ge(m0.h, m0.c)
                                    e.wait_ge(c0.h, c0.c)
                                    e.wait_ge(ev.h, w)
                                    for cc in range(4):
                                        ins = e.matmul(
                                            pt[0:64, 0:512],
                                            pk4[:, ti * 256 + cc * 64:ti * 256 + cc * 64 + 64],
                                            esb[:, ti * 4 * TBLP + cc * TBLP + w3 * 512:
                                                ti * 4 * TBLP + cc * TBLP + w3 * 512 + 512],
                                            start=(cc == 0), stop=(cc == 3))
                                    ins.then_inc(mm.h, 1)
                                mm.c += 1

                                @BV
                                def _(e, ti=ti, w3=w3, tbl=tbl, pt=pt):
                                    e.wait_ge(mm.h, mm.c)
                                    e.tensor_copy(out=tbl[:, w3 * 512:w3 * 512 + 512],
                                                  in_=pt[0:64, 0:512])
                                tmark[q % 2] = ev.c

                        mevk = [tmark[0], tmark[1]]
                        mevs = ev.c
                        for mi, (qk, tbl, ssl) in enumerate(((qth, tblk, s4),
                                                             (kth, tblq, s5))):
                            msbl = m1sb if mi == 0 else m2sb
                            wrm = [ssl.c, ssl.c]
                            for it in range(16):
                                msb = msbl[it % 2]
                                for w3 in range(3):
                                    if w3 == 2:
                                        pma = psS[0][:, 0:512]
                                        wv = mevs
                                    else:
                                        pma = psM2[it % 2][:, w3 * 512:w3 * 512 + 512]
                                        wv = mevk[it % 2]

                                    @block.tensor
                                    def _(e, mi=mi, it=it, w3=w3, qk=qk, tbl=tbl,
                                          pma=pma, w=sc.c, wv=wv):
                                        e.wait_ge(x0.h, x0.c)
                                        e.wait_ge(sc.h, w)
                                        e.wait_ge(ev.h, wv)
                                        e.matmul(pma, qk[:, it * 128:it * 128 + 128],
                                                 tbl[:, w3 * 512:w3 * 512 + 512],
                                                 start=True, stop=True).then_inc(mm.h, 1)
                                    mm.c += 1

                                    if w3 == 1:
                                        @BV
                                        def _(e, mi=mi, it=it, msb=msb,
                                              ssl=ssl, w=wrm[it % 2]):
                                            e.wait_ge(mm.h, mm.c)
                                            e.wait_ge(ssl.h, w)
                                            e.tensor_copy(out=msb[:, 0:1024],
                                                          in_=psM2[it % 2][:, 0:1024])
                                        mevk[it % 2] = ev.c
                                    elif w3 == 2:
                                        @BV
                                        def _(e, mi=mi, it=it, msb=msb,
                                              ssl=ssl, w=wrm[it % 2]):
                                            e.wait_ge(mm.h, mm.c)
                                            e.wait_ge(ssl.h, w)
                                            e.tensor_copy(out=msb[:, 1024:1536],
                                                          in_=psS[0][:, 0:512])
                                        mevs = ev.c

                                if mi == 1:
                                    @BV
                                    def _(e, it=it, msb=msb):
                                        e.tensor_scalar(out=m2bias[:, it * 2:it * 2 + 1],
                                                        in0=msb[:, 1276:1277],
                                                        scalar1=1.0 / SCALE,
                                                        scalar2=None, op0=ALU.mult)
                                        e.tensor_scalar(out=m2bias[:, it * 2 + 1:it * 2 + 2],
                                                        in0=msb[:, 254:255],
                                                        scalar1=1.0 / SCALE,
                                                        scalar2=None, op0=ALU.mult)

                                @block.sync
                                def _(e, mi=mi, it=it, msb=msb, ssl=ssl):
                                    wave(e, ssl,
                                         [(bass.AP((m1d if mi == 0 else m2d).tensor,
                                                   MGUARD + it * MBLK,
                                                   [[TBLP, 128], [1, TBLP]]), msb[:])],
                                         waits=[(ev, ev.c)])
                                wrm[it % 2] = ssl.c

                        for ic in range(4):
                            @block.sync
                            def _(e, h=h, ic=ic, w=mm.c):
                                wave(e, g3,
                                     [(g1sb[:].rearrange("p (b u) -> p b u", b=4),
                                       bass.AP(m1d.tensor, MGUARD + (4 * ic) * MBLK + 253,
                                               [[TBLP - 1, 128], [MBLK, 4], [1, 1152]]))],
                                     waits=[(mm, w), (s4, s4.c)])

                            @block.sync
                            def _(e, h=h, ic=ic, w=mm.c):
                                wave(e, x1,
                                     [(g2sb[:].rearrange("p (j u) -> p j u", j=16),
                                       bass.AP(m2d.tensor, MGUARD + HW + ic * 512,
                                               [[TBLP - 1, 128], [MBLK - 128, 16], [1, 512]]))],
                                     waits=[(mm, w), (s5, s5.c)])

                            for ei, (col, rbuf) in enumerate(((254, rowhi), (1276, rowlo))):
                                @block.tensor
                                def _(e, ic=ic, col=col, w=ev.c):
                                    e.wait_ge(ev.h, w)
                                    e.matmul(psE[:], tblk[:, col:col + 1],
                                             qth[:, ic * 512:ic * 512 + 512],
                                             start=True, stop=True).then_inc(mm.h, 1)
                                mm.c += 1

                                @BV
                                def _(e, rbuf=rbuf):
                                    e.wait_ge(mm.h, mm.c)
                                    e.tensor_copy(out=rbuf[:], in_=psE[:])

                            S0 = sc.c
                            jinfo = []
                            cum = [0]
                            for jt in range(16):
                                deltas = [4 * ic + ib - jt for ib in range(4)]
                                lo_ibs = [ib for ib in range(4) if deltas[ib] <= -(BANDR + 1)]
                                hi_ibs = [ib for ib in range(4) if deltas[ib] >= BANDR + 1]
                                band_ibs = [ib for ib in range(4) if abs(deltas[ib]) <= BANDR]
                                jinfo.append((lo_ibs, hi_ibs, band_ibs))
                                cum.append(cum[-1] + (1 if lo_ibs else 0) +
                                           (1 if band_ibs else 0) + (1 if hi_ibs else 0))
                            mmarks = []
                            for jt in range(16):
                                pb = psS[jt % 2]
                                lo_ibs, hi_ibs, band_ibs = jinfo[jt]

                                @block.tensor
                                def _(e, h=h, ic=ic, jt=jt, pb=pb,
                                      w=S0 + (cum[jt - 1] if jt >= 2 else 0),
                                      lo_ibs=lo_ibs, hi_ibs=hi_ibs, band_ibs=band_ibs):
                                    if jt == 0:
                                        e.wait_ge(x0.h, x0.c)
                                        e.wait_ge(x1.h, x1.c)
                                        e.wait_ge(g3.h, g3.c)
                                        e.wait_ge(c0.h, c0.c)
                                        e.wait_ge(ev.h, ev.c)
                                    ops = [("qk",)]
                                    if lo_ibs:
                                        ops.append(("lo",))
                                    if hi_ibs:
                                        ops.append(("hi",))
                                    if band_ibs:
                                        ops.append(("id",))
                                        ops.extend(("tr", ib) for ib in band_ibs)
                                    for oi, op in enumerate(ops):
                                        st_ = (oi == 0)
                                        sp_ = (oi == len(ops) - 1)
                                        if op[0] == "qk":
                                            ins = e.matmul(pb[:], kth[:, jt * 128:jt * 128 + 128],
                                                           qth[:, ic * 512:ic * 512 + 512],
                                                           start=st_, stop=sp_)
                                            ins._wait_ge(sc.h, w)
                                        elif op[0] == "lo":
                                            i0, i1 = lo_ibs[0] * 128, (lo_ibs[-1] + 1) * 128
                                            ins = e.matmul(pb[:, i0:i1], ones_row[:],
                                                           rowlo[0:1, i0:i1],
                                                           start=st_, stop=sp_)
                                        elif op[0] == "hi":
                                            i0, i1 = hi_ibs[0] * 128, (hi_ibs[-1] + 1) * 128
                                            ins = e.matmul(pb[:, i0:i1], ones_row[:],
                                                           rowhi[0:1, i0:i1],
                                                           start=st_, stop=sp_)
                                        elif op[0] == "id":
                                            i0, i1 = band_ibs[0] * 128, (band_ibs[-1] + 1) * 128
                                            ins = e.matmul(pb[:, i0:i1], identb[:],
                                                           g2sb[:, jt * 512 + i0:jt * 512 + i1],
                                                           start=st_, stop=sp_)
                                        else:
                                            ib = op[1]
                                            off = ib * 1152 + (jt - (4 * ic + ib) + BANDR) * 128
                                            ins = e.matmul(pb[:, ib * 128:ib * 128 + 128],
                                                           g1sb[:, off:off + 128], identb[:],
                                                           start=st_, stop=sp_)
                                    ins.then_inc(mm.h, 1)
                                mm.c += 1
                                mmarks.append(mm.c)

                            @BS
                            def _(e, h=h, ic=ic, jinfo=jinfo, mmarks=mmarks):
                                e.wait_ge(ev.h, ev.c)
                                for jt in range(16):
                                    lo_ibs, hi_ibs, band_ibs = jinfo[jt]
                                    pb = psS[jt % 2]
                                    e.wait_ge(mm.h, mmarks[jt])
                                    regions = []
                                    if lo_ibs:
                                        regions.append((lo_ibs[0] * 128, (lo_ibs[-1] + 1) * 128,
                                                        m2bias[:, jt * 2:jt * 2 + 1]))
                                    if band_ibs:
                                        regions.append((band_ibs[0] * 128,
                                                        (band_ibs[-1] + 1) * 128,
                                                        zcol[:, 0:1]))
                                    if hi_ibs:
                                        regions.append((hi_ibs[0] * 128, (hi_ibs[-1] + 1) * 128,
                                                        m2bias[:, jt * 2 + 1:jt * 2 + 2]))
                                    for (r0_, r1_, bias) in regions:
                                        e.activation(out=ptb[:, jt * 512 + r0_:jt * 512 + r1_],
                                                     in_=pb[:, r0_:r1_], func=AF.Exp,
                                                     scale=1.0 / SCALE, bias=bias)

                            @block.tensor
                            def _(e, h=h, ic=ic):
                                e.wait_ge(ev.h, ev.c)
                                for ck in range(16):
                                    ins = e.matmul(
                                        psC[:],
                                        vall[:, ck * 1040 + h * 65:ck * 1040 + h * 65 + 65],
                                        ptb[:, ck * 512:ck * 512 + 512],
                                        start=(ck == 0), stop=(ck == 15))
                                    if ck == 0:
                                        ins._wait_ge(sc.h, sc.c)
                                ins.then_inc(mm.h, 1)
                            mm.c += 1

                            @BV
                            def _(e, ic=ic):
                                e.wait_ge(mm.h, mm.c)
                                e.wait_ge(rw.h, rw.c)
                                e.tensor_copy(out=rows[0:1, R_DEN * T + ic * 512:
                                                  R_DEN * T + ic * 512 + 512],
                                              in_=psC[64:65, :])

                            @BV
                            def _(e, ic=ic):
                                e.wait_ge(mm.h, mm.c)
                                e.wait_ge(bcs.h, bcs.c)
                                e.tensor_copy(out=ctxsb[:, ic * 512:ic * 512 + 512],
                                              in_=psC[0:64, :])

                        @BS
                        def _(e, h=h):
                            e.wait_ge(ev.h, ev.c)
                            e.activation(out=R(R_DEN), in_=R(R_DEN), func=AF.Ln)
                            e.activation(out=R(R_DEN), in_=R(R_DEN), func=AF.Exp,
                                         scale=-1.0)
                        bcast_row([(dnb[:], R(R_DEN), 64)], waits=[(sc, sc.c)])

                        @block.sync
                        def _(e, h=h, w=s1.c):
                            wave(e, m0, [(xh[:], src[b, h * 64:(h + 1) * 64, :])],
                                 waits=[(ev, ev.c), (s1, w)])

                        @BV
                        def _(e, h=h):
                            e.wait_ge(m0.h, m0.c)
                            e.wait_ge(bcs.h, bcs.c)
                            e.tensor_mul(ctxsb[:], ctxsb[:], dnb[:])
                            e.tensor_add(xh[:], xh[:], ctxsb[:])

                        @block.sync
                        def _(e, h=h):
                            wave(e, s1, [(dst[b, h * 64:(h + 1) * 64, :], xh[:])],
                                 waits=[(ev, ev.c)])
                    barrier()

        first = True
        for sl in SUBLAYERS:
            src = xin if first else xs
            dst = outT if sl == SUBLAYERS[-1] else xs
            if sl == 0:
                for b in range(BPC):
                    emit_attn(b, src, dst)
            elif sl == 2:
                for b in range(BPC):
                    emit_ssm(b, src, dst)
            else:
                for b in range(BPC):
                    emit_ffn(b, src, dst)
            first = False

        @block.sync
        def _(e):
            for s_ in SYNC_SLOTS:
                e.wait_ge(s_.h, s_.c)

        @block.gpsimd
        def _(e):
            for s_ in GQ_SLOTS:
                e.wait_ge(s_.h, s_.c)

    gstack.close()
    return nc



def host_weights(inputs):
    g = lambda k: np.asarray(inputs[k])
    Ef, Er = build_E()
    d = {}
    d["relembT"] = np.ascontiguousarray(g("L0_attn_a_rel_embeddings").T)
    d["Wq"] = np.ascontiguousarray(g("L0_attn_a_Wq"))
    d["Wk"] = np.ascontiguousarray(g("L0_attn_a_Wk"))
    d["Wv"] = np.ascontiguousarray(g("L0_attn_a_Wv"))
    for nm, key in (("bq", "L0_attn_a_bq"), ("bk", "L0_attn_a_bk"),
                    ("bv", "L0_attn_a_bv")):
        v = g(key).astype(np.float32)
        d[nm + "_row"] = np.ascontiguousarray(v[None, :])
        d[nm + "_il"] = il(v, NDT)
    d["Erev"] = Er
    d["Efwd"] = Ef
    d["identf"] = np.eye(128, dtype=np.float32)
    d["identb"] = np.eye(128).astype(BF)
    d["ones_row"] = np.ones((1, 128), dtype=BF)
    d["w1T"] = np.ascontiguousarray(g("L0_ffn_w1").T)
    d["w2T"] = np.ascontiguousarray(g("L0_ffn_w2").T)
    cw = g("L0_ffn_conv_w").reshape(DH, 3)
    d["fconvw_il"] = np.ascontiguousarray(
        cw.reshape(32, 128, 3).transpose(1, 0, 2).reshape(128, 96)).astype(np.float32)
    d["fconvb_il"] = il(g("L0_ffn_conv_b").astype(np.float32), 32)
    d["WinT"] = np.ascontiguousarray(g("L0_attn_b_W_in").T)
    d["WxT"] = np.ascontiguousarray(g("L0_attn_b_W_x").T)
    d["WoutT"] = np.ascontiguousarray(g("L0_attn_b_W_out").T)
    d["bin_il"] = il(g("L0_attn_b_b_in").astype(np.float32), 32)
    bx = g("L0_attn_b_b_x").astype(np.float32)
    d["bxd_il"] = il(bx[:DI], 16)
    d["bx_bc"] = np.ascontiguousarray(bx[DI:][:, None])
    d["bout_il"] = il(g("L0_attn_b_b_out").astype(np.float32), 8)
    d["bconv_il"] = il(g("L0_attn_b_b_conv").astype(np.float32), 16)
    acw = g("L0_attn_b_W_conv").reshape(DI, 4)
    d["aconvw_il"] = np.ascontiguousarray(
        acw.reshape(16, 128, 4).transpose(1, 0, 2).reshape(128, 64)).astype(np.float32)
    Alog = g("L0_attn_b_A_log").astype(np.float32)
    d["A_il"] = np.ascontiguousarray(
        Alog.reshape(16, 128, 16).transpose(1, 0, 2).reshape(128, 256))
    d["D_il"] = il(g("L0_attn_b_D").astype(np.float32), 16)
    for k in d:
        if d[k].dtype == np.float64:
            d[k] = d[k].astype(np.float32)
    return d


def kernel(**inputs):
    x = np.asarray(inputs["L0_attn_a_hidden_states"])
    shared = host_weights(inputs)
    in_maps = []
    for c in range(NCORES):
        xc = x[c * BPC:(c + 1) * BPC]
        m = dict(shared)
        m["xT"] = np.ascontiguousarray(xc.transpose(0, 2, 1))
        in_maps.append(m)
    ncores = DEV_CORES if DEV_CORES else NCORES
    nc = build_program()
    res = run_bass_kernel_spmd(nc, in_maps[:ncores], core_ids=list(range(ncores)))
    outs = []
    for c in range(ncores):
        o = np.asarray(res.results[c]["outT"])
        outs.append(o.transpose(0, 2, 1))
    out = np.concatenate(outs, axis=0).astype(np.float32)
    if out.shape[0] < x.shape[0]:
        out = np.concatenate(
            [out, np.zeros((x.shape[0] - out.shape[0],) + out.shape[1:], np.float32)], 0)
    return out
